# version 13
# baseline (speedup 1.0000x reference)
import contextlib

import numpy as np
import ml_dtypes

import concourse.bass as bass
import concourse.mybir as mybir
from concourse.bass_utils import run_bass_kernel_spmd

F32 = mybir.dt.float32
BF16 = mybir.dt.bfloat16
AF = mybir.ActivationFunctionType
ALU = mybir.AluOpType
AX = mybir.AxisListType

RND_C = 3.0 * 2.0**22
N_CORES = 8
GROUP = 4


class Sem:
    def __init__(self, nc, es, name):
        self.h = es.enter_context(nc.semaphore(name))
        self.c = 0

    def inc(self, instr, n=1):
        instr.then_inc(self.h, n)
        self.c += n
        return self.c


class DSem:

    def __init__(self, nc, es, name, nslots):
        self.s = [Sem(nc, es, f"{name}{i}") for i in range(nslots)]

    def gate(self, eng, slot):
        sm = self.s[slot]
        if sm.c > 0:
            eng.wait_ge(sm.h, sm.c)

    def issue(self, eng, slot, out, in_):
        sm = self.s[slot]
        sm.inc(eng.dma_start(out=out, in_=in_), 16)
        return sm.c

    def cnt(self, slot):
        return self.s[slot].c

    def wait(self, eng, slot, cnt):
        eng.wait_ge(self.s[slot].h, cnt)

    def wait_all(self, eng):
        for sm in self.s:
            if sm.c:
                eng.wait_ge(sm.h, sm.c)


def _pieces(T):
    n = (T + 511) // 512
    w = T // n
    assert w * n == T
    return [(i * w, w) for i in range(n)]


def build_nc(cfg):
    d = cfg["d"]
    I = cfg["I"]
    T = cfg["T"]
    NH = d // 128
    FC = d // 128
    IC = I // 128
    TC = T // 128
    S = GROUP * T
    KC = S // 128
    OSL = d // N_CORES
    ISL = I // N_CORES
    PC = _pieces(T)
    NP = len(PC)
    assert OSL % 128 == 0 and ISL % 128 == 0
    NW1 = ISL // 128
    NW2 = OSL // 128
    W2C = max(1, IC // 16)
    N2C = IC // W2C
    n_bn = d // 512
    assert n_bn * 512 == d and T % 128 == 0
    QA = min(IC, cfg.get("qa", 20))
    QB = IC - QA
    EXPS = np.float32(np.float32(1.0) / np.sqrt(np.float32(128.0)))
    EXPB = np.float32(-20.0)
    HSL = (FC // 2) * ISL
    FH = FC // 2

    STOP = cfg.get("stop", 99)
    nc = bass.Bass("TRN2", target_bir_lowering=False)

    x_in = nc.dram_tensor("x_in", [T, d], F32, kind="ExternalInput").ap()
    wqT = nc.dram_tensor("wqT_sl", [d, OSL], F32, kind="ExternalInput").ap()
    wkT = nc.dram_tensor("wkT_sl", [d, OSL], F32, kind="ExternalInput").ap()
    wvT = nc.dram_tensor("wvT_sl", [d, OSL], F32, kind="ExternalInput").ap()
    woT = nc.dram_tensor("woT_sl", [d, OSL], F32, kind="ExternalInput").ap()
    w1T = nc.dram_tensor("w1T_sl", [d, ISL], F32, kind="ExternalInput").ap()
    w2T = nc.dram_tensor("w2T_sl", [I, OSL], F32, kind="ExternalInput").ap()
    bq_t = nc.dram_tensor("bq_t", [128, NH], F32, kind="ExternalInput").ap()
    bk_t = nc.dram_tensor("bk_t", [128, NH], F32, kind="ExternalInput").ap()
    bv_t = nc.dram_tensor("bv_t", [128, NH], F32, kind="ExternalInput").ap()
    b1_t = nc.dram_tensor("b1_t", [128, IC], F32, kind="ExternalInput").ap()
    b2_t = nc.dram_tensor("b2_t", [128, FC], F32, kind="ExternalInput").ap()
    cos_t = nc.dram_tensor("cos_t", [128, T], F32, kind="ExternalInput").ap()
    sin_t = nc.dram_tensor("sin_t", [128, T], F32, kind="ExternalInput").ap()
    id_bf = nc.dram_tensor("id_bf", [128, 128], BF16, kind="ExternalInput").ap()
    id_f32 = nc.dram_tensor("id_f32", [128, 128], F32, kind="ExternalInput").ap()
    rT_bf = nc.dram_tensor("rT_bf", [128, 128], BF16, kind="ExternalInput").ap()
    ones_bf = nc.dram_tensor("ones_bf", [128, 128], BF16, kind="ExternalInput").ap()
    ones_f32 = nc.dram_tensor("ones_f32", [128, 128], F32, kind="ExternalInput").ap()
    onesf_t = nc.dram_tensor("onesf", [1, 128], F32, kind="ExternalInput").ap()
    out_t = nc.dram_tensor("out", [T, d], F32, kind="ExternalOutput").ap()

    aga_in = nc.dram_tensor("aga_in", [4 * d, OSL], BF16)
    aga_out = nc.dram_tensor("aga_out", [N_CORES * 4 * d, OSL], BF16, addr_space="Shared")
    agb1_in = nc.dram_tensor("agb1_in", [d, ISL], BF16)
    agb1_out = nc.dram_tensor("agb1_out", [N_CORES * d, ISL], BF16, addr_space="Shared")
    agb2_in = nc.dram_tensor("agb2_in", [I, OSL], BF16)
    agb2_out = nc.dram_tensor("agb2_out", [N_CORES * I, OSL], BF16, addr_space="Shared")
    ssz = ISL + OSL
    ags_in = nc.dram_tensor("ags_in", [ssz], F32)
    ags_out = nc.dram_tensor("ags_out", [N_CORES * ssz], F32, addr_space="Shared")
    m1_in = nc.dram_tensor("m1_in", [128], F32)
    m1_out = nc.dram_tensor("m1_out", [128], F32, addr_space="Shared")
    m2_in = nc.dram_tensor("m2_in", [128], F32)
    m2_out = nc.dram_tensor("m2_out", [128], F32, addr_space="Shared")
    agc_sz = NH * 128 * T * 2
    agc_in = nc.dram_tensor("agc_in", [agc_sz], BF16)
    agc_out = nc.dram_tensor("agc_out", [GROUP * agc_sz], BF16)
    xn_sp = nc.dram_tensor("xn_sp", [TC, 128, d], F32)
    h_sp = nc.dram_tensor("h_sp", [IC, 128, T], F32)
    attn_sp = nc.dram_tensor("attn_sp", [FC, 128, T], F32)
    rs_scr = nc.dram_tensor("rs_scr", [max(ISL, OSL)], F32)
    KOFF = 0
    VOFF = NH * 128 * T

    es = contextlib.ExitStack()
    SM = lambda name: Sem(nc, es, name)
    DS = lambda name, n: DSem(nc, es, name, n)
    ld_aux = SM("ld_aux")
    ld_x = DS("ld_x", 3); ld_w1 = DS("ld_w1", 2); ld_w2 = DS("ld_w2", 2)
    ld_wq = DS("ld_wq", 2); ld_wo = DS("ld_wo", 2); ld_q1 = DS("ld_q1", 3)
    ld_kv = DS("ld_kv", 2); ld_h = DS("ld_h", 2); ld_q2 = DS("ld_q2", 2)
    ld_fa = SM("ld_fa"); ld_fx = DS("ld_fx", 2); ld_sw = SM("ld_sw")
    ld_xq = DS("ld_xq", 2)
    st_out = DS("st_out", 4)
    st_xn = DS("st_xn", 3); st_k = DS("st_k", 1); st_v = DS("st_v", 1)
    st_h = DS("st_h", 2); st_w = DS("st_w", 2)
    g_dma = SM("g_dma"); g_csc = SM("g_csc")
    g_w1 = DS("g_w1", 2); g_w2 = DS("g_w2", 2)
    cc_a = SM("cc_a"); cc_b1 = SM("cc_b1"); cc_b2 = SM("cc_b2")
    cc_s = SM("cc_s"); cc_m1 = SM("cc_m1"); cc_m2 = SM("cc_m2"); cc_kv = SM("cc_kv")
    t_xtr = SM("t_xtr"); t_mx = SM("t_mx"); t_sw = SM("t_sw")
    t_p = SM("t_p"); t_r = SM("t_r"); t_vt = SM("t_vt"); t_f1 = SM("t_f1")
    t_S = SM("t_S"); t_A = SM("t_A"); t_O = SM("t_O"); t_wo = SM("t_wo")
    t_f2 = SM("t_f2"); t_F = SM("t_F")
    v_ve = SM("v_ve"); v_xnb = SM("v_xnb"); v_fme = SM("v_fme")
    v_lm = SM("v_lm"); v_mx1 = SM("v_mx1"); v_w2m = SM("v_w2m")
    v_mxr = SM("v_mxr"); v_wq1 = SM("v_wq1"); v_wq2 = SM("v_wq2")
    v_swc = SM("v_swc"); v_sx = SM("v_sx"); v_rope = SM("v_rope")
    v_vte = SM("v_vte"); v_qx = SM("v_qx"); v_hx = SM("v_hx"); v_att = SM("v_att")
    v_qh = SM("v_qh"); v_fin = SM("v_fin"); v_ot = SM("v_ot"); v_pp1 = SM("v_pp1")
    s_ln = SM("s_ln"); s_qb = SM("s_qb"); s_g = SM("s_g"); s_exp = SM("s_exp")
    s_rec = SM("s_rec"); s_wo = SM("s_wo"); s_f2 = SM("s_f2")
    v_acc = v_vte; s_p1 = s_qb
    t_pb = SM("t_pb"); t_bc = SM("t_bc"); t_lm = SM("t_lm")
    g_rr1 = SM("g_rr1"); g_rr2 = SM("g_rr2"); g_gl1 = SM("g_gl1"); g_gl2 = SM("g_gl2")


    sb = lambda name, shape, dt: es.enter_context(nc.sbuf_tensor(name, shape, dt))
    id_sb = sb("id_sb", [128, 128], BF16)
    idf_sb = sb("idf_sb", [128, 128], F32)
    rT_sb = sb("rT_sb", [128, 128], BF16)
    ones_sb = sb("ones_sb", [128, 128], BF16)
    onesf32_sb = sb("onesf32_sb", [128, 128], F32)
    bq_sb = sb("bq_sb", [128, NH], F32)
    bk_sb = sb("bk_sb", [128, NH], F32)
    bv_sb = sb("bv_sb", [128, NH], F32)
    b1_sb = sb("b1_sb", [128, IC], F32)
    b2_sb = sb("b2_sb", [128, FC], F32)
    sw1_col = sb("sw1_col", [128, IC], F32)
    sw2_col = sb("sw2_col", [128, FC], F32)
    s1_col = sb("s1_col", [128, IC], F32)
    s2_col = sb("s2_col", [128, FC], F32)
    onesf_sb = sb("onesf_sb", [1, 128], F32)
    g1row_sb = sb("g1row_sb", [1, 128], F32)
    g2row_sb = sb("g2row_sb", [1, 128], F32)
    gm1_sb = sb("gm1_sb", [1, 1], F32)
    gm2_sb = sb("gm2_sb", [1, 1], F32)
    sx1_col = sb("sx1_col", [128, 1], F32)
    sx2_col = sb("sx2_col", [128, 1], F32)
    r1_col = sb("r1_col", [128, 1], F32)
    r2_col = sb("r2_col", [128, 1], F32)
    csc = sb("csc", [128, 8], F32)
    mv_all = sb("mv_all", [128, 2 * TC], F32)
    ve_all = sb("ve_all", [128, TC], F32)
    rstd_all = sb("rstd_all", [128, TC], F32)
    ln_tmp = sb("ln_tmp", [128, TC], F32)
    lmax_sb = sb("lmax_sb", [128, TC], F32)
    hmax_sb = sb("hmax_sb", [128, IC], F32)

    blk = es.enter_context(nc.Block())
    ppi = {"w": 0}

    esQR = contextlib.ExitStack()
    q_rope = esQR.enter_context(nc.sbuf_tensor("q_rope", [128, NH * T], BF16))
    esA2 = contextlib.ExitStack()
    A2 = lambda n, s, dt: esA2.enter_context(nc.sbuf_tensor(n, s, dt))
    xn_fm = A2("xn_fm", [128, FC * T], BF16)
    cos_sb = A2("cos_sb", [128, T], F32)
    sin_sb = A2("sin_sb", [128, T], F32)
    esA1a = contextlib.ExitStack()
    A1a = lambda n, s, dt: esA1a.enter_context(nc.sbuf_tensor(n, s, dt))
    x_tiles = A1a("x_tiles", [128, 3 * d], F32)
    bn_scr = A1a("bn_scr", [128, 6 * n_bn], F32)
    swsl = A1a("swsl", [max(ISL, OSL) // 128, 128], F32)
    esP2 = contextlib.ExitStack()
    P2 = lambda n, s, dt: esP2.enter_context(nc.sbuf_tensor(n, s, dt))
    wsl3 = P2("wsl3", [128, 2 * 3 * FC * 128], BF16)
    qb_sb = P2("qb_sb", [128, T], BF16)
    vb_sb = P2("vb_sb", [128, T], F32)
    vtm_sb = P2("vtm_sb", [128, T], BF16)
    kr_sb = P2("kr_sb", [128, T], BF16)
    m1r_sb = P2("m1r_sb", [128, T], F32)
    m2r_sb = P2("m2r_sb", [128, T], F32)
    esA1b = contextlib.ExitStack()
    A1b = lambda n, s, dt: esA1b.enter_context(nc.sbuf_tensor(n, s, dt))
    w1p = A1b("w1p", [128, 2 * FC * 128], F32)
    w1q_bf = A1b("w1q_bf", [128, 2 * FC * 128], BF16)
    w2p = A1b("w2p", [128, 2 * W2C * OSL], F32)
    w2q_bf = A1b("w2q_bf", [128, 2 * W2C * OSL], BF16)
    mxw = A1b("mxw", [128, max(128, OSL)], F32)
    mnw = A1b("mnw", [128, max(128, OSL)], F32)
    mxq = A1b("mxq", [128, max(128, OSL)], F32)
    rep1 = A1b("rep1", [128, 128], F32)
    rep2 = A1b("rep2", [128, OSL], F32)
    rr1 = A1b("rr1", [1, 128], F32)
    rr2 = A1b("rr2", [1, OSL], F32)
    esXY = contextlib.ExitStack()
    pX = esXY.enter_context(nc.psum_tensor("pX", [128, T], F32))
    pY = esXY.enter_context(nc.psum_tensor("pY", [128, T], F32))
    esT = contextlib.ExitStack()
    pT = esT.enter_context(nc.psum_tensor("pT", [128, 1024], F32))
    pP1 = esT.enter_context(nc.psum_tensor("pP1", [128, 128], F32))

    w1r = w1T.rearrange("(f p) i -> p f i", p=128)
    w2r = w2T.rearrange("(f p) o -> p f o", p=128)
    agar = aga_out.ap().rearrange("(r j f p) o -> r j p f o", j=4, f=FC, p=128)
    agb1r = agb1_out.ap().rearrange("(r f p) i -> r p f i", f=FC, p=128)
    agb2r = agb2_out.ap().rearrange("(r f p) o -> r p f o", f=IC, p=128)
    agb1w = agb1_in.ap().rearrange("(f p) i -> p f i", p=128)
    agb2w = agb2_in.ap().rearrange("(c f p) o -> c p f o", f=W2C, p=128)
    xcolr = x_in.rearrange("(tc p) o -> p tc o", p=128)

    units = ([("w1", j) for j in range(NW1)]
             + [("w2m", j) for j in range(N2C)]
             + [("w2s", j) for j in range(NW2)]
             + [("pb2", j) for j in range(NW2)]
             + [("w2q", j) for j in range(N2C)])
    NHP = max(1, NH // 2)
    per_head = [units[(len(units) * h) // NHP:(len(units) * (h + 1)) // NHP]
                if h < NHP else [] for h in range(NH)]


    @blk.sync
    def _(sync):
        if STOP < 1:
            return
        for src, dst in [(cos_t, cos_sb), (sin_t, sin_sb), (id_bf, id_sb),
                         (id_f32, idf_sb), (rT_bf, rT_sb), (ones_bf, ones_sb),
                         (ones_f32, onesf32_sb),
                         (bq_t, bq_sb), (bk_t, bk_sb), (bv_t, bv_sb),
                         (b1_t, b1_sb), (b2_t, b2_sb), (onesf_t, onesf_sb)]:
            ld_aux.inc(sync.dma_start(out=dst[:, :], in_=src), 16)
        for tc in range(TC):
            sl = tc % 3
            ld_x.gate(sync, sl)
            if tc >= 3:
                st_xn.wait(sync, (tc - 3) % 3, 16 * ((tc - 3) // 3 + 1))
                sync.wait_ge(t_xtr.h, 2 * (tc - 2))
            ld_x.issue(sync, sl,
                       x_tiles[:, sl * d:sl * d + d],
                       x_in[tc * 128:(tc + 1) * 128, :])

    @blk.gpsimd
    def _(g):
        if STOP < 2:
            return
        for j, w in enumerate([wqT, wkT, wvT, woT]):
            g_dma.inc(g.dma_start(out=aga_in[j * d:j * d + d // 2, :],
                                  in_=w[0:d // 2, :]), 16)
            g_dma.inc(g.dma_start(out=aga_in[j * d + d // 2:(j + 1) * d, :],
                                  in_=w[d // 2:d, :]), 16)
        g.wait_ge(g_dma.h, g_dma.c)
        cc_a.inc(g.collective_compute(
            "AllGather", ALU.bypass, replica_groups=[list(range(N_CORES))],
            ins=[aga_in.ap().opt()], outs=[aga_out.ap().opt()]))
        for j in range(NW1):
            g.wait_ge(v_mxr.h, j + 1)
            g_csc.inc(g.dma_start(out=rs_scr[j * 128:(j + 1) * 128],
                                  in_=csc[:, 2 + (j % 2):3 + (j % 2)]), 16)
            g_csc.inc(g.dma_start(out=ags_in[j * 128:(j + 1) * 128],
                                  in_=csc[:, 4 + (j % 2):5 + (j % 2)]), 16)
            g.wait_ge(g_csc.h, g_csc.c)
            if j >= 1:
                g.wait_ge(t_pb.h, j)
            g_rr1.inc(g.dma_start(out=rr1[:, :],
                                  in_=rs_scr[j * 128:(j + 1) * 128]), 16)
            g.wait_ge(v_wq1.h, j + 1)
            for q4 in range(4):
                fs = FC // 4
                g_w1.issue(g, j % 2,
                           agb1w[:, q4 * fs:(q4 + 1) * fs, j * 128:(j + 1) * 128],
                           w1q_bf.rearrange("p (b f w) -> p b f w", b=2, f=FC)
                           [:, j % 2, q4 * fs:(q4 + 1) * fs])
        g.wait_ge(g_dma.h, g_dma.c)
        g_w1.wait_all(g)
        cc_b1.inc(g.collective_compute(
            "AllGather", ALU.bypass, replica_groups=[list(range(N_CORES))],
            ins=[agb1_in.ap().opt()], outs=[agb1_out.ap().opt()]))
        for j in range(NW2):
            g.wait_ge(v_mxr.h, NW1 + j + 1)
            g_csc.inc(g.dma_start(out=rs_scr[j * 128:(j + 1) * 128],
                                  in_=csc[:, 2 + (j % 2):3 + (j % 2)]), 16)
            g_csc.inc(g.dma_start(out=ags_in[ISL + j * 128:ISL + (j + 1) * 128],
                                  in_=csc[:, 4 + (j % 2):5 + (j % 2)]), 16)
        g.wait_ge(g_csc.h, g_csc.c)
        g_rr2.inc(g.dma_start(out=rr2[:, :], in_=rs_scr[0:OSL]), 16)
        for j in range(N2C):
            g.wait_ge(v_wq2.h, j + 1)
            g_w2.issue(g, j % 2,
                       agb2w[j],
                       w2q_bf.rearrange("p (b f w) -> p b f w", b=2, f=W2C)[:, j % 2])
        g_w2.wait_all(g)
        cc_b2.inc(g.collective_compute(
            "AllGather", ALU.bypass, replica_groups=[list(range(N_CORES))],
            ins=[agb2_in.ap().opt()], outs=[agb2_out.ap().opt()]))
        cc_s.inc(g.collective_compute(
            "AllGather", ALU.bypass, replica_groups=[list(range(N_CORES))],
            ins=[ags_in.ap().opt()], outs=[ags_out.ap().opt()]))
        if STOP < 3:
            return
        g.wait_ge(v_lm.h, 2)
        g_dma.inc(g.dma_start(out=m1_in[0:128], in_=g1row_sb[:, :]), 16)
        g.wait_ge(g_dma.h, g_dma.c)
        cc_m1.inc(g.collective_compute(
            "AllReduce", ALU.max, replica_groups=[list(range(N_CORES))],
            ins=[m1_in.ap().opt()], outs=[m1_out.ap().opt()]))
        g.wait_ge(cc_m1.h, 1)
        g_gl1.inc(g.dma_start(out=sx1_col[:, :],
                              in_=m1_out.ap().rearrange("(p o) -> p o", o=1)), 16)
        if STOP < 4:
            pass
        st_k.wait(g, 0, 16 * NH)
        st_v.wait(g, 0, 32 * NH)
        cc_kv.inc(g.collective_compute(
            "AllGather", ALU.bypass,
            replica_groups=[list(range(G * GROUP, (G + 1) * GROUP))
                            for G in range(N_CORES // GROUP)],
            ins=[agc_in.ap().opt()], outs=[agc_out.ap().opt()]))
        g.wait_ge(v_lm.h, 4)
        g_dma.inc(g.dma_start(out=m2_in[0:128], in_=g2row_sb[:, :]), 16)
        g.wait_ge(g_dma.h, g_dma.c)
        cc_m2.inc(g.collective_compute(
            "AllReduce", ALU.max, replica_groups=[list(range(N_CORES))],
            ins=[m2_in.ap().opt()], outs=[m2_out.ap().opt()]))
        g.wait_ge(cc_m2.h, 1)
        g_gl2.inc(g.dma_start(out=sx2_col[:, :],
                              in_=m2_out.ap().rearrange("(p o) -> p o", o=1)), 16)

    @blk.scalar
    def _(scalar):
        if STOP < 1:
            return
        for tc in range(TC):
            scalar.wait_ge(v_ve.h, tc + 1)
            scalar.activation(ln_tmp[:, tc:tc + 1], ve_all[:, tc:tc + 1], AF.Ln)
            scalar.drain()
            s_ln.inc(scalar.activation(rstd_all[:, tc:tc + 1],
                                       ln_tmp[:, tc:tc + 1], AF.Exp, scale=np.float32(-0.5)))
            if tc >= 1:
                scalar.wait_ge(v_xnb.h, tc)
                st_xn.gate(scalar, (tc - 1) % 3)
                st_xn.issue(scalar, (tc - 1) % 3, xn_sp.ap()[tc - 1],
                            x_tiles[:, ((tc - 1) % 3) * d:((tc - 1) % 3) * d + d])
        scalar.wait_ge(v_xnb.h, TC)
        st_xn.gate(scalar, (TC - 1) % 3)
        st_xn.issue(scalar, (TC - 1) % 3, xn_sp.ap()[TC - 1],
                    x_tiles[:, ((TC - 1) % 3) * d:((TC - 1) % 3) * d + d])

    @blk.vector
    def _(vec):
        if STOP < 1:
            return
        def norm(tc):
            vec.wait_ge(s_ln.h, tc + 1)
            xt = x_tiles[:, (tc % 3) * d:(tc % 3) * d + d]
            vec.tensor_scalar(xt[:, :], xt[:, :], mv_all[:, 2 * tc:2 * tc + 1],
                              rstd_all[:, tc:tc + 1], ALU.subtract, ALU.mult)
            vec.drain()
            vec.tensor_reduce(lmax_sb[:, tc:tc + 1], xt[:, :], AX.X, ALU.max,
                              apply_absolute_value=True)
            v_xnb.inc(vec.drain())
        def xn_evac(jt):
            for half in range(2):
                vec.wait_ge(t_xtr.h, 2 * jt + half + 1)
                for jj in range(FH):
                    fc = half * FH + jj
                    vec.tensor_copy(xn_fm[:, fc * T + jt * 128: fc * T + (jt + 1) * 128],
                                    pT[:, jj * 128:(jj + 1) * 128])
                v_fme.inc(vec.drain())
        for tc in range(TC):
            ld_x.wait(vec, tc % 3, 16 * (tc // 3 + 1))
            xt = x_tiles[:, (tc % 3) * d:(tc % 3) * d + d]
            for j in range(n_bn):
                vec.bn_stats(bn_scr[:, 6 * j:6 * j + 6], xt[:, 512 * j:512 * (j + 1)])
            vec.drain()
            vec.bn_aggr(mv_all[:, 2 * tc:2 * tc + 2], bn_scr[:, :])
            vec.drain()
            vec.tensor_scalar(ve_all[:, tc:tc + 1], mv_all[:, 2 * tc + 1:2 * tc + 2],
                              1e-5, None, ALU.add)
            v_ve.inc(vec.drain())
            if tc >= 1:
                norm(tc - 1)
            if tc >= 2:
                xn_evac(tc - 2)
        norm(TC - 1)
        for jt in range(max(0, TC - 2), TC):
            xn_evac(jt)
        vec.tensor_reduce(csc[:, 0:1], lmax_sb[:, :], AX.X, ALU.max)
        v_lm.inc(vec.drain())

    @blk.tensor
    def _(tensor):
        if STOP < 1:
            return
        tensor.wait_ge(ld_aux.h, ld_aux.c)
        for tc in range(TC):
            tensor.wait_ge(v_xnb.h, tc + 1)
            for half in range(2):
                if 2 * tc + half >= 1:
                    tensor.wait_ge(v_fme.h, 2 * tc + half)
                for j in range(FH):
                    fc = half * FH + j
                    mm = tensor.transpose(
                        pT[:, j * 128:(j + 1) * 128],
                        x_tiles[:, (tc % 3) * d + fc * 128:(tc % 3) * d + fc * 128 + 128],
                        idf_sb[:, :])
                t_xtr.inc(mm)
        tensor.wait_ge(v_lm.h, 1)
        tensor.wait_ge(v_pp1.h, ppi["w"])
        t_lm.inc(tensor.matmul(pP1[0:1, :], csc[:, 0:1], idf_sb[:, :],
                               start=True, stop=True))
        ppi["w"] += 1

    @blk.vector
    def _(vec):
        if STOP < 1:
            return
        vec.wait_ge(t_lm.h, 1)
        vec.tensor_reduce(gm1_sb[:, :], pP1[0:1, :], AX.X, ALU.max)
        vec.drain()
        vec.tensor_scalar(g1row_sb[:, :], pP1[0:1, :], gm1_sb[0:1, 0:1],
                          None, ALU.max)
        vec.drain()
        v_pp1.inc(vec.drain())
        v_lm.inc(vec.drain())

    def _unit_load(sync, kind, j):
        if kind == "w1":
            ld_w1.gate(sync, j % 2)
            if j >= 2:
                sync.wait_ge(v_wq1.h, j - 1)
            for q4 in range(4):
                fs = FC // 4
                ld_w1.issue(sync, j % 2,
                            w1p.rearrange("p (b f w) -> p b f w", b=2, f=FC)
                            [:, j % 2, q4 * fs:(q4 + 1) * fs],
                            w1r[:, q4 * fs:(q4 + 1) * fs, j * 128:(j + 1) * 128])
        elif kind == "w2m":
            ld_w2.gate(sync, j % 2)
            if j >= 2:
                sync.wait_ge(v_w2m.h, j - 1)
            ld_w2.issue(sync, j % 2,
                        w2p.rearrange("p (b f w) -> p b f w", b=2, f=W2C)[:, j % 2],
                        w2r[:, j * W2C:(j + 1) * W2C, :])
        elif kind == "w2q":
            ld_w2.gate(sync, j % 2)
            if j == 0:
                sync.wait_ge(v_mx1.h, NW1 + 1)
            if j >= 2:
                sync.wait_ge(v_wq2.h, j - 1)
            ld_w2.issue(sync, j % 2,
                        w2p.rearrange("p (b f w) -> p b f w", b=2, f=W2C)[:, j % 2],
                        w2r[:, j * W2C:(j + 1) * W2C, :])

    @blk.sync
    def _(sync):
        if STOP < 2:
            return
        sync.wait_ge(cc_a.h, 1)
        for kind, j in per_head[0]:
            _unit_load(sync, kind, j)
        for h in range(NH):
            if h >= 2:
                sync.wait_ge(t_p.h, 3 * (h - 1))
            base = (h % 2) * 3 * FC * 128
            r = (h * 128) // OSL
            co = (h * 128) % OSL
            ld_wq.gate(sync, h % 2)
            fs = FC // 4
            for pj in range(3):
                dst = wsl3[:, base + pj * FC * 128: base + (pj + 1) * FC * 128]
                for q4 in range(4):
                    ld_wq.issue(sync, h % 2,
                                dst.rearrange("p (f w) -> p f w", f=FC)
                                [:, q4 * fs:(q4 + 1) * fs],
                                agar[r, pj, :, q4 * fs:(q4 + 1) * fs, co:co + 128])
            if h + 1 < NH:
                for kind, j in per_head[h + 1]:
                    _unit_load(sync, kind, j)

    @blk.tensor
    def _(tensor):
        if STOP < 2:
            return
        nmx = 0
        for h in range(NH):
            for kind, j in per_head[h]:
                if kind == "w1":
                    tensor.wait_ge(v_mx1.h, nmx + 1)
                    tensor.wait_ge(v_pp1.h, ppi["w"])
                    t_mx.inc(tensor.transpose(pP1[:, :], mxw[:, 0:128], idf_sb[:, :]))
                    ppi["w"] += 1
                    nmx += 1
                    tensor.wait_ge(g_rr1.h, 16 * (j + 1))
                    tensor.wait_ge(v_pp1.h, ppi["w"])
                    t_pb.inc(tensor.matmul(pP1[:, :], onesf_sb[:, :], rr1[:, :],
                                            start=True, stop=True))
                    ppi["w"] += 1
                elif kind == "w2s":
                    tensor.wait_ge(v_mx1.h, nmx + 1)
                    tensor.wait_ge(v_pp1.h, ppi["w"])
                    t_mx.inc(tensor.transpose(pP1[:, :], mxw[:, j * 128:(j + 1) * 128],
                                               idf_sb[:, :]))
                    ppi["w"] += 1
                    nmx += 1
                elif kind == "pb2":
                    tensor.wait_ge(g_rr2.h, 16)
                    tensor.wait_ge(v_pp1.h, ppi["w"])
                    t_pb.inc(tensor.matmul(pP1[:, :], onesf_sb[:, :],
                                            rr2[:, j * 128:(j + 1) * 128],
                                            start=True, stop=True))
                    ppi["w"] += 1
            base = (h % 2) * 3 * FC * 128
            ld_wq.wait(tensor, h % 2, 192 * (h // 2 + 1))
            if h == 0:
                tensor.wait_ge(v_fme.h, 2 * TC)
            tensor.wait_ge(s_qb.h, 3 * h)
            for fc in range(FC):
                wt = wsl3[:, base + fc * 128: base + fc * 128 + 128]
                for (o, w) in PC:
                    mm = tensor.matmul(pX[:, o:o + w], wt,
                                       xn_fm[:, fc * T + o: fc * T + o + w],
                                       start=(fc == 0), stop=(fc == FC - 1))
            t_p.inc(mm)
            tensor.wait_ge(s_qb.h, 3 * h + 1)
            if h > 0:
                tensor.wait_ge(v_rope.h, 2 * h)
            for (o, w) in PC:
                mm = tensor.matmul(pY[:, o:o + w], rT_sb[:, :], qb_sb[:, o:o + w],
                                   start=True, stop=True)
            t_r.inc(mm)
            for fc in range(FC):
                wt = wsl3[:, base + FC * 128 + fc * 128: base + FC * 128 + fc * 128 + 128]
                for (o, w) in PC:
                    mm = tensor.matmul(pX[:, o:o + w], wt,
                                       xn_fm[:, fc * T + o: fc * T + o + w],
                                       start=(fc == 0), stop=(fc == FC - 1))
            t_p.inc(mm)
            tensor.wait_ge(s_qb.h, 3 * h + 2)
            tensor.wait_ge(v_rope.h, 2 * h + 1)
            for (o, w) in PC:
                mm = tensor.matmul(pY[:, o:o + w], rT_sb[:, :], qb_sb[:, o:o + w],
                                   start=True, stop=True)
            t_r.inc(mm)
            for fc in range(FC):
                wt = wsl3[:, base + 2 * FC * 128 + fc * 128: base + 2 * FC * 128 + fc * 128 + 128]
                for (o, w) in PC:
                    mm = tensor.matmul(pX[:, o:o + w], wt,
                                       xn_fm[:, fc * T + o: fc * T + o + w],
                                       start=(fc == 0), stop=(fc == FC - 1))
            t_p.inc(mm)
            tensor.wait_ge(s_qb.h, 3 * h + 3)
            tensor.wait_ge(v_vte.h, h)
            if h == 0:
                tensor.wait_ge(v_fme.h, 2 * TC)
            for tc in range(TC):
                mm = tensor.transpose(pT[:, tc * 128:(tc + 1) * 128],
                                      vb_sb[:, tc * 128:(tc + 1) * 128], idf_sb[:, :])
            t_vt.inc(mm)

    @blk.scalar
    def _(scalar):
        if STOP < 2:
            return
        scalar.wait_ge(ld_aux.h, ld_aux.c)
        for h in range(NH):
            scalar.wait_ge(t_p.h, 3 * h + 1)
            if h > 0:
                scalar.wait_ge(v_rope.h, 2 * h)
            s_qb.inc(scalar.activation(qb_sb[:, :], pX[:, :], AF.Identity,
                                       bias=bq_sb[:, h:h + 1]))
            scalar.wait_ge(t_p.h, 3 * h + 2)
            scalar.wait_ge(v_rope.h, 2 * h + 1)
            s_qb.inc(scalar.activation(qb_sb[:, :], pX[:, :], AF.Identity,
                                       bias=bk_sb[:, h:h + 1]))
            scalar.wait_ge(t_p.h, 3 * h + 3)
            if h > 0:
                scalar.wait_ge(t_vt.h, h)
            s_qb.inc(scalar.activation(vb_sb[:, :], pX[:, :], AF.Identity,
                                       bias=bv_sb[:, h:h + 1]))
            scalar.wait_ge(v_rope.h, 2 * h + 2)
            st_k.gate(scalar, 0)
            st_k.issue(scalar, 0,
                       agc_in.ap()[KOFF + h * 128 * T: KOFF + (h + 1) * 128 * T]
                       .rearrange("(p t) -> p t", p=128),
                       kr_sb[:, :])
            scalar.wait_ge(v_vte.h, h + 1)
            st_v.gate(scalar, 0)
            hh = TC // 2
            for h2 in range(2):
                st_v.issue(scalar, 0,
                           agc_in.ap()[VOFF + h * T * 128: VOFF + (h + 1) * T * 128]
                           .rearrange("(tc p dc) -> p tc dc", p=128, dc=128)
                           [:, h2 * hh:(h2 + 1) * hh],
                           vtm_sb.ap().rearrange("p (tc dc) -> p tc dc", dc=128)
                           [:, h2 * hh:(h2 + 1) * hh])

    @blk.vector
    def _(vec):
        if STOP < 2:
            return
        vec.wait_ge(ld_aux.h, ld_aux.c)
        nmx_v = 0
        w2fold = {"n": 0}
        for h in range(NH):
            for kind, j in per_head[h]:
                if kind == "w1":
                    ld_w1.wait(vec, j % 2, 64 * (j // 2 + 1))
                    src = w1p.rearrange("p (b f w) -> p b f w", b=2, f=FC)[:, j % 2]
                    vec.tensor_tensor(mxw[:, 0:128], src[:, 0], src[:, 1], ALU.max)
                    vec.tensor_tensor(mnw[:, 0:128], src[:, 0], src[:, 1], ALU.min)
                    for fc in range(2, FC):
                        vec.drain()
                        vec.tensor_tensor(mxw[:, 0:128], mxw[:, 0:128], src[:, fc],
                                          ALU.max)
                        vec.tensor_tensor(mnw[:, 0:128], mnw[:, 0:128], src[:, fc],
                                          ALU.min)
                    vec.drain()
                    vec.tensor_scalar(mnw[:, 0:128], mnw[:, 0:128], -1.0, None, ALU.mult)
                    vec.drain()
                    vec.tensor_tensor(mxw[:, 0:128], mxw[:, 0:128], mnw[:, 0:128],
                                      ALU.max)
                    v_mx1.inc(vec.drain())
                    nmx_v += 1
                    vec.wait_ge(t_mx.h, nmx_v)
                    vec.tensor_reduce(csc[:, 1:2], pP1[:, :], AX.X, ALU.max)
                    v_pp1.inc(vec.drain())
                    if j >= 2:
                        vec.wait_ge(g_csc.h, 32 * (j - 1))
                    vec.tensor_scalar(csc[:, 4 + (j % 2):5 + (j % 2)], csc[:, 1:2],
                                      1.0 / 127.0, 1e-8, ALU.mult, ALU.max)
                    vec.drain()
                    vec.reciprocal(csc[:, 2 + (j % 2):3 + (j % 2)],
                                   csc[:, 4 + (j % 2):5 + (j % 2)])
                    v_mxr.inc(vec.drain())
                    vec.wait_ge(t_pb.h, j + 1)
                    vec.tensor_copy(rep1[:, :], pP1[:, :])
                    vec.drain()
                    v_pp1.inc(vec.drain())
                    if j >= 2:
                        g_w1.wait(vec, j % 2, 64 * (j // 2))
                    dstv = w1q_bf.rearrange("p (b f w) -> p b f w", b=2, f=FC)[:, j % 2]
                    for fc in range(FC):
                        vec.tensor_tensor(mxq[:, 0:128], src[:, fc], rep1[:, :], ALU.mult)
                        vec.drain()
                        vec.tensor_scalar(mxq[:, 0:128], mxq[:, 0:128],
                                          RND_C, RND_C, ALU.add, ALU.subtract)
                        vec.drain()
                        vec.tensor_scalar(dstv[:, fc], mxq[:, 0:128],
                                          127.0, -127.0, ALU.min, ALU.max)
                        vec.drain()
                    v_wq1.inc(vec.drain())
                elif kind == "w2m":
                    ld_w2.wait(vec, j % 2, 16 * (j // 2 + 1))
                    if j == 0 and NW1 > 0:
                        vec.wait_ge(t_mx.h, NW1)
                    src = w2p.rearrange("p (b f w) -> p b f w", b=2, f=W2C)[:, j % 2]
                    for f in range(W2C):
                        if j == 0 and f == 0:
                            vec.tensor_tensor(mxw[:, 0:OSL], src[:, 0],
                                              src[:, min(1, W2C - 1)], ALU.max)
                            vec.tensor_tensor(mnw[:, 0:OSL], src[:, 0],
                                              src[:, min(1, W2C - 1)], ALU.min)
                            vec.drain()
                            continue
                        if j == 0 and f == 1 and W2C > 1:
                            continue
                        vec.tensor_tensor(mxw[:, 0:OSL], mxw[:, 0:OSL], src[:, f],
                                          ALU.max)
                        vec.tensor_tensor(mnw[:, 0:OSL], mnw[:, 0:OSL], src[:, f],
                                          ALU.min)
                        vec.drain()
                    v_w2m.inc(vec.drain())
                    w2fold["n"] += 1
                    if w2fold["n"] == N2C:
                        vec.tensor_scalar(mnw[:, 0:OSL], mnw[:, 0:OSL], -1.0,
                                          None, ALU.mult)
                        vec.drain()
                        vec.tensor_tensor(mxw[:, 0:OSL], mxw[:, 0:OSL],
                                          mnw[:, 0:OSL], ALU.max)
                        v_mx1.inc(vec.drain())
                elif kind == "w2s":
                    if j > 0:
                        v_mx1.inc(vec.drain())
                    nmx_v += 1
                    vec.wait_ge(t_mx.h, nmx_v)
                    vec.tensor_reduce(csc[:, 1:2], pP1[:, :], AX.X, ALU.max)
                    v_pp1.inc(vec.drain())
                    vec.wait_ge(g_csc.h, 32 * NW1)
                    vec.tensor_scalar(csc[:, 4 + (j % 2):5 + (j % 2)], csc[:, 1:2],
                                      1.0 / 127.0, 1e-8, ALU.mult, ALU.max)
                    vec.drain()
                    vec.reciprocal(csc[:, 2 + (j % 2):3 + (j % 2)],
                                   csc[:, 4 + (j % 2):5 + (j % 2)])
                    v_mxr.inc(vec.drain())
                elif kind == "pb2":
                    vec.wait_ge(t_pb.h, NW1 + j + 1)
                    vec.tensor_copy(rep2[:, j * 128:(j + 1) * 128], pP1[:, :])
                    vec.drain()
                    v_pp1.inc(vec.drain())
                else:
                    ld_w2.wait(vec, j % 2, 16 * (N2C // 2 + j // 2 + 1))
                    if j >= 2:
                        g_w2.wait(vec, j % 2, 16 * (j // 2))
                    src = w2p.rearrange("p (b f w) -> p b f w", b=2, f=W2C)[:, j % 2]
                    dstv = w2q_bf.rearrange("p (b f w) -> p b f w", b=2, f=W2C)[:, j % 2]
                    for f in range(W2C):
                        vec.tensor_tensor(mxq[:, 0:OSL], src[:, f], rep2[:, :], ALU.mult)
                        vec.drain()
                        vec.tensor_scalar(mxq[:, 0:OSL], mxq[:, 0:OSL],
                                          RND_C, RND_C, ALU.add, ALU.subtract)
                        vec.drain()
                        vec.tensor_scalar(dstv[:, f], mxq[:, 0:OSL],
                                          127.0, -127.0, ALU.min, ALU.max)
                        vec.drain()
                    v_wq2.inc(vec.drain())
            vec.wait_ge(s_qb.h, 3 * h + 1)
            vec.wait_ge(t_r.h, 2 * h + 1)
            vec.tensor_tensor(m1r_sb[:, :], qb_sb[:, :], cos_sb[:, :], ALU.mult)
            vec.tensor_tensor(m2r_sb[:, :], pY[:, :], sin_sb[:, :], ALU.mult)
            vec.drain()
            vec.tensor_tensor(q_rope[:, h * T:(h + 1) * T], m1r_sb[:, :],
                              m2r_sb[:, :], ALU.add)
            v_rope.inc(vec.drain())
            vec.wait_ge(s_qb.h, 3 * h + 2)
            vec.wait_ge(t_r.h, 2 * h + 2)
            if h >= 1:
                st_k.wait(vec, 0, 16 * h)
            vec.tensor_tensor(m1r_sb[:, :], qb_sb[:, :], cos_sb[:, :], ALU.mult)
            vec.tensor_tensor(m2r_sb[:, :], pY[:, :], sin_sb[:, :], ALU.mult)
            vec.drain()
            vec.tensor_tensor(kr_sb[:, :], m1r_sb[:, :], m2r_sb[:, :], ALU.add)
            v_rope.inc(vec.drain())
            vec.wait_ge(t_vt.h, h + 1)
            if h >= 1:
                st_v.wait(vec, 0, 32 * h)
            vec.tensor_copy(vtm_sb[:, :], pT[:, 0:T])
            v_vte.inc(vec.drain())

    esA1b.close()
    esP2.close()
    esP3 = contextlib.ExitStack()
    q1slab = esP3.enter_context(nc.sbuf_tensor("q1slab", [128, 3 * HSL], BF16))
    h_sb = esP3.enter_context(nc.sbuf_tensor("h_sb", [128, 2 * T], F32))

    K1 = ISL // 128
    K2 = OSL // 128

    @blk.sync
    def _(sync):
        if STOP < 3:
            return
        sync.wait_ge(cc_s.h, 1)
        for r in range(2 * N_CORES):
            if r >= 1:
                sync.wait_ge(t_sw.h, r)
            if r < N_CORES:
                src = ags_out.ap()[r * ssz: r * ssz + ISL].rearrange("(k p) -> k p", p=128)
                ld_sw.inc(sync.dma_start(out=swsl[0:K1, :], in_=src), 16)
            else:
                rr = r - N_CORES
                src = ags_out.ap()[rr * ssz + ISL: (rr + 1) * ssz].rearrange("(k p) -> k p", p=128)
                ld_sw.inc(sync.dma_start(out=swsl[0:K2, :], in_=src), 16)

    @blk.tensor
    def _(tensor):
        if STOP < 3:
            return
        for r in range(2 * N_CORES):
            k = K1 if r < N_CORES else K2
            tensor.wait_ge(ld_sw.h, 16 * (r + 1))
            tensor.wait_ge(v_pp1.h, ppi["w"])
            t_sw.inc(tensor.transpose(pP1[:, 0:k], swsl[0:k, :], idf_sb[0:k, 0:k]))
            ppi["w"] += 1

    @blk.vector
    def _(vec):
        if STOP < 3:
            return
        for r in range(2 * N_CORES):
            vec.wait_ge(t_sw.h, r + 1)
            if r < N_CORES:
                vec.tensor_copy(sw1_col[:, r * K1:(r + 1) * K1], pP1[:, 0:K1])
            else:
                rr = r - N_CORES
                vec.tensor_copy(sw2_col[:, rr * K2:(rr + 1) * K2], pP1[:, 0:K2])
            vec.drain()
            v_swc.inc(vec.drain())
            v_pp1.inc(vec.drain())
        vec.wait_ge(g_gl1.h, 16)
        vec.tensor_scalar(csc[:, 6:7], sx1_col[:, :], 1.0 / 127.0, 1e-8,
                          ALU.mult, ALU.max)
        vec.drain()
        vec.reciprocal(r1_col[:, :], csc[:, 6:7])
        vec.drain()
        vec.tensor_scalar(s1_col[:, :], sw1_col[:, :], csc[:, 6:7], None, ALU.mult)
        v_sx.inc(vec.drain())

    @blk.sync
    def _(sync):
        if STOP < 3:
            return
        st_xn.wait_all(sync)
        sync.wait_ge(t_xtr.h, 2 * TC)
        for tc in range(TC):
            ld_xq.gate(sync, tc % 2)
            if tc >= 2:
                sync.wait_ge(t_xtr.h, 2 * TC + 2 * (tc - 1))
            ld_xq.issue(sync, tc % 2,
                        x_tiles[:, (tc % 2) * d:(tc % 2) * d + d],
                        xn_sp.ap()[tc])

    @blk.vector
    def _(vec):
        if STOP < 3:
            return
        def qx_evac(jt):
            for half in range(2):
                vec.wait_ge(t_xtr.h, 2 * TC + 2 * jt + half + 1)
                for jj in range(FH):
                    fc = half * FH + jj
                    vec.tensor_copy(xn_fm[:, fc * T + jt * 128: fc * T + (jt + 1) * 128],
                                    pT[:, jj * 128:(jj + 1) * 128])
                v_fme.inc(vec.drain())
        vec.wait_ge(t_p.h, 3 * NH)
        for tc in range(TC):
            ld_xq.wait(vec, tc % 2, 16 * (tc // 2 + 1))
            xo = x_tiles[:, (tc % 2) * d:(tc % 2) * d + d]
            vec.tensor_scalar(xo[:, :], xo[:, :], r1_col[:, :], RND_C,
                              ALU.mult, ALU.add)
            vec.drain()
            vec.tensor_scalar(xo[:, :], xo[:, :], RND_C, 127.0,
                              ALU.subtract, ALU.min)
            vec.drain()
            vec.tensor_scalar(xo[:, :], xo[:, :], -127.0, None, ALU.max)
            v_qx.inc(vec.drain())
            if tc >= 1:
                qx_evac(tc - 1)
        qx_evac(TC - 1)

    @blk.tensor
    def _(tensor):
        if STOP < 3:
            return
        tensor.wait_ge(v_vte.h, NH)
        for tc in range(TC):
            tensor.wait_ge(v_qx.h, tc + 1)
            for half in range(2):
                tensor.wait_ge(v_fme.h, 2 * TC + 2 * tc + half)
                for j in range(FH):
                    fc = half * FH + j
                    mm = tensor.transpose(
                        pT[:, j * 128:(j + 1) * 128],
                        x_tiles[:, (tc % 2) * d + fc * 128:(tc % 2) * d + fc * 128 + 128],
                        idf_sb[:, :])
                t_xtr.inc(mm)

    @blk.sync
    def _(sync):
        if STOP < 3:
            return
        sync.wait_ge(cc_b1.h, 1)
        sync.wait_ge(v_wq2.h, N2C)
        sync.wait_ge(v_rope.h, 2 * NH)
        sync.wait_ge(v_vte.h, NH)
        st_k.wait(sync, 0, 16 * NH)
        st_v.wait(sync, 0, 32 * NH)
        for hf in range(2 * N_CORES):
            r, hh = hf // 2, hf % 2
            ld_q1.gate(sync, hf % 3)
            if hf >= 3:
                sync.wait_ge(t_f1.h, min(((hf - 3) // 2 + 1) * NW1, IC))
            ld_q1.issue(sync, hf % 3,
                        q1slab[:, (hf % 3) * HSL:(hf % 3 + 1) * HSL]
                        .rearrange("p (f i) -> p f i", f=FC // 2),
                        agb1r[r, :, hh * (FC // 2):(hh + 1) * (FC // 2), :])

    @blk.tensor
    def _(tensor):
        if STOP < 3:
            return
        tensor.wait_ge(v_fme.h, 4 * TC)
        tensor.wait_ge(s_qb.h, 3 * NH)
        tensor.wait_ge(v_rope.h, 2 * NH)
        for ic in range(IC):
            r = ic // NW1
            for hf in (2 * r, 2 * r + 1):
                ld_q1.wait(tensor, hf % 3, 16 * (hf // 3 + 1))
            if ic >= 2:
                tensor.wait_ge(s_g.h, ic - 1)
            pp = pX if ic % 2 == 0 else pY
            for fc in range(FC):
                slot = (2 * r + fc // (FC // 2)) % 3
                off = slot * HSL + (fc % (FC // 2)) * ISL + (ic % NW1) * 128
                wt = q1slab[:, off:off + 128]
                for (o, w) in PC:
                    mm = tensor.matmul(pp[:, o:o + w], wt,
                                       xn_fm[:, fc * T + o: fc * T + o + w],
                                       start=(fc == 0), stop=(fc == FC - 1))
            t_f1.inc(mm)

    @blk.scalar
    def _(scalar):
        if STOP < 3:
            return
        for ic in range(IC):
            scalar.wait_ge(t_f1.h, ic + 1)
            if ic == 0:
                scalar.wait_ge(v_sx.h, 1)
            if ic >= 2:
                scalar.wait_ge(v_hx.h, ic - 1)
            st_h.gate(scalar, ic % 2)
            pp = pX if ic % 2 == 0 else pY
            scalar.activation(h_sb[:, (ic % 2) * T:(ic % 2) * T + T], pp[:, :],
                              AF.Gelu, bias=b1_sb[:, ic:ic + 1],
                              scale=s1_col[:, ic:ic + 1])
            s_g.inc(scalar.drain())
            st_h.issue(scalar, ic % 2, h_sp.ap()[ic],
                       h_sb[:, (ic % 2) * T:(ic % 2) * T + T])

    @blk.vector
    def _(vec):
        if STOP < 3:
            return
        for ic in range(IC):
            vec.wait_ge(s_g.h, ic + 1)
            vec.tensor_reduce(hmax_sb[:, ic:ic + 1],
                              h_sb[:, (ic % 2) * T:(ic % 2) * T + T], AX.X,
                              ALU.max, apply_absolute_value=True)
            v_hx.inc(vec.drain())
        vec.wait_ge(t_lm.h, 1)
        vec.tensor_reduce(csc[:, 0:1], hmax_sb[:, :], AX.X, ALU.max)
        v_lm.inc(vec.drain())

    @blk.tensor
    def _(tensor):
        if STOP < 3:
            return
        tensor.wait_ge(v_lm.h, 3)
        tensor.wait_ge(v_pp1.h, ppi["w"])
        t_lm.inc(tensor.matmul(pP1[0:1, :], csc[:, 0:1], idf_sb[:, :],
                               start=True, stop=True))
        ppi["w"] += 1

    @blk.vector
    def _(vec):
        if STOP < 3:
            return
        vec.wait_ge(t_lm.h, 2)
        vec.tensor_reduce(gm2_sb[:, :], pP1[0:1, :], AX.X, ALU.max)
        vec.drain()
        vec.tensor_scalar(g2row_sb[:, :], pP1[0:1, :], gm2_sb[0:1, 0:1],
                          None, ALU.max)
        vec.drain()
        v_pp1.inc(vec.drain())
        v_lm.inc(vec.drain())

    esT.close()
    esP3.close()
    esA1a.close()
    esP4 = contextlib.ExitStack()
    P4 = lambda n, s, dt: esP4.enter_context(nc.sbuf_tensor(n, s, dt))
    k_sb = P4("k_sb", [128, 2 * S], BF16)
    v_sb = P4("v_sb", [128, 2 * S], BF16)
    exp_sb = P4("exp_sb", [128, 4 * T], BF16)
    recip_sb = P4("recip_sb", [128, T], F32)
    acc2 = P4("acc2", [128, 2 * T], F32)
    hrd_sb = P4("hrd_sb", [128, 2 * T], F32) if QA else None
    es_qha = contextlib.ExitStack()
    qh_a = es_qha.enter_context(
        nc.sbuf_tensor("qh_a", [128, QA * T], BF16, side="right")) if QA else None
    es_ao = contextlib.ExitStack()
    attn_out = es_ao.enter_context(
        nc.sbuf_tensor("attn_out", [128, NH * T], BF16, side="right"))

    _qh_b_ref = []

    def qh_sl(ic, o, w):
        if ic < QA:
            return qh_a[:, ic * T + o: ic * T + o + w]
        qb = _qh_b_ref[0]
        return qb[:, (ic - QA) * T + o: (ic - QA) * T + o + w]
    esPO = contextlib.ExitStack()
    pO = esPO.enter_context(nc.psum_tensor("pO", [128, T], F32))
    pU = [esPO.enter_context(nc.psum_tensor(f"pU{i}", [128, PC[0][1]], F32))
          for i in range(NP)]

    agcr = agc_out.ap().rearrange("(r e) -> r e", r=GROUP)
    accb = v_acc.c
    p1b = s_p1.c

    @blk.sync
    def _(sync):
        if STOP < 4:
            return
        sync.wait_ge(cc_kv.h, 1)
        sync.wait_ge(t_f1.h, IC)
        st_h.wait_all(sync)
        sync.wait_ge(v_hx.h, IC)
        for h in range(NH):
            slot = h % 2
            ld_kv.gate(sync, slot)
            if h >= 2:
                sync.wait_ge(t_O.h, h - 1)
            for r in range(GROUP):
                src = agcr[r, KOFF + h * 128 * T: KOFF + (h + 1) * 128 * T] \
                    .rearrange("(p t) -> p t", p=128)
                ld_kv.issue(sync, slot,
                            k_sb[:, slot * S + r * T: slot * S + (r + 1) * T], src)
            for r in range(GROUP):
                src = agcr[r, VOFF + h * T * 128: VOFF + (h + 1) * T * 128] \
                    .rearrange("(tc p dc) -> p tc dc", p=128, dc=128)
                dstv = v_sb[:, slot * S + r * T: slot * S + (r + 1) * T] \
                    .rearrange("p (tc dc) -> p tc dc", dc=128)
                hh = TC // 2
                for h2 in range(2):
                    ld_kv.issue(sync, slot,
                                dstv[:, h2 * hh:(h2 + 1) * hh],
                                src[:, h2 * hh:(h2 + 1) * hh])

    @blk.tensor
    def _(tensor):
        if STOP < 4:
            return
        pS = [pX, pY]
        flat = [(h, kc) for h in range(NH) for kc in range(KC)]

        def emit_scores(h, kc):
            idx = h * KC + kc
            if kc == 0:
                ld_kv.wait(tensor, h % 2, 16 * 3 * GROUP * (h // 2 + 1))
            if idx == 0:
                tensor.wait_ge(s_g.h, IC)
                tensor.wait_ge(v_pp1.h, ppi["w"])
                tensor.wait_ge(v_fme.h, 4 * TC)
            if idx >= 2:
                tensor.wait_ge(s_exp.h, idx - 1)
            kt = k_sb[:, (h % 2) * S + kc * 128: (h % 2) * S + (kc + 1) * 128]
            for (o, w) in PC:
                mm = tensor.matmul(pS[idx % 2][:, o:o + w], kt,
                                   q_rope[:, h * T + o: h * T + o + w],
                                   start=True, stop=True)
            t_S.inc(mm)

        def emit_av(h, kc):
            idx = h * KC + kc
            tensor.wait_ge(s_exp.h, idx + 1)
            if kc == 0:
                tensor.wait_ge(v_att.h, h)
                tensor.wait_ge(s_rec.h, h)
            vt = v_sb[:, (h % 2) * S + kc * 128: (h % 2) * S + (kc + 1) * 128]
            ex = exp_sb[:, (idx % 4) * T:(idx % 4) * T + T]
            for (o, w) in PC:
                tensor.matmul(pO[:, o:o + w], vt, ex[:, o:o + w],
                              start=(kc == 0), stop=(kc == KC - 1))
            if kc % 2 == 1:
                for pi, (o, w) in enumerate(PC):
                    tensor.matmul(pU[pi][:, :], ones_sb[:, :], ex[:, o:o + w],
                                  start=(kc == 1), stop=False)
            t_A.inc(tensor.drain())
            if kc == KC - 1:
                tensor.wait_ge(v_acc.h, accb + (h + 1) * (KC // 2))
                for pi, (o, w) in enumerate(PC):
                    tensor.matmul(pU[pi][:, :], onesf32_sb[:, :],
                                  acc2[:, (h % 2) * T + o:(h % 2) * T + o + w],
                                  start=False, stop=True)
                t_O.inc(tensor.drain())

        emit_scores(*flat[0])
        emit_scores(*flat[1])
        for i, (h, kc) in enumerate(flat):
            if i + 2 < len(flat):
                emit_scores(*flat[i + 2])
            emit_av(h, kc)

    @blk.scalar
    def _(scalar):
        if STOP < 4:
            return
        st_h.wait_all(scalar)
        for h in range(NH):
            for kc in range(KC):
                scalar.wait_ge(t_S.h, h * KC + kc + 1)
                if h * KC + kc >= 4:
                    scalar.wait_ge(t_A.h, h * KC + kc - 3)
                    j = h * KC + kc - 4
                    jh, jk = j // KC, j % KC
                    if jk % 2 == 0:
                        scalar.wait_ge(v_acc.h,
                                       accb + jh * (KC // 2) + jk // 2 + 1)
                idx = h * KC + kc
                s_exp.inc(scalar.activation(
                    exp_sb[:, (idx % 4) * T:(idx % 4) * T + T],
                    (pX if idx % 2 == 0 else pY)[:, :],
                    AF.Exp, scale=EXPS, bias=EXPB))
            scalar.wait_ge(t_O.h, h + 1)
            if h > 0:
                scalar.wait_ge(v_att.h, h)
            for pi, (o, w) in enumerate(PC):
                scalar.activation(recip_sb[:, o:o + w], pU[pi][:, :], AF.Ln)
            scalar.drain()
            s_rec.inc(scalar.activation(recip_sb[:, :], recip_sb[:, :],
                                        AF.Exp, scale=np.float32(-1.0)))

    @blk.vector
    def _(vec):
        if STOP < 4:
            return
        for h in range(NH):
            acc = acc2[:, (h % 2) * T:(h % 2) * T + T]
            for kc in range(0, KC, 2):
                idx = h * KC + kc
                vec.wait_ge(s_exp.h, idx + 1)
                ex = exp_sb[:, (idx % 4) * T:(idx % 4) * T + T]
                if kc == 0:
                    vec.tensor_scalar(acc[:, :], ex[:, :], 0.0,
                                      None, ALU.add)
                else:
                    vec.tensor_tensor(acc[:, :], acc[:, :], ex[:, :], ALU.add)
                v_acc.inc(vec.drain())
            vec.wait_ge(s_rec.h, h + 1)
            vec.tensor_tensor(attn_out[:, h * T:(h + 1) * T], pO[:, :],
                              recip_sb[:, :], ALU.mult)
            v_att.inc(vec.drain())

    esP4.close()
    esA2.close()
    esQR.close()
    esPO.close()
    esXY.close()
    esQH = contextlib.ExitStack()
    qh_b = esQH.enter_context(nc.sbuf_tensor("qh_b", [128, max(QB, 1) * T], BF16))
    _qh_b_ref.append(qh_b)
    hrd2_sb = esQH.enter_context(nc.sbuf_tensor("hrd2_sb", [128, 2 * T], F32))
    h2q_sb = esQH.enter_context(nc.sbuf_tensor("h2q_sb", [128, 2 * T], F32))

    esP5 = contextlib.ExitStack()
    P5 = lambda n, s, dt: esP5.enter_context(nc.sbuf_tensor(n, s, dt))
    wo_sb = P5("wo_sb", [128, 2 * FC * 128], BF16)
    asp_sb = P5("asp_sb", [128, 2 * T], F32)
    esW = contextlib.ExitStack()
    pW0 = esW.enter_context(nc.psum_tensor("pW0", [128, T], F32))
    pW1 = esW.enter_context(nc.psum_tensor("pW1", [128, T], F32))
    pF = esW.enter_context(nc.psum_tensor("pF", [128, 1024], F32))

    def pF_sl(s):
        c = (s % 2) * 512 + (s // 2) * 128
        return pF[:, c:c + 128]

    @blk.sync
    def _(sync):
        if STOP < 5:
            return
        sync.wait_ge(t_O.h, NH)
        st_h.wait_all(sync)
        for oc in range(FC):
            ld_wo.gate(sync, oc % 2)
            if oc >= 2:
                sync.wait_ge(t_wo.h, oc - 1)
            r = (oc * 128) // OSL
            co = (oc * 128) % OSL
            dst = wo_sb[:, (oc % 2) * FC * 128:(oc % 2 + 1) * FC * 128]
            fs = FC // 4
            for q4 in range(4):
                ld_wo.issue(sync, oc % 2,
                            dst.rearrange("p (f w) -> p f w", f=FC)
                            [:, q4 * fs:(q4 + 1) * fs],
                            agar[r, 3, :, q4 * fs:(q4 + 1) * fs, co:co + 128])


    @blk.gpsimd
    def _(g):
        if STOP < 5:
            return
        g.wait_ge(t_O.h, NH)
        g.wait_ge(s_rec.h, NH)
        g.wait_ge(v_att.h, NH)
        st_h.wait_all(g)
        for ic in range(IC):
            ld_h.gate(g, ic % 2)
            if ic >= 2:
                g.wait_ge(s_p1.h, p1b + ic - 1)
            ld_h.issue(g, ic % 2,
                       hrd2_sb[:, (ic % 2) * T:(ic % 2) * T + T],
                       h_sp.ap()[ic])

    @blk.tensor
    def _(tensor):
        if STOP < 5:
            return
        for oc in range(FC):
            ld_wo.wait(tensor, oc % 2, 64 * (oc // 2 + 1))
            if oc == 0:
                tensor.wait_ge(v_att.h, NH)
                tensor.wait_ge(s_rec.h, NH)
                tensor.wait_ge(s_exp.h, NH * KC)
            if oc >= 2:
                tensor.wait_ge(s_wo.h, oc - 1)
            pp = pW0 if oc % 2 == 0 else pW1
            for fc in range(FC):
                wt = wo_sb[:, (oc % 2) * FC * 128 + fc * 128:(oc % 2) * FC * 128 + fc * 128 + 128]
                for (o, w) in PC:
                    mm = tensor.matmul(pp[:, o:o + w], wt,
                                       attn_out[:, fc * T + o: fc * T + o + w],
                                       start=(fc == 0), stop=(fc == FC - 1))
            t_wo.inc(mm)

    @blk.scalar
    def _(scalar):
        if STOP < 5:
            return
        npo = IC // FC
        for oc in range(FC):
            scalar.wait_ge(t_wo.h, oc + 1)
            st_w.gate(scalar, oc % 2)
            scalar.activation(asp_sb[:, (oc % 2) * T:(oc % 2) * T + T],
                              (pW0 if oc % 2 == 0 else pW1)[:, :], AF.Copy)
            s_wo.inc(scalar.drain())
            st_w.issue(scalar, oc % 2, attn_sp.ap()[oc],
                       asp_sb[:, (oc % 2) * T:(oc % 2) * T + T])
            for ic in range(oc * npo, (oc + 1) * npo):
                if ic == 0:
                    scalar.wait_ge(v_sx.h, 2)
                if ic >= 2:
                    scalar.wait_ge(v_qh.h, ic - 1)
                ld_h.wait(scalar, ic % 2, 16 * (ic // 2 + 1))
                s_p1.inc(scalar.activation(
                    h2q_sb[:, (ic % 2) * T:(ic % 2) * T + T],
                    hrd2_sb[:, (ic % 2) * T:(ic % 2) * T + T],
                    AF.Identity, bias=np.float32(RND_C), scale=r2_col[:, 0:1]))

    @blk.vector
    def _(vec):
        if STOP < 5:
            return
        vec.wait_ge(g_gl2.h, 16)
        vec.tensor_scalar(csc[:, 7:8], sx2_col[:, :], 1.0 / 127.0, 1e-8,
                          ALU.mult, ALU.max)
        vec.drain()
        vec.reciprocal(r2_col[:, :], csc[:, 7:8])
        vec.drain()
        vec.tensor_scalar(s2_col[:, :], sw2_col[:, :], csc[:, 7:8], None, ALU.mult)
        v_sx.inc(vec.drain())
        for ic in range(IC):
            vec.wait_ge(s_p1.h, p1b + ic + 1)
            hs = h2q_sb[:, (ic % 2) * T:(ic % 2) * T + T]
            vec.tensor_scalar(hs[:, :], hs[:, :], RND_C, 127.0,
                              ALU.subtract, ALU.min)
            vec.drain()
            vec.tensor_scalar(qh_sl(ic, 0, T), hs[:, :], -127.0, None, ALU.max)
            v_qh.inc(vec.drain())

    es_ao.close()
    esP5.close()
    esP6 = contextlib.ExitStack()
    P6 = lambda n, s, dt: esP6.enter_context(nc.sbuf_tensor(n, s, dt))
    q2_sb = P6("q2_sb", [128, 2 * IC * 128], BF16)
    f2_sb = P6("f2_sb", [128, 2 * T], F32)
    asp2_sb = P6("asp2_sb", [128, T], F32)
    fin_sb = P6("fin_sb", [128, T], F32)
    xcol_sb = P6("xcol_sb", [128, 2 * T], F32)
    outt_sb = P6("outt_sb", [128, 4 * 128], F32)

    @blk.sync
    def _(sync):
        if STOP < 6:
            return
        sync.wait_ge(s_wo.h, FC)
        st_w.wait_all(sync)
        for oc in range(FC):
            ld_q2.gate(sync, oc % 2)
            if oc >= 2:
                sync.wait_ge(t_f2.h, oc - 1)
            r = (oc * 128) // OSL
            co = (oc * 128) % OSL
            i16 = max(1, IC // 16)
            for q16 in range(IC // i16):
                ld_q2.issue(sync, oc % 2,
                            q2_sb.rearrange("p (b f w) -> p b f w", b=2, f=IC)
                            [:, oc % 2, q16 * i16:(q16 + 1) * i16],
                            agb2r[r, :, q16 * i16:(q16 + 1) * i16, co:co + 128])
            if oc >= 1:
                sync.wait_ge(v_fin.h, oc)
            ld_fa.inc(sync.dma_start(
                out=asp2_sb[:, :], in_=attn_sp.ap()[oc]), 16)
            ld_fx.gate(sync, oc % 2)
            if oc >= 2:
                sync.wait_ge(v_ot.h, (oc - 1) * TC)
            hh = TC // 2
            for h2 in range(2):
                ld_fx.issue(sync, oc % 2,
                            xcol_sb.rearrange("p (b tc w) -> p b tc w", b=2, tc=TC)
                            [:, oc % 2, h2 * hh:(h2 + 1) * hh],
                            xcolr[:, h2 * hh:(h2 + 1) * hh, oc * 128:(oc + 1) * 128])
        st_out.wait_all(sync)

    def _btr(tensor, oc):
        tensor.wait_ge(v_fin.h, oc + 1)
        for tc in range(TC):
            ti = oc * TC + tc
            if ti >= 2:
                tensor.wait_ge(v_ot.h, ti - 1)
            mm = tensor.transpose(pF_sl(ti % 4),
                                  fin_sb[:, tc * 128:(tc + 1) * 128],
                                  idf_sb[:, :])
            t_F.inc(mm)

    @blk.tensor
    def _(tensor):
        if STOP < 6:
            return
        tensor.wait_ge(cc_b2.h, 1)
        boff = IC // 4
        bstep = max(1, (IC - boff) // TC)
        bt = {boff + bstep * tc: tc for tc in range(TC)}
        for oc in range(FC):
            ld_q2.wait(tensor, oc % 2, 16 * (IC // max(1, IC // 16)) * (oc // 2 + 1))
            if oc == 0:
                tensor.wait_ge(s_wo.h, FC)
            if oc >= 2:
                tensor.wait_ge(s_f2.h, oc - 1)
            pp = pW0 if oc % 2 == 0 else pW1
            for ic in range(IC):
                if oc == 0:
                    tensor.wait_ge(v_qh.h, ic + 1)
                if oc >= 1 and ic in bt:
                    tc = bt[ic]
                    ti = (oc - 1) * TC + tc
                    if tc == 0:
                        tensor.wait_ge(v_fin.h, oc)
                    if ti >= 2:
                        tensor.wait_ge(v_ot.h, ti - 1)
                    t_F.inc(tensor.transpose(pF_sl(ti % 4),
                                             fin_sb[:, tc * 128:(tc + 1) * 128],
                                             idf_sb[:, :]))
                wt = q2_sb[:, (oc % 2) * IC * 128 + ic * 128:(oc % 2) * IC * 128 + ic * 128 + 128]
                for (o, w) in PC:
                    mm = tensor.matmul(pp[:, o:o + w], wt, qh_sl(ic, o, w),
                                       start=(ic == 0), stop=(ic == IC - 1))
            t_f2.inc(mm)
        _btr(tensor, FC - 1)

    @blk.scalar
    def _(scalar):
        if STOP < 6:
            return
        for oc in range(FC):
            scalar.wait_ge(t_f2.h, oc + 1)
            if oc == 0:
                scalar.wait_ge(v_sx.h, 2)
            if oc >= 2:
                scalar.wait_ge(v_fin.h, oc - 1)
            s_f2.inc(scalar.activation(
                f2_sb[:, (oc % 2) * T:(oc % 2) * T + T],
                (pW0 if oc % 2 == 0 else pW1)[:, :], AF.Identity,
                bias=b2_sb[:, oc:oc + 1], scale=s2_col[:, oc:oc + 1]))

    @blk.vector
    def _(vec):
        if STOP < 6:
            return
        for oc in range(FC):
            vec.wait_ge(s_f2.h, oc + 1)
            vec.wait_ge(ld_fa.h, 16 * (oc + 1))
            if oc >= 1:
                vec.wait_ge(t_F.h, oc * TC)
            vec.tensor_tensor(fin_sb[:, :],
                              f2_sb[:, (oc % 2) * T:(oc % 2) * T + T],
                              asp2_sb[:, :], ALU.add)
            v_fin.inc(vec.drain())
            ld_fx.wait(vec, oc % 2, 32 * (oc // 2 + 1))
            for tc in range(TC):
                ti = oc * TC + tc
                vec.wait_ge(t_F.h, ti + 1)
                if ti >= 4:
                    st_out.wait(vec, ti % 4, 16 * ((ti - 4) // 4 + 1))
                vec.tensor_tensor(
                    outt_sb[:, (ti % 4) * 128:(ti % 4) * 128 + 128],
                    pF_sl(ti % 4),
                    xcol_sb[:, (oc % 2) * T + tc * 128:(oc % 2) * T + (tc + 1) * 128],
                    ALU.add)
                v_ot.inc(vec.drain())

    @blk.gpsimd
    def _(g):
        if STOP < 6:
            return
        for oc in range(FC):
            for tc in range(TC):
                ti = oc * TC + tc
                g.wait_ge(v_ot.h, ti + 1)
                st_out.gate(g, ti % 4)
                st_out.issue(g, ti % 4,
                             out_t[tc * 128:(tc + 1) * 128, oc * 128:(oc + 1) * 128],
                             outt_sb[:, (ti % 4) * 128:(ti % 4) * 128 + 128])

    if STOP < 6:
        @blk.sync
        def _(sync):
            if STOP >= 1:
                st_xn.wait_all(sync)
                sync.wait_ge(v_lm.h, 2)
                sync.wait_ge(v_fme.h, 2 * TC)
            if STOP >= 2:
                st_k.wait(sync, 0, 16 * NH)
                st_v.wait(sync, 0, 32 * NH)
                sync.wait_ge(v_wq2.h, N2C)
                sync.wait_ge(v_vte.h, NH)
            if STOP >= 3:
                st_h.wait_all(sync)
                sync.wait_ge(v_lm.h, 4)
                sync.wait_ge(v_fme.h, 4 * TC)
            if STOP >= 4:
                sync.wait_ge(v_att.h, NH)
            if STOP >= 5:
                st_w.wait_all(sync)
                sync.wait_ge(v_qh.h, IC)

    esP6.close()
    esQH.close()
    es_qha.close()
    esW.close()
    es.close()
    return nc



_CFG_FULL = {"d": 2048, "I": 8192, "T": 1024, "qa": 0}
_NC_CACHE = {}


def _get_nc(key, cfg):
    if key not in _NC_CACHE:
        _NC_CACHE[key] = build_nc(dict(cfg))
    return _NC_CACHE[key]


def host_inputs(core, x_full, Wq, bq, Wk, bk, Wv, bv, Wo, w1, b1, w2, b2, cfg):
    d, I, T = cfg["d"], cfg["I"], cfg["T"]
    NH, FC, IC = d // 128, d // 128, I // 128
    OSL, ISL = d // N_CORES, I // N_CORES
    b = core // GROUP
    q0 = (core % GROUP) * T
    f32 = np.float32
    bf16 = ml_dtypes.bfloat16

    m = {}
    m["x_in"] = np.ascontiguousarray(x_full[b, q0:q0 + T, :], dtype=f32)
    m["wqT_sl"] = np.ascontiguousarray(Wq.T[:, core * OSL:(core + 1) * OSL], dtype=f32)
    m["wkT_sl"] = np.ascontiguousarray(Wk.T[:, core * OSL:(core + 1) * OSL], dtype=f32)
    m["wvT_sl"] = np.ascontiguousarray(Wv.T[:, core * OSL:(core + 1) * OSL], dtype=f32)
    m["woT_sl"] = np.ascontiguousarray(Wo.T[:, core * OSL:(core + 1) * OSL], dtype=f32)
    m["w1T_sl"] = np.ascontiguousarray(w1.T[:, core * ISL:(core + 1) * ISL], dtype=f32)
    m["w2T_sl"] = np.ascontiguousarray(w2.T[:, core * OSL:(core + 1) * OSL], dtype=f32)
    m["bq_t"] = np.ascontiguousarray(bq.reshape(NH, 128).T, dtype=f32)
    m["bk_t"] = np.ascontiguousarray(bk.reshape(NH, 128).T, dtype=f32)
    m["bv_t"] = np.ascontiguousarray(bv.reshape(NH, 128).T, dtype=f32)
    m["b1_t"] = np.ascontiguousarray(b1.reshape(IC, 128).T, dtype=f32)
    m["b2_t"] = np.ascontiguousarray(b2.reshape(FC, 128).T, dtype=f32)
    hd = 128
    inv = (1.0 / (np.float32(1000000.0) **
                  (np.arange(0, hd, 2, dtype=f32) / f32(hd)))).astype(f32)
    pos = np.arange(q0, q0 + T, dtype=f32)
    fr = np.outer(pos, inv).astype(f32)
    emb = np.concatenate([fr, fr], axis=1)
    m["cos_t"] = np.ascontiguousarray(np.cos(emb).T, dtype=f32)
    m["sin_t"] = np.ascontiguousarray(np.sin(emb).T, dtype=f32)
    m["id_bf"] = np.eye(128, dtype=bf16)
    m["id_f32"] = np.eye(128, dtype=f32)
    R = np.zeros((128, 128), dtype=f32)
    for i in range(64):
        R[i, i + 64] = -1.0
        R[i + 64, i] = 1.0
    m["rT_bf"] = np.ascontiguousarray(R.T).astype(bf16)
    m["ones_bf"] = np.ones((128, 128), dtype=bf16)
    m["ones_f32"] = np.ones((128, 128), dtype=f32)
    m["onesf"] = np.ones((1, 128), dtype=f32)
    return m


def kernel(L0_attn_hidden_states, L0_attn_Wq, L0_attn_bq, L0_attn_Wk, L0_attn_bk,
           L0_attn_Wv, L0_attn_bv, L0_attn_Wo, L0_ffn_w1, L0_ffn_b1,
           L0_ffn_w2, L0_ffn_b2):
    cfg = _CFG_FULL
    x = np.asarray(L0_attn_hidden_states, dtype=np.float32)
    args = [np.asarray(a, dtype=np.float32) for a in
            (L0_attn_Wq, L0_attn_bq, L0_attn_Wk, L0_attn_bk, L0_attn_Wv,
             L0_attn_bv, L0_attn_Wo, L0_ffn_w1, L0_ffn_b1, L0_ffn_w2, L0_ffn_b2)]
    nc = _get_nc("full", cfg)
    in_maps = [host_inputs(c, x, *args, cfg) for c in range(N_CORES)]
    res = run_bass_kernel_spmd(nc, in_maps, core_ids=list(range(N_CORES)))
    T, d = cfg["T"], cfg["d"]
    B, Sq = x.shape[0], x.shape[1]
    out = np.empty((B, Sq, d), dtype=np.float32)
    for c in range(N_CORES):
        b = c // GROUP
        q0 = (c % GROUP) * T
        out[b, q0:q0 + T, :] = np.asarray(res.results[c]["out"])
    return out


# version 14
# speedup vs baseline: 1.0428x; 1.0428x over previous
import contextlib

import numpy as np
import ml_dtypes

import concourse.bass as bass
import concourse.mybir as mybir
from concourse.bass_utils import run_bass_kernel_spmd

F32 = mybir.dt.float32
BF16 = mybir.dt.bfloat16
AF = mybir.ActivationFunctionType
ALU = mybir.AluOpType
AX = mybir.AxisListType

RND_C = 3.0 * 2.0**22
N_CORES = 8
GROUP = 4


class Sem:
    def __init__(self, nc, es, name):
        self.h = es.enter_context(nc.semaphore(name))
        self.c = 0

    def inc(self, instr, n=1):
        instr.then_inc(self.h, n)
        self.c += n
        return self.c


class DSem:

    def __init__(self, nc, es, name, nslots):
        self.s = [Sem(nc, es, f"{name}{i}") for i in range(nslots)]

    def gate(self, eng, slot):
        sm = self.s[slot]
        if sm.c > 0:
            eng.wait_ge(sm.h, sm.c)

    def issue(self, eng, slot, out, in_):
        sm = self.s[slot]
        sm.inc(eng.dma_start(out=out, in_=in_), 16)
        return sm.c

    def cnt(self, slot):
        return self.s[slot].c

    def wait(self, eng, slot, cnt):
        eng.wait_ge(self.s[slot].h, cnt)

    def wait_all(self, eng):
        for sm in self.s:
            if sm.c:
                eng.wait_ge(sm.h, sm.c)


def _pieces(T):
    n = (T + 511) // 512
    w = T // n
    assert w * n == T
    return [(i * w, w) for i in range(n)]


def build_nc(cfg):
    d = cfg["d"]
    I = cfg["I"]
    T = cfg["T"]
    NH = d // 128
    FC = d // 128
    IC = I // 128
    TC = T // 128
    S = GROUP * T
    KC = S // 128
    OSL = d // N_CORES
    ISL = I // N_CORES
    PC = _pieces(T)
    NP = len(PC)
    assert OSL % 128 == 0 and ISL % 128 == 0
    NW1 = ISL // 128
    NW2 = OSL // 128
    W2C = max(1, IC // 16)
    N2C = IC // W2C
    n_bn = d // 512
    assert n_bn * 512 == d and T % 128 == 0
    QA = min(IC, cfg.get("qa", 20))
    QB = IC - QA
    EXPS = np.float32(np.float32(1.0) / np.sqrt(np.float32(128.0)))
    EXPB = np.float32(-20.0)
    HSL = (FC // 2) * ISL
    FH = FC // 2

    STOP = cfg.get("stop", 99)
    nc = bass.Bass("TRN2", target_bir_lowering=False)

    x_in = nc.dram_tensor("x_in", [T, d], F32, kind="ExternalInput").ap()
    wqT = nc.dram_tensor("wqT_sl", [d, OSL], F32, kind="ExternalInput").ap()
    wkT = nc.dram_tensor("wkT_sl", [d, OSL], F32, kind="ExternalInput").ap()
    wvT = nc.dram_tensor("wvT_sl", [d, OSL], F32, kind="ExternalInput").ap()
    woT = nc.dram_tensor("woT_sl", [d, OSL], F32, kind="ExternalInput").ap()
    w1T = nc.dram_tensor("w1T_sl", [d, ISL], F32, kind="ExternalInput").ap()
    w2T = nc.dram_tensor("w2T_sl", [I, OSL], F32, kind="ExternalInput").ap()
    bq_t = nc.dram_tensor("bq_t", [128, NH], F32, kind="ExternalInput").ap()
    bk_t = nc.dram_tensor("bk_t", [128, NH], F32, kind="ExternalInput").ap()
    bv_t = nc.dram_tensor("bv_t", [128, NH], F32, kind="ExternalInput").ap()
    b1_t = nc.dram_tensor("b1_t", [128, IC], F32, kind="ExternalInput").ap()
    b2_t = nc.dram_tensor("b2_t", [128, FC], F32, kind="ExternalInput").ap()
    cos_t = nc.dram_tensor("cos_t", [128, T], F32, kind="ExternalInput").ap()
    sin_t = nc.dram_tensor("sin_t", [128, T], F32, kind="ExternalInput").ap()
    id_bf = nc.dram_tensor("id_bf", [128, 128], BF16, kind="ExternalInput").ap()
    id_f32 = nc.dram_tensor("id_f32", [128, 128], F32, kind="ExternalInput").ap()
    rT_bf = nc.dram_tensor("rT_bf", [128, 128], BF16, kind="ExternalInput").ap()
    ones_bf = nc.dram_tensor("ones_bf", [128, 128], BF16, kind="ExternalInput").ap()
    ones_f32 = nc.dram_tensor("ones_f32", [128, 128], F32, kind="ExternalInput").ap()
    onesf_t = nc.dram_tensor("onesf", [1, 128], F32, kind="ExternalInput").ap()
    out_t = nc.dram_tensor("out", [T, d], F32, kind="ExternalOutput").ap()

    aga_in = nc.dram_tensor("aga_in", [4 * d, OSL], BF16)
    aga_out = nc.dram_tensor("aga_out", [N_CORES * 4 * d, OSL], BF16, addr_space="Shared")
    agb1_in = nc.dram_tensor("agb1_in", [d, ISL], BF16)
    agb1_out = nc.dram_tensor("agb1_out", [N_CORES * d, ISL], BF16, addr_space="Shared")
    agb2_in = nc.dram_tensor("agb2_in", [I, OSL], BF16)
    agb2_out = nc.dram_tensor("agb2_out", [N_CORES * I, OSL], BF16, addr_space="Shared")
    ssz = ISL + OSL
    ags_in = nc.dram_tensor("ags_in", [ssz], F32)
    ags_out = nc.dram_tensor("ags_out", [N_CORES * ssz], F32, addr_space="Shared")
    m1_in = nc.dram_tensor("m1_in", [128], F32)
    m1_out = nc.dram_tensor("m1_out", [128], F32, addr_space="Shared")
    m2_in = nc.dram_tensor("m2_in", [128], F32)
    m2_out = nc.dram_tensor("m2_out", [128], F32, addr_space="Shared")
    agc_sz = NH * 128 * T * 2
    agc_in = nc.dram_tensor("agc_in", [agc_sz], BF16)
    agc_out = nc.dram_tensor("agc_out", [GROUP * agc_sz], BF16)
    xn_sp = nc.dram_tensor("xn_sp", [TC, 128, d], F32)
    h_sp = nc.dram_tensor("h_sp", [IC, 128, T], F32)
    attn_sp = nc.dram_tensor("attn_sp", [FC, 128, T], F32)
    rs_scr = nc.dram_tensor("rs_scr", [max(ISL, OSL)], F32)
    KOFF = 0
    VOFF = NH * 128 * T

    es = contextlib.ExitStack()
    SM = lambda name: Sem(nc, es, name)
    DS = lambda name, n: DSem(nc, es, name, n)
    ld_aux = SM("ld_aux")
    ld_x = DS("ld_x", 3); ld_w1 = DS("ld_w1", 2); ld_w2 = DS("ld_w2", 2)
    ld_wq = DS("ld_wq", 2); ld_wo = DS("ld_wo", 2); ld_q1 = DS("ld_q1", 3)
    ld_kv = DS("ld_kv", 2); ld_h = DS("ld_h", 2); ld_q2 = DS("ld_q2", 2)
    ld_fa = SM("ld_fa"); ld_fx = DS("ld_fx", 2); ld_sw = SM("ld_sw")
    ld_xq = DS("ld_xq", 2)
    st_out = DS("st_out", 4)
    st_xn = DS("st_xn", 3); st_k = DS("st_k", 1); st_v = DS("st_v", 1)
    st_h = DS("st_h", 2); st_w = DS("st_w", 2)
    g_dma = SM("g_dma"); g_csc = SM("g_csc")
    g_w1 = DS("g_w1", 2); g_w2 = DS("g_w2", 2)
    cc_a = SM("cc_a"); cc_b1 = SM("cc_b1"); cc_b2 = SM("cc_b2")
    cc_s = SM("cc_s"); cc_m1 = SM("cc_m1"); cc_m2 = SM("cc_m2"); cc_kv = SM("cc_kv")
    t_xtr = SM("t_xtr"); t_mx = SM("t_mx"); t_sw = SM("t_sw")
    t_p = SM("t_p"); t_r = SM("t_r"); t_vt = SM("t_vt"); t_f1 = SM("t_f1")
    t_S = SM("t_S"); t_A = SM("t_A"); t_O = SM("t_O"); t_wo = SM("t_wo")
    t_f2 = SM("t_f2"); t_F = SM("t_F")
    v_ve = SM("v_ve"); v_xnb = SM("v_xnb"); v_fme = SM("v_fme")
    v_lm = SM("v_lm"); v_mx1 = SM("v_mx1"); v_w2m = SM("v_w2m")
    v_mxr = SM("v_mxr"); v_wq1 = SM("v_wq1"); v_wq2 = SM("v_wq2")
    v_swc = SM("v_swc"); v_sx = SM("v_sx"); v_rope = SM("v_rope")
    v_vte = SM("v_vte"); v_qx = SM("v_qx"); v_hx = SM("v_hx"); v_att = SM("v_att")
    v_qh = SM("v_qh"); v_fin = SM("v_fin"); v_ot = SM("v_ot"); v_pp1 = SM("v_pp1")
    s_ln = SM("s_ln"); s_qb = SM("s_qb"); s_g = SM("s_g"); s_exp = SM("s_exp")
    s_rec = SM("s_rec"); s_wo = SM("s_wo"); s_f2 = SM("s_f2")
    v_acc = v_vte; s_p1 = s_qb
    t_pb = SM("t_pb"); t_bc = SM("t_bc"); t_lm = SM("t_lm")
    g_rr1 = SM("g_rr1"); g_rr2 = SM("g_rr2"); g_gl1 = SM("g_gl1"); g_gl2 = SM("g_gl2")


    sb = lambda name, shape, dt: es.enter_context(nc.sbuf_tensor(name, shape, dt))
    id_sb = sb("id_sb", [128, 128], BF16)
    idf_sb = sb("idf_sb", [128, 128], F32)
    rT_sb = sb("rT_sb", [128, 128], BF16)
    ones_sb = sb("ones_sb", [128, 128], BF16)
    onesf32_sb = sb("onesf32_sb", [128, 128], F32)
    bq_sb = sb("bq_sb", [128, NH], F32)
    bk_sb = sb("bk_sb", [128, NH], F32)
    bv_sb = sb("bv_sb", [128, NH], F32)
    b1_sb = sb("b1_sb", [128, IC], F32)
    b2_sb = sb("b2_sb", [128, FC], F32)
    sw1_col = sb("sw1_col", [128, IC], F32)
    sw2_col = sb("sw2_col", [128, FC], F32)
    s1_col = sb("s1_col", [128, IC], F32)
    s2_col = sb("s2_col", [128, FC], F32)
    onesf_sb = sb("onesf_sb", [1, 128], F32)
    g1row_sb = sb("g1row_sb", [1, 128], F32)
    g2row_sb = sb("g2row_sb", [1, 128], F32)
    gm1_sb = sb("gm1_sb", [1, 1], F32)
    gm2_sb = sb("gm2_sb", [1, 1], F32)
    sx1_col = sb("sx1_col", [128, 1], F32)
    sx2_col = sb("sx2_col", [128, 1], F32)
    r1_col = sb("r1_col", [128, 1], F32)
    r2_col = sb("r2_col", [128, 1], F32)
    csc = sb("csc", [128, 8], F32)
    mv_all = sb("mv_all", [128, 2 * TC], F32)
    ve_all = sb("ve_all", [128, TC], F32)
    rstd_all = sb("rstd_all", [128, TC], F32)
    ln_tmp = sb("ln_tmp", [128, TC], F32)
    lmax_sb = sb("lmax_sb", [128, TC], F32)
    hmax_sb = sb("hmax_sb", [128, IC], F32)

    blk = es.enter_context(nc.Block())
    ppi = {"w": 0}

    esQR = contextlib.ExitStack()
    q_rope = esQR.enter_context(nc.sbuf_tensor("q_rope", [128, NH * T], BF16))
    esA2 = contextlib.ExitStack()
    A2 = lambda n, s, dt: esA2.enter_context(nc.sbuf_tensor(n, s, dt))
    xn_fm = A2("xn_fm", [128, FC * T], BF16)
    cos_sb = A2("cos_sb", [128, T], F32)
    sin_sb = A2("sin_sb", [128, T], F32)
    esA1a = contextlib.ExitStack()
    A1a = lambda n, s, dt: esA1a.enter_context(nc.sbuf_tensor(n, s, dt))
    x_tiles = A1a("x_tiles", [128, 3 * d], F32)
    bn_scr = A1a("bn_scr", [128, 6 * n_bn], F32)
    swsl = A1a("swsl", [max(ISL, OSL) // 128, 128], F32)
    esP2 = contextlib.ExitStack()
    P2 = lambda n, s, dt: esP2.enter_context(nc.sbuf_tensor(n, s, dt))
    wsl3 = P2("wsl3", [128, 2 * 3 * FC * 128], BF16)
    qb_sb = P2("qb_sb", [128, T], BF16)
    vb_sb = P2("vb_sb", [128, T], F32)
    vtm_sb = P2("vtm_sb", [128, T], BF16)
    kr_sb = P2("kr_sb", [128, T], BF16)
    m1r_sb = P2("m1r_sb", [128, T], F32)
    m2r_sb = P2("m2r_sb", [128, T], F32)
    esA1b = contextlib.ExitStack()
    A1b = lambda n, s, dt: esA1b.enter_context(nc.sbuf_tensor(n, s, dt))
    w1p = A1b("w1p", [128, 2 * FC * 128], F32)
    w1q_bf = A1b("w1q_bf", [128, 2 * FC * 128], BF16)
    w2p = A1b("w2p", [128, 2 * W2C * OSL], F32)
    w2q_bf = A1b("w2q_bf", [128, 2 * W2C * OSL], BF16)
    mxw = A1b("mxw", [128, max(128, OSL)], F32)
    mnw = A1b("mnw", [128, max(128, OSL)], F32)
    mxq = A1b("mxq", [128, max(128, OSL)], F32)
    rep1 = A1b("rep1", [128, 128], F32)
    rep2 = A1b("rep2", [128, OSL], F32)
    rr1 = A1b("rr1", [1, 128], F32)
    rr2 = A1b("rr2", [1, OSL], F32)
    esXY = contextlib.ExitStack()
    pX = esXY.enter_context(nc.psum_tensor("pX", [128, T], F32))
    pY = esXY.enter_context(nc.psum_tensor("pY", [128, T], F32))
    esT = contextlib.ExitStack()
    pT = esT.enter_context(nc.psum_tensor("pT", [128, 1024], F32))
    pP1 = esT.enter_context(nc.psum_tensor("pP1", [128, 128], F32))

    w1r = w1T.rearrange("(f p) i -> p f i", p=128)
    w2r = w2T.rearrange("(f p) o -> p f o", p=128)
    agar = aga_out.ap().rearrange("(r j f p) o -> r j p f o", j=4, f=FC, p=128)
    agb1r = agb1_out.ap().rearrange("(r f p) i -> r p f i", f=FC, p=128)
    agb2r = agb2_out.ap().rearrange("(r f p) o -> r p f o", f=IC, p=128)
    agb1w = agb1_in.ap().rearrange("(f p) i -> p f i", p=128)
    agb2w = agb2_in.ap().rearrange("(c f p) o -> c p f o", f=W2C, p=128)
    xcolr = x_in.rearrange("(tc p) o -> p tc o", p=128)

    units = ([("w1", j) for j in range(NW1)]
             + [("w2m", j) for j in range(N2C)]
             + [("w2s", j) for j in range(NW2)]
             + [("pb2", j) for j in range(NW2)]
             + [("w2q", j) for j in range(N2C)])
    NHP = max(1, NH // 2)
    per_head = [units[(len(units) * h) // NHP:(len(units) * (h + 1)) // NHP]
                if h < NHP else [] for h in range(NH)]


    @blk.sync
    def _(sync):
        if STOP < 1:
            return
        for src, dst in [(cos_t, cos_sb), (sin_t, sin_sb), (id_bf, id_sb),
                         (id_f32, idf_sb), (rT_bf, rT_sb), (ones_bf, ones_sb),
                         (ones_f32, onesf32_sb),
                         (bq_t, bq_sb), (bk_t, bk_sb), (bv_t, bv_sb),
                         (b1_t, b1_sb), (b2_t, b2_sb), (onesf_t, onesf_sb)]:
            ld_aux.inc(sync.dma_start(out=dst[:, :], in_=src), 16)
        for tc in range(TC):
            sl = tc % 3
            ld_x.gate(sync, sl)
            if tc >= 3:
                st_xn.wait(sync, (tc - 3) % 3, 16 * ((tc - 3) // 3 + 1))
                sync.wait_ge(t_xtr.h, 2 * (tc - 2))
            ld_x.issue(sync, sl,
                       x_tiles[:, sl * d:sl * d + d],
                       x_in[tc * 128:(tc + 1) * 128, :])

    @blk.gpsimd
    def _(g):
        if STOP < 2:
            return
        for j, w in enumerate([wqT, wkT, wvT, woT]):
            g_dma.inc(g.dma_start(out=aga_in[j * d:j * d + d // 2, :],
                                  in_=w[0:d // 2, :]), 16)
            g_dma.inc(g.dma_start(out=aga_in[j * d + d // 2:(j + 1) * d, :],
                                  in_=w[d // 2:d, :]), 16)
        g.wait_ge(g_dma.h, g_dma.c)
        cc_a.inc(g.collective_compute(
            "AllGather", ALU.bypass, replica_groups=[list(range(N_CORES))],
            ins=[aga_in.ap().opt()], outs=[aga_out.ap().opt()]))
        for j in range(NW1):
            g.wait_ge(v_mxr.h, j + 1)
            g_csc.inc(g.dma_start(out=rs_scr[j * 128:(j + 1) * 128],
                                  in_=csc[:, 2 + (j % 2):3 + (j % 2)]), 16)
            g_csc.inc(g.dma_start(out=ags_in[j * 128:(j + 1) * 128],
                                  in_=csc[:, 4 + (j % 2):5 + (j % 2)]), 16)
            g.wait_ge(g_csc.h, g_csc.c)
            if j >= 1:
                g.wait_ge(t_pb.h, j)
            g_rr1.inc(g.dma_start(out=rr1[:, :],
                                  in_=rs_scr[j * 128:(j + 1) * 128]), 16)
            g.wait_ge(v_wq1.h, j + 1)
            for q4 in range(4):
                fs = FC // 4
                g_w1.issue(g, j % 2,
                           agb1w[:, q4 * fs:(q4 + 1) * fs, j * 128:(j + 1) * 128],
                           w1q_bf.rearrange("p (b f w) -> p b f w", b=2, f=FC)
                           [:, j % 2, q4 * fs:(q4 + 1) * fs])
        g.wait_ge(g_dma.h, g_dma.c)
        g_w1.wait_all(g)
        cc_b1.inc(g.collective_compute(
            "AllGather", ALU.bypass, replica_groups=[list(range(N_CORES))],
            ins=[agb1_in.ap().opt()], outs=[agb1_out.ap().opt()]))
        for j in range(NW2):
            g.wait_ge(v_mxr.h, NW1 + j + 1)
            g_csc.inc(g.dma_start(out=rs_scr[j * 128:(j + 1) * 128],
                                  in_=csc[:, 2 + (j % 2):3 + (j % 2)]), 16)
            g_csc.inc(g.dma_start(out=ags_in[ISL + j * 128:ISL + (j + 1) * 128],
                                  in_=csc[:, 4 + (j % 2):5 + (j % 2)]), 16)
        g.wait_ge(g_csc.h, g_csc.c)
        g_rr2.inc(g.dma_start(out=rr2[:, :], in_=rs_scr[0:OSL]), 16)
        for j in range(N2C):
            g.wait_ge(v_wq2.h, j + 1)
            g_w2.issue(g, j % 2,
                       agb2w[j],
                       w2q_bf.rearrange("p (b f w) -> p b f w", b=2, f=W2C)[:, j % 2])
        g_w2.wait_all(g)
        cc_b2.inc(g.collective_compute(
            "AllGather", ALU.bypass, replica_groups=[list(range(N_CORES))],
            ins=[agb2_in.ap().opt()], outs=[agb2_out.ap().opt()]))
        cc_s.inc(g.collective_compute(
            "AllGather", ALU.bypass, replica_groups=[list(range(N_CORES))],
            ins=[ags_in.ap().opt()], outs=[ags_out.ap().opt()]))
        if STOP < 3:
            return
        g.wait_ge(v_lm.h, 2)
        g_dma.inc(g.dma_start(out=m1_in[0:128], in_=g1row_sb[:, :]), 16)
        g.wait_ge(g_dma.h, g_dma.c)
        cc_m1.inc(g.collective_compute(
            "AllReduce", ALU.max, replica_groups=[list(range(N_CORES))],
            ins=[m1_in.ap().opt()], outs=[m1_out.ap().opt()]))
        g.wait_ge(cc_m1.h, 1)
        g_gl1.inc(g.dma_start(out=sx1_col[:, :],
                              in_=m1_out.ap().rearrange("(p o) -> p o", o=1)), 16)
        if STOP < 4:
            pass
        st_k.wait(g, 0, 16 * NH)
        st_v.wait(g, 0, 32 * NH)
        cc_kv.inc(g.collective_compute(
            "AllGather", ALU.bypass,
            replica_groups=[list(range(G * GROUP, (G + 1) * GROUP))
                            for G in range(N_CORES // GROUP)],
            ins=[agc_in.ap().opt()], outs=[agc_out.ap().opt()]))
        g.wait_ge(v_lm.h, 4)
        g_dma.inc(g.dma_start(out=m2_in[0:128], in_=g2row_sb[:, :]), 16)
        g.wait_ge(g_dma.h, g_dma.c)
        cc_m2.inc(g.collective_compute(
            "AllReduce", ALU.max, replica_groups=[list(range(N_CORES))],
            ins=[m2_in.ap().opt()], outs=[m2_out.ap().opt()]))
        g.wait_ge(cc_m2.h, 1)
        g_gl2.inc(g.dma_start(out=sx2_col[:, :],
                              in_=m2_out.ap().rearrange("(p o) -> p o", o=1)), 16)

    @blk.scalar
    def _(scalar):
        if STOP < 1:
            return
        for tc in range(TC):
            scalar.wait_ge(v_ve.h, tc + 1)
            scalar.activation(ln_tmp[:, tc:tc + 1], ve_all[:, tc:tc + 1], AF.Ln)
            scalar.drain()
            s_ln.inc(scalar.activation(rstd_all[:, tc:tc + 1],
                                       ln_tmp[:, tc:tc + 1], AF.Exp, scale=np.float32(-0.5)))
            if tc >= 1:
                scalar.wait_ge(v_xnb.h, tc)
                st_xn.gate(scalar, (tc - 1) % 3)
                st_xn.issue(scalar, (tc - 1) % 3, xn_sp.ap()[tc - 1],
                            x_tiles[:, ((tc - 1) % 3) * d:((tc - 1) % 3) * d + d])
        scalar.wait_ge(v_xnb.h, TC)
        st_xn.gate(scalar, (TC - 1) % 3)
        st_xn.issue(scalar, (TC - 1) % 3, xn_sp.ap()[TC - 1],
                    x_tiles[:, ((TC - 1) % 3) * d:((TC - 1) % 3) * d + d])

    @blk.vector
    def _(vec):
        if STOP < 1:
            return
        def norm(tc):
            vec.wait_ge(s_ln.h, tc + 1)
            xt = x_tiles[:, (tc % 3) * d:(tc % 3) * d + d]
            vec.tensor_scalar(xt[:, :], xt[:, :], mv_all[:, 2 * tc:2 * tc + 1],
                              rstd_all[:, tc:tc + 1], ALU.subtract, ALU.mult)
            vec.drain()
            vec.tensor_reduce(lmax_sb[:, tc:tc + 1], xt[:, :], AX.X, ALU.max,
                              apply_absolute_value=True)
            v_xnb.inc(vec.drain())
        def xn_evac(jt):
            for half in range(2):
                vec.wait_ge(t_xtr.h, 2 * jt + half + 1)
                for jj in range(FH):
                    fc = half * FH + jj
                    vec.tensor_copy(xn_fm[:, fc * T + jt * 128: fc * T + (jt + 1) * 128],
                                    pT[:, jj * 128:(jj + 1) * 128])
                v_fme.inc(vec.drain())
        for tc in range(TC):
            ld_x.wait(vec, tc % 3, 16 * (tc // 3 + 1))
            xt = x_tiles[:, (tc % 3) * d:(tc % 3) * d + d]
            for j in range(n_bn):
                vec.bn_stats(bn_scr[:, 6 * j:6 * j + 6], xt[:, 512 * j:512 * (j + 1)])
            vec.drain()
            vec.bn_aggr(mv_all[:, 2 * tc:2 * tc + 2], bn_scr[:, :])
            vec.drain()
            vec.tensor_scalar(ve_all[:, tc:tc + 1], mv_all[:, 2 * tc + 1:2 * tc + 2],
                              1e-5, None, ALU.add)
            v_ve.inc(vec.drain())
            if tc >= 1:
                norm(tc - 1)
            if tc >= 2:
                xn_evac(tc - 2)
        norm(TC - 1)
        for jt in range(max(0, TC - 2), TC):
            xn_evac(jt)
        vec.tensor_reduce(csc[:, 0:1], lmax_sb[:, :], AX.X, ALU.max)
        v_lm.inc(vec.drain())

    @blk.tensor
    def _(tensor):
        if STOP < 1:
            return
        tensor.wait_ge(ld_aux.h, ld_aux.c)
        for tc in range(TC):
            tensor.wait_ge(v_xnb.h, tc + 1)
            for half in range(2):
                if 2 * tc + half >= 1:
                    tensor.wait_ge(v_fme.h, 2 * tc + half)
                for j in range(FH):
                    fc = half * FH + j
                    mm = tensor.transpose(
                        pT[:, j * 128:(j + 1) * 128],
                        x_tiles[:, (tc % 3) * d + fc * 128:(tc % 3) * d + fc * 128 + 128],
                        idf_sb[:, :])
                t_xtr.inc(mm)
        tensor.wait_ge(v_lm.h, 1)
        tensor.wait_ge(v_pp1.h, ppi["w"])
        t_lm.inc(tensor.matmul(pP1[0:1, :], csc[:, 0:1], idf_sb[:, :],
                               start=True, stop=True))
        ppi["w"] += 1

    @blk.vector
    def _(vec):
        if STOP < 1:
            return
        vec.wait_ge(t_lm.h, 1)
        vec.tensor_reduce(gm1_sb[:, :], pP1[0:1, :], AX.X, ALU.max)
        vec.drain()
        vec.tensor_scalar(g1row_sb[:, :], pP1[0:1, :], gm1_sb[0:1, 0:1],
                          None, ALU.max)
        vec.drain()
        v_pp1.inc(vec.drain())
        v_lm.inc(vec.drain())

    def _unit_load(sync, kind, j):
        if kind == "w1":
            ld_w1.gate(sync, j % 2)
            if j >= 2:
                sync.wait_ge(v_wq1.h, j - 1)
            for q4 in range(4):
                fs = FC // 4
                ld_w1.issue(sync, j % 2,
                            w1p.rearrange("p (b f w) -> p b f w", b=2, f=FC)
                            [:, j % 2, q4 * fs:(q4 + 1) * fs],
                            w1r[:, q4 * fs:(q4 + 1) * fs, j * 128:(j + 1) * 128])
        elif kind == "w2m":
            ld_w2.gate(sync, j % 2)
            if j >= 2:
                sync.wait_ge(v_w2m.h, j - 1)
            ld_w2.issue(sync, j % 2,
                        w2p.rearrange("p (b f w) -> p b f w", b=2, f=W2C)[:, j % 2],
                        w2r[:, j * W2C:(j + 1) * W2C, :])
        elif kind == "w2q":
            ld_w2.gate(sync, j % 2)
            if j == 0:
                sync.wait_ge(v_mx1.h, NW1 + 1)
            if j >= 2:
                sync.wait_ge(v_wq2.h, j - 1)
            ld_w2.issue(sync, j % 2,
                        w2p.rearrange("p (b f w) -> p b f w", b=2, f=W2C)[:, j % 2],
                        w2r[:, j * W2C:(j + 1) * W2C, :])

    @blk.sync
    def _(sync):
        if STOP < 2:
            return
        sync.wait_ge(cc_a.h, 1)
        for kind, j in per_head[0]:
            _unit_load(sync, kind, j)
        for h in range(NH):
            if h >= 2:
                sync.wait_ge(t_p.h, 3 * (h - 1))
            base = (h % 2) * 3 * FC * 128
            r = (h * 128) // OSL
            co = (h * 128) % OSL
            ld_wq.gate(sync, h % 2)
            fs = FC // 4
            for pj in range(3):
                dst = wsl3[:, base + pj * FC * 128: base + (pj + 1) * FC * 128]
                for q4 in range(4):
                    ld_wq.issue(sync, h % 2,
                                dst.rearrange("p (f w) -> p f w", f=FC)
                                [:, q4 * fs:(q4 + 1) * fs],
                                agar[r, pj, :, q4 * fs:(q4 + 1) * fs, co:co + 128])
            if h + 1 < NH:
                for kind, j in per_head[h + 1]:
                    _unit_load(sync, kind, j)

    @blk.tensor
    def _(tensor):
        if STOP < 2:
            return
        nmx = 0
        for h in range(NH):
            for kind, j in per_head[h]:
                if kind == "w1":
                    tensor.wait_ge(v_mx1.h, nmx + 1)
                    tensor.wait_ge(v_pp1.h, ppi["w"])
                    t_mx.inc(tensor.transpose(pP1[:, :], mxw[:, 0:128], idf_sb[:, :]))
                    ppi["w"] += 1
                    nmx += 1
                    tensor.wait_ge(g_rr1.h, 16 * (j + 1))
                    tensor.wait_ge(v_pp1.h, ppi["w"])
                    t_pb.inc(tensor.matmul(pP1[:, :], onesf_sb[:, :], rr1[:, :],
                                            start=True, stop=True))
                    ppi["w"] += 1
                elif kind == "w2s":
                    tensor.wait_ge(v_mx1.h, nmx + 1)
                    tensor.wait_ge(v_pp1.h, ppi["w"])
                    t_mx.inc(tensor.transpose(pP1[:, :], mxw[:, j * 128:(j + 1) * 128],
                                               idf_sb[:, :]))
                    ppi["w"] += 1
                    nmx += 1
                elif kind == "pb2":
                    tensor.wait_ge(g_rr2.h, 16)
                    tensor.wait_ge(v_pp1.h, ppi["w"])
                    t_pb.inc(tensor.matmul(pP1[:, :], onesf_sb[:, :],
                                            rr2[:, j * 128:(j + 1) * 128],
                                            start=True, stop=True))
                    ppi["w"] += 1
            base = (h % 2) * 3 * FC * 128
            ld_wq.wait(tensor, h % 2, 192 * (h // 2 + 1))
            if h == 0:
                tensor.wait_ge(v_fme.h, 2 * TC)
            tensor.wait_ge(s_qb.h, 3 * h)
            for fc in range(FC):
                wt = wsl3[:, base + fc * 128: base + fc * 128 + 128]
                for (o, w) in PC:
                    mm = tensor.matmul(pX[:, o:o + w], wt,
                                       xn_fm[:, fc * T + o: fc * T + o + w],
                                       start=(fc == 0), stop=(fc == FC - 1))
            t_p.inc(mm)
            tensor.wait_ge(s_qb.h, 3 * h + 1)
            if h > 0:
                tensor.wait_ge(v_rope.h, 2 * h)
            for (o, w) in PC:
                mm = tensor.matmul(pY[:, o:o + w], rT_sb[:, :], qb_sb[:, o:o + w],
                                   start=True, stop=True)
            t_r.inc(mm)
            for fc in range(FC):
                wt = wsl3[:, base + FC * 128 + fc * 128: base + FC * 128 + fc * 128 + 128]
                for (o, w) in PC:
                    mm = tensor.matmul(pX[:, o:o + w], wt,
                                       xn_fm[:, fc * T + o: fc * T + o + w],
                                       start=(fc == 0), stop=(fc == FC - 1))
            t_p.inc(mm)
            tensor.wait_ge(s_qb.h, 3 * h + 2)
            tensor.wait_ge(v_rope.h, 2 * h + 1)
            for (o, w) in PC:
                mm = tensor.matmul(pY[:, o:o + w], rT_sb[:, :], qb_sb[:, o:o + w],
                                   start=True, stop=True)
            t_r.inc(mm)
            for fc in range(FC):
                wt = wsl3[:, base + 2 * FC * 128 + fc * 128: base + 2 * FC * 128 + fc * 128 + 128]
                for (o, w) in PC:
                    mm = tensor.matmul(pX[:, o:o + w], wt,
                                       xn_fm[:, fc * T + o: fc * T + o + w],
                                       start=(fc == 0), stop=(fc == FC - 1))
            t_p.inc(mm)
            tensor.wait_ge(s_qb.h, 3 * h + 3)
            tensor.wait_ge(v_vte.h, h)
            if h == 0:
                tensor.wait_ge(v_fme.h, 2 * TC)
            for tc in range(TC):
                mm = tensor.transpose(pT[:, tc * 128:(tc + 1) * 128],
                                      vb_sb[:, tc * 128:(tc + 1) * 128], idf_sb[:, :])
            t_vt.inc(mm)

    @blk.scalar
    def _(scalar):
        if STOP < 2:
            return
        scalar.wait_ge(ld_aux.h, ld_aux.c)
        for h in range(NH):
            scalar.wait_ge(t_p.h, 3 * h + 1)
            if h > 0:
                scalar.wait_ge(v_rope.h, 2 * h)
            s_qb.inc(scalar.activation(qb_sb[:, :], pX[:, :], AF.Identity,
                                       bias=bq_sb[:, h:h + 1]))
            scalar.wait_ge(t_p.h, 3 * h + 2)
            scalar.wait_ge(v_rope.h, 2 * h + 1)
            s_qb.inc(scalar.activation(qb_sb[:, :], pX[:, :], AF.Identity,
                                       bias=bk_sb[:, h:h + 1]))
            scalar.wait_ge(t_p.h, 3 * h + 3)
            if h > 0:
                scalar.wait_ge(t_vt.h, h)
            s_qb.inc(scalar.activation(vb_sb[:, :], pX[:, :], AF.Identity,
                                       bias=bv_sb[:, h:h + 1]))
            scalar.wait_ge(v_rope.h, 2 * h + 2)
            st_k.gate(scalar, 0)
            st_k.issue(scalar, 0,
                       agc_in.ap()[KOFF + h * 128 * T: KOFF + (h + 1) * 128 * T]
                       .rearrange("(p t) -> p t", p=128),
                       kr_sb[:, :])
            scalar.wait_ge(v_vte.h, h + 1)
            st_v.gate(scalar, 0)
            hh = TC // 2
            for h2 in range(2):
                st_v.issue(scalar, 0,
                           agc_in.ap()[VOFF + h * T * 128: VOFF + (h + 1) * T * 128]
                           .rearrange("(tc p dc) -> p tc dc", p=128, dc=128)
                           [:, h2 * hh:(h2 + 1) * hh],
                           vtm_sb.ap().rearrange("p (tc dc) -> p tc dc", dc=128)
                           [:, h2 * hh:(h2 + 1) * hh])

    @blk.vector
    def _(vec):
        if STOP < 2:
            return
        vec.wait_ge(ld_aux.h, ld_aux.c)
        nmx_v = 0
        w2fold = {"n": 0}
        for h in range(NH):
            for kind, j in per_head[h]:
                if kind == "w1":
                    ld_w1.wait(vec, j % 2, 64 * (j // 2 + 1))
                    src = w1p.rearrange("p (b f w) -> p b f w", b=2, f=FC)[:, j % 2]
                    vec.tensor_tensor(mxw[:, 0:128], src[:, 0], src[:, 1], ALU.max)
                    vec.tensor_tensor(mnw[:, 0:128], src[:, 0], src[:, 1], ALU.min)
                    for fc in range(2, FC):
                        vec.drain()
                        vec.tensor_tensor(mxw[:, 0:128], mxw[:, 0:128], src[:, fc],
                                          ALU.max)
                        vec.tensor_tensor(mnw[:, 0:128], mnw[:, 0:128], src[:, fc],
                                          ALU.min)
                    vec.drain()
                    vec.tensor_scalar(mnw[:, 0:128], mnw[:, 0:128], -1.0, None, ALU.mult)
                    vec.drain()
                    vec.tensor_tensor(mxw[:, 0:128], mxw[:, 0:128], mnw[:, 0:128],
                                      ALU.max)
                    v_mx1.inc(vec.drain())
                    nmx_v += 1
                    vec.wait_ge(t_mx.h, nmx_v)
                    vec.tensor_reduce(csc[:, 1:2], pP1[:, :], AX.X, ALU.max)
                    v_pp1.inc(vec.drain())
                    if j >= 2:
                        vec.wait_ge(g_csc.h, 32 * (j - 1))
                    vec.tensor_scalar(csc[:, 4 + (j % 2):5 + (j % 2)], csc[:, 1:2],
                                      1.0 / 127.0, 1e-8, ALU.mult, ALU.max)
                    vec.drain()
                    vec.reciprocal(csc[:, 2 + (j % 2):3 + (j % 2)],
                                   csc[:, 4 + (j % 2):5 + (j % 2)])
                    v_mxr.inc(vec.drain())
                    vec.wait_ge(t_pb.h, j + 1)
                    vec.tensor_copy(rep1[:, :], pP1[:, :])
                    vec.drain()
                    v_pp1.inc(vec.drain())
                    if j >= 2:
                        g_w1.wait(vec, j % 2, 64 * (j // 2))
                    dstv = w1q_bf.rearrange("p (b f w) -> p b f w", b=2, f=FC)[:, j % 2]
                    for fc in range(FC):
                        vec.tensor_tensor(mxq[:, 0:128], src[:, fc], rep1[:, :], ALU.mult)
                        vec.drain()
                        vec.tensor_scalar(mxq[:, 0:128], mxq[:, 0:128],
                                          RND_C, RND_C, ALU.add, ALU.subtract)
                        vec.drain()
                        vec.tensor_scalar(dstv[:, fc], mxq[:, 0:128],
                                          127.0, -127.0, ALU.min, ALU.max)
                        vec.drain()
                    v_wq1.inc(vec.drain())
                elif kind == "w2m":
                    ld_w2.wait(vec, j % 2, 16 * (j // 2 + 1))
                    if j == 0 and NW1 > 0:
                        vec.wait_ge(t_mx.h, NW1)
                    src = w2p.rearrange("p (b f w) -> p b f w", b=2, f=W2C)[:, j % 2]
                    for f in range(W2C):
                        if j == 0 and f == 0:
                            vec.tensor_tensor(mxw[:, 0:OSL], src[:, 0],
                                              src[:, min(1, W2C - 1)], ALU.max)
                            vec.tensor_tensor(mnw[:, 0:OSL], src[:, 0],
                                              src[:, min(1, W2C - 1)], ALU.min)
                            vec.drain()
                            continue
                        if j == 0 and f == 1 and W2C > 1:
                            continue
                        vec.tensor_tensor(mxw[:, 0:OSL], mxw[:, 0:OSL], src[:, f],
                                          ALU.max)
                        vec.tensor_tensor(mnw[:, 0:OSL], mnw[:, 0:OSL], src[:, f],
                                          ALU.min)
                        vec.drain()
                    v_w2m.inc(vec.drain())
                    w2fold["n"] += 1
                    if w2fold["n"] == N2C:
                        vec.tensor_scalar(mnw[:, 0:OSL], mnw[:, 0:OSL], -1.0,
                                          None, ALU.mult)
                        vec.drain()
                        vec.tensor_tensor(mxw[:, 0:OSL], mxw[:, 0:OSL],
                                          mnw[:, 0:OSL], ALU.max)
                        v_mx1.inc(vec.drain())
                elif kind == "w2s":
                    if j > 0:
                        v_mx1.inc(vec.drain())
                    nmx_v += 1
                    vec.wait_ge(t_mx.h, nmx_v)
                    vec.tensor_reduce(csc[:, 1:2], pP1[:, :], AX.X, ALU.max)
                    v_pp1.inc(vec.drain())
                    vec.wait_ge(g_csc.h, 32 * NW1)
                    vec.tensor_scalar(csc[:, 4 + (j % 2):5 + (j % 2)], csc[:, 1:2],
                                      1.0 / 127.0, 1e-8, ALU.mult, ALU.max)
                    vec.drain()
                    vec.reciprocal(csc[:, 2 + (j % 2):3 + (j % 2)],
                                   csc[:, 4 + (j % 2):5 + (j % 2)])
                    v_mxr.inc(vec.drain())
                elif kind == "pb2":
                    vec.wait_ge(t_pb.h, NW1 + j + 1)
                    vec.tensor_copy(rep2[:, j * 128:(j + 1) * 128], pP1[:, :])
                    vec.drain()
                    v_pp1.inc(vec.drain())
                else:
                    ld_w2.wait(vec, j % 2, 16 * (N2C // 2 + j // 2 + 1))
                    if j >= 2:
                        g_w2.wait(vec, j % 2, 16 * (j // 2))
                    src = w2p.rearrange("p (b f w) -> p b f w", b=2, f=W2C)[:, j % 2]
                    dstv = w2q_bf.rearrange("p (b f w) -> p b f w", b=2, f=W2C)[:, j % 2]
                    for f in range(W2C):
                        vec.tensor_tensor(mxq[:, 0:OSL], src[:, f], rep2[:, :], ALU.mult)
                        vec.drain()
                        vec.tensor_scalar(mxq[:, 0:OSL], mxq[:, 0:OSL],
                                          RND_C, RND_C, ALU.add, ALU.subtract)
                        vec.drain()
                        vec.tensor_scalar(dstv[:, f], mxq[:, 0:OSL],
                                          127.0, -127.0, ALU.min, ALU.max)
                        vec.drain()
                    v_wq2.inc(vec.drain())
            vec.wait_ge(s_qb.h, 3 * h + 1)
            vec.wait_ge(t_r.h, 2 * h + 1)
            vec.tensor_tensor(m1r_sb[:, :], qb_sb[:, :], cos_sb[:, :], ALU.mult)
            vec.tensor_tensor(m2r_sb[:, :], pY[:, :], sin_sb[:, :], ALU.mult)
            vec.drain()
            vec.tensor_tensor(q_rope[:, h * T:(h + 1) * T], m1r_sb[:, :],
                              m2r_sb[:, :], ALU.add)
            v_rope.inc(vec.drain())
            vec.wait_ge(s_qb.h, 3 * h + 2)
            vec.wait_ge(t_r.h, 2 * h + 2)
            if h >= 1:
                st_k.wait(vec, 0, 16 * h)
            vec.tensor_tensor(m1r_sb[:, :], qb_sb[:, :], cos_sb[:, :], ALU.mult)
            vec.tensor_tensor(m2r_sb[:, :], pY[:, :], sin_sb[:, :], ALU.mult)
            vec.drain()
            vec.tensor_tensor(kr_sb[:, :], m1r_sb[:, :], m2r_sb[:, :], ALU.add)
            v_rope.inc(vec.drain())
            vec.wait_ge(t_vt.h, h + 1)
            if h >= 1:
                st_v.wait(vec, 0, 32 * h)
            vec.tensor_copy(vtm_sb[:, :], pT[:, 0:T])
            v_vte.inc(vec.drain())

    esA1b.close()
    esP2.close()
    esP3 = contextlib.ExitStack()
    q1slab = esP3.enter_context(nc.sbuf_tensor("q1slab", [128, 3 * HSL], BF16))
    h_sb = esP3.enter_context(nc.sbuf_tensor("h_sb", [128, 2 * T], F32))

    K1 = ISL // 128
    K2 = OSL // 128

    @blk.sync
    def _(sync):
        if STOP < 3:
            return
        sync.wait_ge(cc_s.h, 1)
        for r in range(2 * N_CORES):
            if r >= 1:
                sync.wait_ge(t_sw.h, r)
            if r < N_CORES:
                src = ags_out.ap()[r * ssz: r * ssz + ISL].rearrange("(k p) -> k p", p=128)
                ld_sw.inc(sync.dma_start(out=swsl[0:K1, :], in_=src), 16)
            else:
                rr = r - N_CORES
                src = ags_out.ap()[rr * ssz + ISL: (rr + 1) * ssz].rearrange("(k p) -> k p", p=128)
                ld_sw.inc(sync.dma_start(out=swsl[0:K2, :], in_=src), 16)

    @blk.tensor
    def _(tensor):
        if STOP < 3:
            return
        for r in range(2 * N_CORES):
            k = K1 if r < N_CORES else K2
            tensor.wait_ge(ld_sw.h, 16 * (r + 1))
            tensor.wait_ge(v_pp1.h, ppi["w"])
            t_sw.inc(tensor.transpose(pP1[:, 0:k], swsl[0:k, :], idf_sb[0:k, 0:k]))
            ppi["w"] += 1

    @blk.vector
    def _(vec):
        if STOP < 3:
            return
        for r in range(2 * N_CORES):
            vec.wait_ge(t_sw.h, r + 1)
            if r < N_CORES:
                vec.tensor_copy(sw1_col[:, r * K1:(r + 1) * K1], pP1[:, 0:K1])
            else:
                rr = r - N_CORES
                vec.tensor_copy(sw2_col[:, rr * K2:(rr + 1) * K2], pP1[:, 0:K2])
            vec.drain()
            v_swc.inc(vec.drain())
            v_pp1.inc(vec.drain())
        vec.wait_ge(g_gl1.h, 16)
        vec.tensor_scalar(csc[:, 6:7], sx1_col[:, :], 1.0 / 127.0, 1e-8,
                          ALU.mult, ALU.max)
        vec.drain()
        vec.reciprocal(r1_col[:, :], csc[:, 6:7])
        vec.drain()
        vec.tensor_scalar(s1_col[:, :], sw1_col[:, :], csc[:, 6:7], None, ALU.mult)
        v_sx.inc(vec.drain())

    @blk.sync
    def _(sync):
        if STOP < 3:
            return
        st_xn.wait_all(sync)
        sync.wait_ge(t_xtr.h, 2 * TC)
        for tc in range(TC):
            ld_xq.gate(sync, tc % 2)
            if tc >= 2:
                sync.wait_ge(t_xtr.h, 2 * TC + 2 * (tc - 1))
            ld_xq.issue(sync, tc % 2,
                        x_tiles[:, (tc % 2) * d:(tc % 2) * d + d],
                        xn_sp.ap()[tc])

    @blk.vector
    def _(vec):
        if STOP < 3:
            return
        def qx_evac(jt):
            for half in range(2):
                vec.wait_ge(t_xtr.h, 2 * TC + 2 * jt + half + 1)
                for jj in range(FH):
                    fc = half * FH + jj
                    vec.tensor_copy(xn_fm[:, fc * T + jt * 128: fc * T + (jt + 1) * 128],
                                    pT[:, jj * 128:(jj + 1) * 128])
                v_fme.inc(vec.drain())
        vec.wait_ge(t_p.h, 3 * NH)
        for tc in range(TC):
            ld_xq.wait(vec, tc % 2, 16 * (tc // 2 + 1))
            xo = x_tiles[:, (tc % 2) * d:(tc % 2) * d + d]
            vec.tensor_scalar(xo[:, :], xo[:, :], r1_col[:, :], RND_C,
                              ALU.mult, ALU.add)
            vec.drain()
            vec.tensor_scalar(xo[:, :], xo[:, :], RND_C, 127.0,
                              ALU.subtract, ALU.min)
            vec.drain()
            vec.tensor_scalar(xo[:, :], xo[:, :], -127.0, None, ALU.max)
            v_qx.inc(vec.drain())
            if tc >= 1:
                qx_evac(tc - 1)
        qx_evac(TC - 1)

    @blk.tensor
    def _(tensor):
        if STOP < 3:
            return
        tensor.wait_ge(v_vte.h, NH)
        for tc in range(TC):
            tensor.wait_ge(v_qx.h, tc + 1)
            for half in range(2):
                tensor.wait_ge(v_fme.h, 2 * TC + 2 * tc + half)
                for j in range(FH):
                    fc = half * FH + j
                    mm = tensor.transpose(
                        pT[:, j * 128:(j + 1) * 128],
                        x_tiles[:, (tc % 2) * d + fc * 128:(tc % 2) * d + fc * 128 + 128],
                        idf_sb[:, :])
                t_xtr.inc(mm)

    @blk.sync
    def _(sync):
        if STOP < 3:
            return
        sync.wait_ge(cc_b1.h, 1)
        sync.wait_ge(v_wq2.h, N2C)
        sync.wait_ge(v_rope.h, 2 * NH)
        sync.wait_ge(v_vte.h, NH)
        st_k.wait(sync, 0, 16 * NH)
        st_v.wait(sync, 0, 32 * NH)
        for hf in range(2 * N_CORES):
            r, hh = hf // 2, hf % 2
            ld_q1.gate(sync, hf % 3)
            if hf >= 3:
                sync.wait_ge(t_f1.h, min(((hf - 3) // 2 + 1) * NW1, IC))
            ld_q1.issue(sync, hf % 3,
                        q1slab[:, (hf % 3) * HSL:(hf % 3 + 1) * HSL]
                        .rearrange("p (f i) -> p f i", f=FC // 2),
                        agb1r[r, :, hh * (FC // 2):(hh + 1) * (FC // 2), :])

    @blk.tensor
    def _(tensor):
        if STOP < 3:
            return
        tensor.wait_ge(v_fme.h, 4 * TC)
        tensor.wait_ge(s_qb.h, 3 * NH)
        tensor.wait_ge(v_rope.h, 2 * NH)
        for ic in range(IC):
            r = ic // NW1
            for hf in (2 * r, 2 * r + 1):
                ld_q1.wait(tensor, hf % 3, 16 * (hf // 3 + 1))
            if ic >= 2:
                tensor.wait_ge(s_g.h, ic - 1)
            pp = pX if ic % 2 == 0 else pY
            for fc in range(FC):
                slot = (2 * r + fc // (FC // 2)) % 3
                off = slot * HSL + (fc % (FC // 2)) * ISL + (ic % NW1) * 128
                wt = q1slab[:, off:off + 128]
                for (o, w) in PC:
                    mm = tensor.matmul(pp[:, o:o + w], wt,
                                       xn_fm[:, fc * T + o: fc * T + o + w],
                                       start=(fc == 0), stop=(fc == FC - 1))
            t_f1.inc(mm)

    @blk.scalar
    def _(scalar):
        if STOP < 3:
            return
        for ic in range(IC):
            scalar.wait_ge(t_f1.h, ic + 1)
            if ic == 0:
                scalar.wait_ge(v_sx.h, 1)
            if ic >= 2:
                scalar.wait_ge(v_hx.h, ic - 1)
            st_h.gate(scalar, ic % 2)
            pp = pX if ic % 2 == 0 else pY
            scalar.activation(h_sb[:, (ic % 2) * T:(ic % 2) * T + T], pp[:, :],
                              AF.Gelu, bias=b1_sb[:, ic:ic + 1],
                              scale=s1_col[:, ic:ic + 1])
            s_g.inc(scalar.drain())
            st_h.issue(scalar, ic % 2, h_sp.ap()[ic],
                       h_sb[:, (ic % 2) * T:(ic % 2) * T + T])

    @blk.vector
    def _(vec):
        if STOP < 3:
            return
        for ic in range(IC):
            vec.wait_ge(s_g.h, ic + 1)
            vec.tensor_reduce(hmax_sb[:, ic:ic + 1],
                              h_sb[:, (ic % 2) * T:(ic % 2) * T + T], AX.X,
                              ALU.max, apply_absolute_value=True)
            v_hx.inc(vec.drain())
        vec.wait_ge(t_lm.h, 1)
        vec.tensor_reduce(csc[:, 0:1], hmax_sb[:, :], AX.X, ALU.max)
        v_lm.inc(vec.drain())

    @blk.tensor
    def _(tensor):
        if STOP < 3:
            return
        tensor.wait_ge(v_lm.h, 3)
        tensor.wait_ge(v_pp1.h, ppi["w"])
        t_lm.inc(tensor.matmul(pP1[0:1, :], csc[:, 0:1], idf_sb[:, :],
                               start=True, stop=True))
        ppi["w"] += 1

    @blk.vector
    def _(vec):
        if STOP < 3:
            return
        vec.wait_ge(t_lm.h, 2)
        vec.tensor_reduce(gm2_sb[:, :], pP1[0:1, :], AX.X, ALU.max)
        vec.drain()
        vec.tensor_scalar(g2row_sb[:, :], pP1[0:1, :], gm2_sb[0:1, 0:1],
                          None, ALU.max)
        vec.drain()
        v_pp1.inc(vec.drain())
        v_lm.inc(vec.drain())

    esT.close()
    esP3.close()
    esA1a.close()
    esP4 = contextlib.ExitStack()
    P4 = lambda n, s, dt: esP4.enter_context(nc.sbuf_tensor(n, s, dt))
    k_sb = P4("k_sb", [128, 2 * S], BF16)
    v_sb = P4("v_sb", [128, 2 * S], BF16)
    exp_sb = P4("exp_sb", [128, 4 * T], BF16)
    recip_sb = P4("recip_sb", [128, T], F32)
    acc2 = P4("acc2", [128, 2 * T], F32)
    hrd_sb = P4("hrd_sb", [128, 2 * T], F32) if QA else None
    es_qha = contextlib.ExitStack()
    qh_a = es_qha.enter_context(
        nc.sbuf_tensor("qh_a", [128, QA * T], BF16, side="right")) if QA else None
    es_ao = contextlib.ExitStack()
    attn_out = es_ao.enter_context(
        nc.sbuf_tensor("attn_out", [128, NH * T], BF16, side="right"))

    _qh_b_ref = []

    def qh_sl(ic, o, w):
        if ic < QA:
            return qh_a[:, ic * T + o: ic * T + o + w]
        qb = _qh_b_ref[0]
        return qb[:, (ic - QA) * T + o: (ic - QA) * T + o + w]
    esPO = contextlib.ExitStack()
    pO = esPO.enter_context(nc.psum_tensor("pO", [128, T], F32))
    pU = [esPO.enter_context(nc.psum_tensor(f"pU{i}", [128, PC[0][1]], F32))
          for i in range(NP)]

    agcr = agc_out.ap().rearrange("(r e) -> r e", r=GROUP)
    accb = v_acc.c
    p1b = s_p1.c

    @blk.sync
    def _(sync):
        if STOP < 4:
            return
        sync.wait_ge(cc_kv.h, 1)
        sync.wait_ge(t_f1.h, IC)
        st_h.wait_all(sync)
        sync.wait_ge(v_hx.h, IC)
        for h in range(NH):
            slot = h % 2
            ld_kv.gate(sync, slot)
            if h >= 2:
                sync.wait_ge(t_O.h, h - 1)
            for r in range(GROUP):
                src = agcr[r, KOFF + h * 128 * T: KOFF + (h + 1) * 128 * T] \
                    .rearrange("(p t) -> p t", p=128)
                ld_kv.issue(sync, slot,
                            k_sb[:, slot * S + r * T: slot * S + (r + 1) * T], src)
            for r in range(GROUP):
                src = agcr[r, VOFF + h * T * 128: VOFF + (h + 1) * T * 128] \
                    .rearrange("(tc p dc) -> p tc dc", p=128, dc=128)
                dstv = v_sb[:, slot * S + r * T: slot * S + (r + 1) * T] \
                    .rearrange("p (tc dc) -> p tc dc", dc=128)
                hh = TC // 2
                for h2 in range(2):
                    ld_kv.issue(sync, slot,
                                dstv[:, h2 * hh:(h2 + 1) * hh],
                                src[:, h2 * hh:(h2 + 1) * hh])

    @blk.tensor
    def _(tensor):
        if STOP < 4:
            return
        pS = [pX, pY]
        flat = [(h, kc) for h in range(NH) for kc in range(KC)]

        def emit_scores(h, kc):
            idx = h * KC + kc
            if kc == 0:
                ld_kv.wait(tensor, h % 2, 16 * 3 * GROUP * (h // 2 + 1))
            if idx == 0:
                tensor.wait_ge(s_g.h, IC)
                tensor.wait_ge(v_pp1.h, ppi["w"])
                tensor.wait_ge(v_fme.h, 4 * TC)
            if idx >= 2:
                tensor.wait_ge(s_exp.h, idx - 1)
            kt = k_sb[:, (h % 2) * S + kc * 128: (h % 2) * S + (kc + 1) * 128]
            for (o, w) in PC:
                mm = tensor.matmul(pS[idx % 2][:, o:o + w], kt,
                                   q_rope[:, h * T + o: h * T + o + w],
                                   start=True, stop=True)
            t_S.inc(mm)

        def emit_av(h, kc):
            idx = h * KC + kc
            tensor.wait_ge(s_exp.h, idx + 1)
            if kc == 0:
                tensor.wait_ge(v_att.h, h)
                tensor.wait_ge(s_rec.h, h)
            vt = v_sb[:, (h % 2) * S + kc * 128: (h % 2) * S + (kc + 1) * 128]
            ex = exp_sb[:, (idx % 4) * T:(idx % 4) * T + T]
            for (o, w) in PC:
                mm = tensor.matmul(pO[:, o:o + w], vt, ex[:, o:o + w],
                                   start=(kc == 0), stop=(kc == KC - 1))
            if kc % 2 == 1:
                for pi, (o, w) in enumerate(PC):
                    mm = tensor.matmul(pU[pi][:, :], ones_sb[:, :], ex[:, o:o + w],
                                       start=(kc == 1), stop=False)
            t_A.inc(mm)
            if kc == KC - 1:
                tensor.wait_ge(v_acc.h, accb + (h + 1) * (KC // 2))
                for pi, (o, w) in enumerate(PC):
                    tensor.matmul(pU[pi][:, :], onesf32_sb[:, :],
                                  acc2[:, (h % 2) * T + o:(h % 2) * T + o + w],
                                  start=False, stop=True)
                t_O.inc(tensor.drain())

        emit_scores(*flat[0])
        emit_scores(*flat[1])
        for i, (h, kc) in enumerate(flat):
            if i + 2 < len(flat):
                emit_scores(*flat[i + 2])
            emit_av(h, kc)

    @blk.scalar
    def _(scalar):
        if STOP < 4:
            return
        st_h.wait_all(scalar)
        for h in range(NH):
            for kc in range(KC):
                scalar.wait_ge(t_S.h, h * KC + kc + 1)
                if h * KC + kc >= 4:
                    scalar.wait_ge(t_A.h, h * KC + kc - 3)
                    j = h * KC + kc - 4
                    jh, jk = j // KC, j % KC
                    if jk % 2 == 0:
                        scalar.wait_ge(v_acc.h,
                                       accb + jh * (KC // 2) + jk // 2 + 1)
                idx = h * KC + kc
                s_exp.inc(scalar.activation(
                    exp_sb[:, (idx % 4) * T:(idx % 4) * T + T],
                    (pX if idx % 2 == 0 else pY)[:, :],
                    AF.Exp, scale=EXPS, bias=EXPB))
            scalar.wait_ge(t_O.h, h + 1)
            if h > 0:
                scalar.wait_ge(v_att.h, h)
            for pi, (o, w) in enumerate(PC):
                scalar.activation(recip_sb[:, o:o + w], pU[pi][:, :], AF.Ln)
            scalar.drain()
            s_rec.inc(scalar.activation(recip_sb[:, :], recip_sb[:, :],
                                        AF.Exp, scale=np.float32(-1.0)))

    @blk.vector
    def _(vec):
        if STOP < 4:
            return
        for h in range(NH):
            acc = acc2[:, (h % 2) * T:(h % 2) * T + T]
            for kc in range(0, KC, 2):
                idx = h * KC + kc
                vec.wait_ge(s_exp.h, idx + 1)
                ex = exp_sb[:, (idx % 4) * T:(idx % 4) * T + T]
                if kc == 0:
                    vec.tensor_scalar(acc[:, :], ex[:, :], 0.0,
                                      None, ALU.add)
                else:
                    vec.tensor_tensor(acc[:, :], acc[:, :], ex[:, :], ALU.add)
                v_acc.inc(vec.drain())
            vec.wait_ge(s_rec.h, h + 1)
            vec.tensor_tensor(attn_out[:, h * T:(h + 1) * T], pO[:, :],
                              recip_sb[:, :], ALU.mult)
            v_att.inc(vec.drain())

    esP4.close()
    esA2.close()
    esQR.close()
    esPO.close()
    esXY.close()
    esQH = contextlib.ExitStack()
    qh_b = esQH.enter_context(nc.sbuf_tensor("qh_b", [128, max(QB, 1) * T], BF16))
    _qh_b_ref.append(qh_b)
    hrd2_sb = esQH.enter_context(nc.sbuf_tensor("hrd2_sb", [128, 2 * T], F32))
    h2q_sb = esQH.enter_context(nc.sbuf_tensor("h2q_sb", [128, 2 * T], F32))

    esP5 = contextlib.ExitStack()
    P5 = lambda n, s, dt: esP5.enter_context(nc.sbuf_tensor(n, s, dt))
    wo_sb = P5("wo_sb", [128, 2 * FC * 128], BF16)
    asp_sb = P5("asp_sb", [128, 2 * T], F32)
    esW = contextlib.ExitStack()
    pW0 = esW.enter_context(nc.psum_tensor("pW0", [128, T], F32))
    pW1 = esW.enter_context(nc.psum_tensor("pW1", [128, T], F32))
    pF = esW.enter_context(nc.psum_tensor("pF", [128, 1024], F32))

    def pF_sl(s):
        c = (s % 2) * 512 + (s // 2) * 128
        return pF[:, c:c + 128]

    @blk.sync
    def _(sync):
        if STOP < 5:
            return
        sync.wait_ge(t_O.h, NH)
        st_h.wait_all(sync)
        for oc in range(FC):
            ld_wo.gate(sync, oc % 2)
            if oc >= 2:
                sync.wait_ge(t_wo.h, oc - 1)
            r = (oc * 128) // OSL
            co = (oc * 128) % OSL
            dst = wo_sb[:, (oc % 2) * FC * 128:(oc % 2 + 1) * FC * 128]
            fs = FC // 4
            for q4 in range(4):
                ld_wo.issue(sync, oc % 2,
                            dst.rearrange("p (f w) -> p f w", f=FC)
                            [:, q4 * fs:(q4 + 1) * fs],
                            agar[r, 3, :, q4 * fs:(q4 + 1) * fs, co:co + 128])


    @blk.gpsimd
    def _(g):
        if STOP < 5:
            return
        g.wait_ge(t_O.h, NH)
        g.wait_ge(s_rec.h, NH)
        g.wait_ge(v_att.h, NH)
        st_h.wait_all(g)
        for ic in range(IC):
            ld_h.gate(g, ic % 2)
            if ic >= 2:
                g.wait_ge(s_p1.h, p1b + ic - 1)
            ld_h.issue(g, ic % 2,
                       hrd2_sb[:, (ic % 2) * T:(ic % 2) * T + T],
                       h_sp.ap()[ic])

    @blk.tensor
    def _(tensor):
        if STOP < 5:
            return
        for oc in range(FC):
            ld_wo.wait(tensor, oc % 2, 64 * (oc // 2 + 1))
            if oc == 0:
                tensor.wait_ge(v_att.h, NH)
                tensor.wait_ge(s_rec.h, NH)
                tensor.wait_ge(s_exp.h, NH * KC)
            if oc >= 2:
                tensor.wait_ge(s_wo.h, oc - 1)
            pp = pW0 if oc % 2 == 0 else pW1
            for fc in range(FC):
                wt = wo_sb[:, (oc % 2) * FC * 128 + fc * 128:(oc % 2) * FC * 128 + fc * 128 + 128]
                for (o, w) in PC:
                    mm = tensor.matmul(pp[:, o:o + w], wt,
                                       attn_out[:, fc * T + o: fc * T + o + w],
                                       start=(fc == 0), stop=(fc == FC - 1))
            t_wo.inc(mm)

    @blk.scalar
    def _(scalar):
        if STOP < 5:
            return
        npo = IC // FC
        for oc in range(FC):
            scalar.wait_ge(t_wo.h, oc + 1)
            st_w.gate(scalar, oc % 2)
            scalar.activation(asp_sb[:, (oc % 2) * T:(oc % 2) * T + T],
                              (pW0 if oc % 2 == 0 else pW1)[:, :], AF.Copy)
            s_wo.inc(scalar.drain())
            st_w.issue(scalar, oc % 2, attn_sp.ap()[oc],
                       asp_sb[:, (oc % 2) * T:(oc % 2) * T + T])
            for ic in range(oc * npo, (oc + 1) * npo):
                if ic == 0:
                    scalar.wait_ge(v_sx.h, 2)
                if ic >= 2:
                    scalar.wait_ge(v_qh.h, ic - 1)
                ld_h.wait(scalar, ic % 2, 16 * (ic // 2 + 1))
                s_p1.inc(scalar.activation(
                    h2q_sb[:, (ic % 2) * T:(ic % 2) * T + T],
                    hrd2_sb[:, (ic % 2) * T:(ic % 2) * T + T],
                    AF.Identity, bias=np.float32(RND_C), scale=r2_col[:, 0:1]))

    @blk.vector
    def _(vec):
        if STOP < 5:
            return
        vec.wait_ge(g_gl2.h, 16)
        vec.tensor_scalar(csc[:, 7:8], sx2_col[:, :], 1.0 / 127.0, 1e-8,
                          ALU.mult, ALU.max)
        vec.drain()
        vec.reciprocal(r2_col[:, :], csc[:, 7:8])
        vec.drain()
        vec.tensor_scalar(s2_col[:, :], sw2_col[:, :], csc[:, 7:8], None, ALU.mult)
        v_sx.inc(vec.drain())
        for ic in range(IC):
            vec.wait_ge(s_p1.h, p1b + ic + 1)
            hs = h2q_sb[:, (ic % 2) * T:(ic % 2) * T + T]
            vec.tensor_scalar(hs[:, :], hs[:, :], RND_C, 127.0,
                              ALU.subtract, ALU.min)
            vec.drain()
            vec.tensor_scalar(qh_sl(ic, 0, T), hs[:, :], -127.0, None, ALU.max)
            v_qh.inc(vec.drain())

    es_ao.close()
    esP5.close()
    esP6 = contextlib.ExitStack()
    P6 = lambda n, s, dt: esP6.enter_context(nc.sbuf_tensor(n, s, dt))
    q2_sb = P6("q2_sb", [128, 2 * IC * 128], BF16)
    f2_sb = P6("f2_sb", [128, 2 * T], F32)
    asp2_sb = P6("asp2_sb", [128, T], F32)
    fin_sb = P6("fin_sb", [128, T], F32)
    xcol_sb = P6("xcol_sb", [128, 2 * T], F32)
    outt_sb = P6("outt_sb", [128, 4 * 128], F32)

    @blk.sync
    def _(sync):
        if STOP < 6:
            return
        sync.wait_ge(s_wo.h, FC)
        st_w.wait_all(sync)
        for oc in range(FC):
            ld_q2.gate(sync, oc % 2)
            if oc >= 2:
                sync.wait_ge(t_f2.h, oc - 1)
            r = (oc * 128) // OSL
            co = (oc * 128) % OSL
            i16 = max(1, IC // 16)
            for q16 in range(IC // i16):
                ld_q2.issue(sync, oc % 2,
                            q2_sb.rearrange("p (b f w) -> p b f w", b=2, f=IC)
                            [:, oc % 2, q16 * i16:(q16 + 1) * i16],
                            agb2r[r, :, q16 * i16:(q16 + 1) * i16, co:co + 128])
            if oc >= 1:
                sync.wait_ge(v_fin.h, oc)
            ld_fa.inc(sync.dma_start(
                out=asp2_sb[:, :], in_=attn_sp.ap()[oc]), 16)
            ld_fx.gate(sync, oc % 2)
            if oc >= 2:
                sync.wait_ge(v_ot.h, (oc - 1) * TC)
            hh = TC // 2
            for h2 in range(2):
                ld_fx.issue(sync, oc % 2,
                            xcol_sb.rearrange("p (b tc w) -> p b tc w", b=2, tc=TC)
                            [:, oc % 2, h2 * hh:(h2 + 1) * hh],
                            xcolr[:, h2 * hh:(h2 + 1) * hh, oc * 128:(oc + 1) * 128])
        st_out.wait_all(sync)

    def _btr(tensor, oc):
        tensor.wait_ge(v_fin.h, oc + 1)
        for tc in range(TC):
            ti = oc * TC + tc
            if ti >= 2:
                tensor.wait_ge(v_ot.h, ti - 1)
            mm = tensor.transpose(pF_sl(ti % 4),
                                  fin_sb[:, tc * 128:(tc + 1) * 128],
                                  idf_sb[:, :])
            t_F.inc(mm)

    @blk.tensor
    def _(tensor):
        if STOP < 6:
            return
        tensor.wait_ge(cc_b2.h, 1)
        boff = IC // 4
        bstep = max(1, (IC - boff) // TC)
        bt = {boff + bstep * tc: tc for tc in range(TC)}
        for oc in range(FC):
            ld_q2.wait(tensor, oc % 2, 16 * (IC // max(1, IC // 16)) * (oc // 2 + 1))
            if oc == 0:
                tensor.wait_ge(s_wo.h, FC)
            if oc >= 2:
                tensor.wait_ge(s_f2.h, oc - 1)
            pp = pW0 if oc % 2 == 0 else pW1
            for ic in range(IC):
                if oc == 0:
                    tensor.wait_ge(v_qh.h, ic + 1)
                if oc >= 1 and ic in bt:
                    tc = bt[ic]
                    ti = (oc - 1) * TC + tc
                    if tc == 0:
                        tensor.wait_ge(v_fin.h, oc)
                    if ti >= 2:
                        tensor.wait_ge(v_ot.h, ti - 1)
                    t_F.inc(tensor.transpose(pF_sl(ti % 4),
                                             fin_sb[:, tc * 128:(tc + 1) * 128],
                                             idf_sb[:, :]))
                wt = q2_sb[:, (oc % 2) * IC * 128 + ic * 128:(oc % 2) * IC * 128 + ic * 128 + 128]
                for (o, w) in PC:
                    mm = tensor.matmul(pp[:, o:o + w], wt, qh_sl(ic, o, w),
                                       start=(ic == 0), stop=(ic == IC - 1))
            t_f2.inc(mm)
        _btr(tensor, FC - 1)

    @blk.scalar
    def _(scalar):
        if STOP < 6:
            return
        for oc in range(FC):
            scalar.wait_ge(t_f2.h, oc + 1)
            if oc == 0:
                scalar.wait_ge(v_sx.h, 2)
            if oc >= 2:
                scalar.wait_ge(v_fin.h, oc - 1)
            s_f2.inc(scalar.activation(
                f2_sb[:, (oc % 2) * T:(oc % 2) * T + T],
                (pW0 if oc % 2 == 0 else pW1)[:, :], AF.Identity,
                bias=b2_sb[:, oc:oc + 1], scale=s2_col[:, oc:oc + 1]))

    @blk.vector
    def _(vec):
        if STOP < 6:
            return
        for oc in range(FC):
            vec.wait_ge(s_f2.h, oc + 1)
            vec.wait_ge(ld_fa.h, 16 * (oc + 1))
            if oc >= 1:
                vec.wait_ge(t_F.h, oc * TC)
            vec.tensor_tensor(fin_sb[:, :],
                              f2_sb[:, (oc % 2) * T:(oc % 2) * T + T],
                              asp2_sb[:, :], ALU.add)
            v_fin.inc(vec.drain())
            ld_fx.wait(vec, oc % 2, 32 * (oc // 2 + 1))
            for tc in range(TC):
                ti = oc * TC + tc
                vec.wait_ge(t_F.h, ti + 1)
                if ti >= 4:
                    st_out.wait(vec, ti % 4, 16 * ((ti - 4) // 4 + 1))
                vec.tensor_tensor(
                    outt_sb[:, (ti % 4) * 128:(ti % 4) * 128 + 128],
                    pF_sl(ti % 4),
                    xcol_sb[:, (oc % 2) * T + tc * 128:(oc % 2) * T + (tc + 1) * 128],
                    ALU.add)
                v_ot.inc(vec.drain())

    @blk.gpsimd
    def _(g):
        if STOP < 6:
            return
        for oc in range(FC):
            for tc in range(TC):
                ti = oc * TC + tc
                g.wait_ge(v_ot.h, ti + 1)
                st_out.gate(g, ti % 4)
                st_out.issue(g, ti % 4,
                             out_t[tc * 128:(tc + 1) * 128, oc * 128:(oc + 1) * 128],
                             outt_sb[:, (ti % 4) * 128:(ti % 4) * 128 + 128])

    if STOP < 6:
        @blk.sync
        def _(sync):
            if STOP >= 1:
                st_xn.wait_all(sync)
                sync.wait_ge(v_lm.h, 2)
                sync.wait_ge(v_fme.h, 2 * TC)
            if STOP >= 2:
                st_k.wait(sync, 0, 16 * NH)
                st_v.wait(sync, 0, 32 * NH)
                sync.wait_ge(v_wq2.h, N2C)
                sync.wait_ge(v_vte.h, NH)
            if STOP >= 3:
                st_h.wait_all(sync)
                sync.wait_ge(v_lm.h, 4)
                sync.wait_ge(v_fme.h, 4 * TC)
            if STOP >= 4:
                sync.wait_ge(v_att.h, NH)
            if STOP >= 5:
                st_w.wait_all(sync)
                sync.wait_ge(v_qh.h, IC)

    esP6.close()
    esQH.close()
    es_qha.close()
    esW.close()
    es.close()
    return nc



_CFG_FULL = {"d": 2048, "I": 8192, "T": 1024, "qa": 0}
_NC_CACHE = {}


def _get_nc(key, cfg):
    if key not in _NC_CACHE:
        _NC_CACHE[key] = build_nc(dict(cfg))
    return _NC_CACHE[key]


def host_inputs(core, x_full, Wq, bq, Wk, bk, Wv, bv, Wo, w1, b1, w2, b2, cfg):
    d, I, T = cfg["d"], cfg["I"], cfg["T"]
    NH, FC, IC = d // 128, d // 128, I // 128
    OSL, ISL = d // N_CORES, I // N_CORES
    b = core // GROUP
    q0 = (core % GROUP) * T
    f32 = np.float32
    bf16 = ml_dtypes.bfloat16

    m = {}
    m["x_in"] = np.ascontiguousarray(x_full[b, q0:q0 + T, :], dtype=f32)
    m["wqT_sl"] = np.ascontiguousarray(Wq.T[:, core * OSL:(core + 1) * OSL], dtype=f32)
    m["wkT_sl"] = np.ascontiguousarray(Wk.T[:, core * OSL:(core + 1) * OSL], dtype=f32)
    m["wvT_sl"] = np.ascontiguousarray(Wv.T[:, core * OSL:(core + 1) * OSL], dtype=f32)
    m["woT_sl"] = np.ascontiguousarray(Wo.T[:, core * OSL:(core + 1) * OSL], dtype=f32)
    m["w1T_sl"] = np.ascontiguousarray(w1.T[:, core * ISL:(core + 1) * ISL], dtype=f32)
    m["w2T_sl"] = np.ascontiguousarray(w2.T[:, core * OSL:(core + 1) * OSL], dtype=f32)
    m["bq_t"] = np.ascontiguousarray(bq.reshape(NH, 128).T, dtype=f32)
    m["bk_t"] = np.ascontiguousarray(bk.reshape(NH, 128).T, dtype=f32)
    m["bv_t"] = np.ascontiguousarray(bv.reshape(NH, 128).T, dtype=f32)
    m["b1_t"] = np.ascontiguousarray(b1.reshape(IC, 128).T, dtype=f32)
    m["b2_t"] = np.ascontiguousarray(b2.reshape(FC, 128).T, dtype=f32)
    hd = 128
    inv = (1.0 / (np.float32(1000000.0) **
                  (np.arange(0, hd, 2, dtype=f32) / f32(hd)))).astype(f32)
    pos = np.arange(q0, q0 + T, dtype=f32)
    fr = np.outer(pos, inv).astype(f32)
    emb = np.concatenate([fr, fr], axis=1)
    m["cos_t"] = np.ascontiguousarray(np.cos(emb).T, dtype=f32)
    m["sin_t"] = np.ascontiguousarray(np.sin(emb).T, dtype=f32)
    m["id_bf"] = np.eye(128, dtype=bf16)
    m["id_f32"] = np.eye(128, dtype=f32)
    R = np.zeros((128, 128), dtype=f32)
    for i in range(64):
        R[i, i + 64] = -1.0
        R[i + 64, i] = 1.0
    m["rT_bf"] = np.ascontiguousarray(R.T).astype(bf16)
    m["ones_bf"] = np.ones((128, 128), dtype=bf16)
    m["ones_f32"] = np.ones((128, 128), dtype=f32)
    m["onesf"] = np.ones((1, 128), dtype=f32)
    return m


def kernel(L0_attn_hidden_states, L0_attn_Wq, L0_attn_bq, L0_attn_Wk, L0_attn_bk,
           L0_attn_Wv, L0_attn_bv, L0_attn_Wo, L0_ffn_w1, L0_ffn_b1,
           L0_ffn_w2, L0_ffn_b2):
    cfg = _CFG_FULL
    x = np.asarray(L0_attn_hidden_states, dtype=np.float32)
    args = [np.asarray(a, dtype=np.float32) for a in
            (L0_attn_Wq, L0_attn_bq, L0_attn_Wk, L0_attn_bk, L0_attn_Wv,
             L0_attn_bv, L0_attn_Wo, L0_ffn_w1, L0_ffn_b1, L0_ffn_w2, L0_ffn_b2)]
    nc = _get_nc("full", cfg)
    in_maps = [host_inputs(c, x, *args, cfg) for c in range(N_CORES)]
    res = run_bass_kernel_spmd(nc, in_maps, core_ids=list(range(N_CORES)))
    T, d = cfg["T"], cfg["d"]
    B, Sq = x.shape[0], x.shape[1]
    out = np.empty((B, Sq, d), dtype=np.float32)
    for c in range(N_CORES):
        b = c // GROUP
        q0 = (c % GROUP) * T
        out[b, q0:q0 + T, :] = np.asarray(res.results[c]["out"])
    return out


# version 15
# speedup vs baseline: 1.0651x; 1.0213x over previous
import contextlib

import numpy as np
import ml_dtypes

import concourse.bass as bass
import concourse.mybir as mybir
from concourse.bass_utils import run_bass_kernel_spmd

F32 = mybir.dt.float32
BF16 = mybir.dt.bfloat16
AF = mybir.ActivationFunctionType
ALU = mybir.AluOpType
AX = mybir.AxisListType

RND_C = 3.0 * 2.0**22
N_CORES = 8
GROUP = 4


class Sem:
    def __init__(self, nc, es, name):
        self.h = es.enter_context(nc.semaphore(name))
        self.c = 0

    def inc(self, instr, n=1):
        instr.then_inc(self.h, n)
        self.c += n
        return self.c


class DSem:

    def __init__(self, nc, es, name, nslots):
        self.s = [Sem(nc, es, f"{name}{i}") for i in range(nslots)]

    def gate(self, eng, slot):
        sm = self.s[slot]
        if sm.c > 0:
            eng.wait_ge(sm.h, sm.c)

    def issue(self, eng, slot, out, in_):
        sm = self.s[slot]
        sm.inc(eng.dma_start(out=out, in_=in_), 16)
        return sm.c

    def cnt(self, slot):
        return self.s[slot].c

    def wait(self, eng, slot, cnt):
        eng.wait_ge(self.s[slot].h, cnt)

    def wait_all(self, eng):
        for sm in self.s:
            if sm.c:
                eng.wait_ge(sm.h, sm.c)


def _pieces(T):
    n = (T + 511) // 512
    w = T // n
    assert w * n == T
    return [(i * w, w) for i in range(n)]


def build_nc(cfg):
    d = cfg["d"]
    I = cfg["I"]
    T = cfg["T"]
    NH = d // 128
    FC = d // 128
    IC = I // 128
    TC = T // 128
    S = GROUP * T
    KC = S // 128
    OSL = d // N_CORES
    ISL = I // N_CORES
    PC = _pieces(T)
    NP = len(PC)
    assert OSL % 128 == 0 and ISL % 128 == 0
    NW1 = ISL // 128
    NW2 = OSL // 128
    W2C = max(1, IC // 16)
    N2C = IC // W2C
    n_bn = d // 512
    assert n_bn * 512 == d and T % 128 == 0
    QA = min(IC, cfg.get("qa", 20))
    QB = IC - QA
    EXPS = np.float32(np.float32(1.0) / np.sqrt(np.float32(128.0)))
    EXPB = np.float32(-20.0)
    HSL = (FC // 2) * ISL
    FH = FC // 2

    STOP = cfg.get("stop", 99)
    nc = bass.Bass("TRN2", target_bir_lowering=False)

    x_in = nc.dram_tensor("x_in", [T, d], F32, kind="ExternalInput").ap()
    wqT = nc.dram_tensor("wqT_sl", [d, OSL], F32, kind="ExternalInput").ap()
    wkT = nc.dram_tensor("wkT_sl", [d, OSL], F32, kind="ExternalInput").ap()
    wvT = nc.dram_tensor("wvT_sl", [d, OSL], F32, kind="ExternalInput").ap()
    woT = nc.dram_tensor("woT_sl", [d, OSL], F32, kind="ExternalInput").ap()
    w1T = nc.dram_tensor("w1T_sl", [d, ISL], F32, kind="ExternalInput").ap()
    w2T = nc.dram_tensor("w2T_sl", [I, OSL], F32, kind="ExternalInput").ap()
    bq_t = nc.dram_tensor("bq_t", [128, NH], F32, kind="ExternalInput").ap()
    bk_t = nc.dram_tensor("bk_t", [128, NH], F32, kind="ExternalInput").ap()
    bv_t = nc.dram_tensor("bv_t", [128, NH], F32, kind="ExternalInput").ap()
    b1_t = nc.dram_tensor("b1_t", [128, IC], F32, kind="ExternalInput").ap()
    b2_t = nc.dram_tensor("b2_t", [128, FC], F32, kind="ExternalInput").ap()
    cos_t = nc.dram_tensor("cos_t", [128, T], F32, kind="ExternalInput").ap()
    sin_t = nc.dram_tensor("sin_t", [128, T], F32, kind="ExternalInput").ap()
    id_bf = nc.dram_tensor("id_bf", [128, 128], BF16, kind="ExternalInput").ap()
    id_f32 = nc.dram_tensor("id_f32", [128, 128], F32, kind="ExternalInput").ap()
    rT_bf = nc.dram_tensor("rT_bf", [128, 128], BF16, kind="ExternalInput").ap()
    ones_bf = nc.dram_tensor("ones_bf", [128, 128], BF16, kind="ExternalInput").ap()
    ones_f32 = nc.dram_tensor("ones_f32", [128, 128], F32, kind="ExternalInput").ap()
    onesf_t = nc.dram_tensor("onesf", [1, 128], F32, kind="ExternalInput").ap()
    out_t = nc.dram_tensor("out", [T, d], F32, kind="ExternalOutput").ap()

    aga_in = nc.dram_tensor("aga_in", [4 * d, OSL], BF16)
    aga_out = nc.dram_tensor("aga_out", [N_CORES * 4 * d, OSL], BF16, addr_space="Shared")
    agb1_in = nc.dram_tensor("agb1_in", [d, ISL], BF16)
    agb1_out = nc.dram_tensor("agb1_out", [N_CORES * d, ISL], BF16, addr_space="Shared")
    agb2_in = nc.dram_tensor("agb2_in", [I, OSL], BF16)
    agb2_out = nc.dram_tensor("agb2_out", [N_CORES * I, OSL], BF16, addr_space="Shared")
    ssz = ISL + OSL
    ags_in = nc.dram_tensor("ags_in", [ssz], F32)
    ags_out = nc.dram_tensor("ags_out", [N_CORES * ssz], F32, addr_space="Shared")
    m1_in = nc.dram_tensor("m1_in", [128], F32)
    m1_out = nc.dram_tensor("m1_out", [128], F32, addr_space="Shared")
    m2_in = nc.dram_tensor("m2_in", [128], F32)
    m2_out = nc.dram_tensor("m2_out", [128], F32, addr_space="Shared")
    agc_sz = NH * 128 * T * 2
    agc_in = nc.dram_tensor("agc_in", [agc_sz], BF16)
    agc_out = nc.dram_tensor("agc_out", [GROUP * agc_sz], BF16)
    xn_sp = nc.dram_tensor("xn_sp", [TC, 128, d], F32)
    h_sp = nc.dram_tensor("h_sp", [IC, 128, T], F32)
    attn_sp = nc.dram_tensor("attn_sp", [FC, 128, T], F32)
    rs_scr = nc.dram_tensor("rs_scr", [max(ISL, OSL)], F32)
    KOFF = 0
    VOFF = NH * 128 * T

    es = contextlib.ExitStack()
    SM = lambda name: Sem(nc, es, name)
    DS = lambda name, n: DSem(nc, es, name, n)
    ld_aux = SM("ld_aux")
    ld_x = DS("ld_x", 3); ld_w1 = DS("ld_w1", 2); ld_w2 = DS("ld_w2", 2)
    ld_wq = DS("ld_wq", 2); ld_wo = DS("ld_wo", 2); ld_q1 = DS("ld_q1", 4)
    ld_kv = DS("ld_kv", 2); ld_h = DS("ld_h", 2); ld_q2 = DS("ld_q2", 2)
    ld_fa = SM("ld_fa"); ld_fx = DS("ld_fx", 2); ld_sw = SM("ld_sw")
    ld_xq = DS("ld_xq", 3)
    st_out = DS("st_out", 4)
    st_xn = DS("st_xn", 3); st_k = DS("st_k", 1); st_v = DS("st_v", 1)
    st_h = DS("st_h", 2); st_w = DS("st_w", 2)
    g_dma = SM("g_dma"); g_csc = SM("g_csc")
    g_w1 = DS("g_w1", 2); g_w2 = DS("g_w2", 2)
    cc_a = SM("cc_a"); cc_b1 = SM("cc_b1"); cc_b2 = SM("cc_b2")
    cc_s = SM("cc_s"); cc_m1 = SM("cc_m1"); cc_m2 = SM("cc_m2"); cc_kv = SM("cc_kv")
    t_xtr = SM("t_xtr"); t_mx = SM("t_mx"); t_sw = SM("t_sw")
    t_p = SM("t_p"); t_r = SM("t_r"); t_vt = SM("t_vt"); t_f1 = SM("t_f1")
    t_S = SM("t_S"); t_A = SM("t_A"); t_O = SM("t_O"); t_wo = SM("t_wo")
    t_f2 = SM("t_f2"); t_F = SM("t_F")
    v_ve = SM("v_ve"); v_xnb = SM("v_xnb"); v_fme = SM("v_fme")
    v_lm = SM("v_lm"); v_mx1 = SM("v_mx1"); v_w2m = SM("v_w2m")
    v_mxr = SM("v_mxr"); v_wq1 = SM("v_wq1"); v_wq2 = SM("v_wq2")
    v_swc = SM("v_swc"); v_sx = SM("v_sx"); v_rope = SM("v_rope")
    v_vte = SM("v_vte"); v_qx = SM("v_qx"); v_hx = SM("v_hx"); v_att = SM("v_att")
    v_qh = SM("v_qh"); v_fin = SM("v_fin"); v_ot = SM("v_ot"); v_pp1 = SM("v_pp1")
    s_ln = SM("s_ln"); s_qb = SM("s_qb"); s_g = SM("s_g"); s_exp = SM("s_exp")
    s_rec = SM("s_rec"); s_wo = SM("s_wo"); s_f2 = SM("s_f2")
    v_acc = v_vte; s_p1 = s_qb
    t_pb = SM("t_pb"); t_lm = SM("t_lm")
    g_rr1 = SM("g_rr1"); g_rr2 = SM("g_rr2"); g_gl1 = SM("g_gl1"); g_gl2 = SM("g_gl2")


    sb = lambda name, shape, dt: es.enter_context(nc.sbuf_tensor(name, shape, dt))
    id_sb = sb("id_sb", [128, 128], BF16)
    idf_sb = sb("idf_sb", [128, 128], F32)
    rT_sb = sb("rT_sb", [128, 128], BF16)
    ones_sb = sb("ones_sb", [128, 128], BF16)
    onesf32_sb = sb("onesf32_sb", [128, 128], F32)
    bq_sb = sb("bq_sb", [128, NH], F32)
    bk_sb = sb("bk_sb", [128, NH], F32)
    bv_sb = sb("bv_sb", [128, NH], F32)
    b1_sb = sb("b1_sb", [128, IC], F32)
    b2_sb = sb("b2_sb", [128, FC], F32)
    sw1_col = sb("sw1_col", [128, IC], F32)
    sw2_col = sb("sw2_col", [128, FC], F32)
    s1_col = sb("s1_col", [128, IC], F32)
    s2_col = sb("s2_col", [128, FC], F32)
    onesf_sb = sb("onesf_sb", [1, 128], F32)
    g1row_sb = sb("g1row_sb", [1, 128], F32)
    g2row_sb = sb("g2row_sb", [1, 128], F32)
    gm1_sb = sb("gm1_sb", [1, 1], F32)
    gm2_sb = sb("gm2_sb", [1, 1], F32)
    sx1_col = sb("sx1_col", [128, 1], F32)
    sx2_col = sb("sx2_col", [128, 1], F32)
    r1_col = sb("r1_col", [128, 1], F32)
    r2_col = sb("r2_col", [128, 1], F32)
    csc = sb("csc", [128, 8], F32)
    mv_all = sb("mv_all", [128, 2 * TC], F32)
    ve_all = sb("ve_all", [128, TC], F32)
    rstd_all = sb("rstd_all", [128, TC], F32)
    ln_tmp = sb("ln_tmp", [128, TC], F32)
    lmax_sb = sb("lmax_sb", [128, TC], F32)
    hmax_sb = sb("hmax_sb", [128, IC], F32)

    blk = es.enter_context(nc.Block())
    ppi = {"w": 0}

    esQR = contextlib.ExitStack()
    q_rope = esQR.enter_context(nc.sbuf_tensor("q_rope", [128, NH * T], BF16))
    esA2 = contextlib.ExitStack()
    A2 = lambda n, s, dt: esA2.enter_context(nc.sbuf_tensor(n, s, dt))
    xn_fm = A2("xn_fm", [128, FC * T], BF16)
    cos_sb = A2("cos_sb", [128, T], F32)
    sin_sb = A2("sin_sb", [128, T], F32)
    esA1a = contextlib.ExitStack()
    A1a = lambda n, s, dt: esA1a.enter_context(nc.sbuf_tensor(n, s, dt))
    x_tiles = A1a("x_tiles", [128, 3 * d], F32)
    bn_scr = A1a("bn_scr", [128, 6 * n_bn], F32)
    swsl = A1a("swsl", [max(ISL, OSL) // 128, 128], F32)
    esP2 = contextlib.ExitStack()
    P2 = lambda n, s, dt: esP2.enter_context(nc.sbuf_tensor(n, s, dt))
    wsl3 = P2("wsl3", [128, 2 * 3 * FC * 128], BF16)
    qb_sb = P2("qb_sb", [128, T], BF16)
    vb_sb = P2("vb_sb", [128, T], F32)
    vtm_sb = P2("vtm_sb", [128, T], BF16)
    kr_sb = P2("kr_sb", [128, T], BF16)
    m1r_sb = P2("m1r_sb", [128, T], F32)
    m2r_sb = P2("m2r_sb", [128, T], F32)
    esA1b = contextlib.ExitStack()
    A1b = lambda n, s, dt: esA1b.enter_context(nc.sbuf_tensor(n, s, dt))
    w1p = A1b("w1p", [128, 2 * FC * 128], F32)
    w1q_bf = A1b("w1q_bf", [128, 2 * FC * 128], BF16)
    w2p = A1b("w2p", [128, 2 * W2C * OSL], F32)
    w2q_bf = A1b("w2q_bf", [128, 2 * W2C * OSL], BF16)
    mxw = A1b("mxw", [128, max(128, OSL)], F32)
    mnw = A1b("mnw", [128, max(128, OSL)], F32)
    mxq = A1b("mxq", [128, max(128, OSL)], F32)
    rep1 = A1b("rep1", [128, 128], F32)
    rep2 = A1b("rep2", [128, OSL], F32)
    rr1 = A1b("rr1", [1, 128], F32)
    rr2 = A1b("rr2", [1, OSL], F32)
    esXY = contextlib.ExitStack()
    pX = esXY.enter_context(nc.psum_tensor("pX", [128, T], F32))
    pY = esXY.enter_context(nc.psum_tensor("pY", [128, T], F32))
    esT = contextlib.ExitStack()
    pT = esT.enter_context(nc.psum_tensor("pT", [128, 1024], F32))
    pP1 = esT.enter_context(nc.psum_tensor("pP1", [128, 128], F32))

    w1r = w1T.rearrange("(f p) i -> p f i", p=128)
    w2r = w2T.rearrange("(f p) o -> p f o", p=128)
    agar = aga_out.ap().rearrange("(r j f p) o -> r j p f o", j=4, f=FC, p=128)
    agb1r = agb1_out.ap().rearrange("(r f p) i -> r p f i", f=FC, p=128)
    agb2r = agb2_out.ap().rearrange("(r f p) o -> r p f o", f=IC, p=128)
    agb1w = agb1_in.ap().rearrange("(f p) i -> p f i", p=128)
    agb2w = agb2_in.ap().rearrange("(c f p) o -> c p f o", f=W2C, p=128)
    xcolr = x_in.rearrange("(tc p) o -> p tc o", p=128)

    units = ([("w1", j) for j in range(NW1)]
             + [("w2m", j) for j in range(N2C)]
             + [("w2s", j) for j in range(NW2)]
             + [("pb2", j) for j in range(NW2)]
             + [("w2q", j) for j in range(N2C)])
    NHP = max(1, NH // 2)
    per_head = [units[(len(units) * h) // NHP:(len(units) * (h + 1)) // NHP]
                if h < NHP else [] for h in range(NH)]


    @blk.sync
    def _(sync):
        if STOP < 1:
            return
        for src, dst in [(cos_t, cos_sb), (sin_t, sin_sb), (id_bf, id_sb),
                         (id_f32, idf_sb), (rT_bf, rT_sb), (ones_bf, ones_sb),
                         (ones_f32, onesf32_sb),
                         (bq_t, bq_sb), (bk_t, bk_sb), (bv_t, bv_sb),
                         (b1_t, b1_sb), (b2_t, b2_sb), (onesf_t, onesf_sb)]:
            ld_aux.inc(sync.dma_start(out=dst[:, :], in_=src), 16)
        for tc in range(TC):
            sl = tc % 3
            ld_x.gate(sync, sl)
            if tc >= 3:
                st_xn.wait(sync, (tc - 3) % 3, 16 * ((tc - 3) // 3 + 1))
                sync.wait_ge(t_xtr.h, 2 * (tc - 2))
            ld_x.issue(sync, sl,
                       x_tiles[:, sl * d:sl * d + d],
                       x_in[tc * 128:(tc + 1) * 128, :])

    @blk.gpsimd
    def _(g):
        if STOP < 2:
            return
        for j, w in enumerate([wqT, wkT, wvT, woT]):
            g_dma.inc(g.dma_start(out=aga_in[j * d:j * d + d // 2, :],
                                  in_=w[0:d // 2, :]), 16)
            g_dma.inc(g.dma_start(out=aga_in[j * d + d // 2:(j + 1) * d, :],
                                  in_=w[d // 2:d, :]), 16)
        g.wait_ge(g_dma.h, g_dma.c)
        cc_a.inc(g.collective_compute(
            "AllGather", ALU.bypass, replica_groups=[list(range(N_CORES))],
            ins=[aga_in.ap().opt()], outs=[aga_out.ap().opt()]))
        for j in range(NW1):
            g.wait_ge(v_mxr.h, j + 1)
            g_csc.inc(g.dma_start(out=rs_scr[j * 128:(j + 1) * 128],
                                  in_=csc[:, 2 + (j % 2):3 + (j % 2)]), 16)
            g_csc.inc(g.dma_start(out=ags_in[j * 128:(j + 1) * 128],
                                  in_=csc[:, 4 + (j % 2):5 + (j % 2)]), 16)
            g.wait_ge(g_csc.h, g_csc.c)
            if j >= 1:
                g.wait_ge(t_pb.h, j)
            g_rr1.inc(g.dma_start(out=rr1[:, :],
                                  in_=rs_scr[j * 128:(j + 1) * 128]), 16)
            g.wait_ge(v_wq1.h, j + 1)
            for q4 in range(4):
                fs = FC // 4
                g_w1.issue(g, j % 2,
                           agb1w[:, q4 * fs:(q4 + 1) * fs, j * 128:(j + 1) * 128],
                           w1q_bf.rearrange("p (b f w) -> p b f w", b=2, f=FC)
                           [:, j % 2, q4 * fs:(q4 + 1) * fs])
        g.wait_ge(g_dma.h, g_dma.c)
        g_w1.wait_all(g)
        cc_b1.inc(g.collective_compute(
            "AllGather", ALU.bypass, replica_groups=[list(range(N_CORES))],
            ins=[agb1_in.ap().opt()], outs=[agb1_out.ap().opt()]))
        for j in range(NW2):
            g.wait_ge(v_mxr.h, NW1 + j + 1)
            g_csc.inc(g.dma_start(out=rs_scr[j * 128:(j + 1) * 128],
                                  in_=csc[:, 2 + (j % 2):3 + (j % 2)]), 16)
            g_csc.inc(g.dma_start(out=ags_in[ISL + j * 128:ISL + (j + 1) * 128],
                                  in_=csc[:, 4 + (j % 2):5 + (j % 2)]), 16)
        g.wait_ge(g_csc.h, g_csc.c)
        g_rr2.inc(g.dma_start(out=rr2[:, :], in_=rs_scr[0:OSL]), 16)
        for j in range(N2C):
            g.wait_ge(v_wq2.h, j + 1)
            g_w2.issue(g, j % 2,
                       agb2w[j],
                       w2q_bf.rearrange("p (b f w) -> p b f w", b=2, f=W2C)[:, j % 2])
        g_w2.wait_all(g)
        cc_b2.inc(g.collective_compute(
            "AllGather", ALU.bypass, replica_groups=[list(range(N_CORES))],
            ins=[agb2_in.ap().opt()], outs=[agb2_out.ap().opt()]))
        cc_s.inc(g.collective_compute(
            "AllGather", ALU.bypass, replica_groups=[list(range(N_CORES))],
            ins=[ags_in.ap().opt()], outs=[ags_out.ap().opt()]))
        if STOP < 3:
            return
        g.wait_ge(v_lm.h, 2)
        g_dma.inc(g.dma_start(out=m1_in[0:128], in_=g1row_sb[:, :]), 16)
        g.wait_ge(g_dma.h, g_dma.c)
        cc_m1.inc(g.collective_compute(
            "AllReduce", ALU.max, replica_groups=[list(range(N_CORES))],
            ins=[m1_in.ap().opt()], outs=[m1_out.ap().opt()]))
        g.wait_ge(cc_m1.h, 1)
        g_gl1.inc(g.dma_start(out=sx1_col[:, :],
                              in_=m1_out.ap().rearrange("(p o) -> p o", o=1)), 16)
        if STOP < 4:
            pass
        st_k.wait(g, 0, 16 * NH)
        st_v.wait(g, 0, 32 * NH)
        cc_kv.inc(g.collective_compute(
            "AllGather", ALU.bypass,
            replica_groups=[list(range(G * GROUP, (G + 1) * GROUP))
                            for G in range(N_CORES // GROUP)],
            ins=[agc_in.ap().opt()], outs=[agc_out.ap().opt()]))
        g.wait_ge(v_lm.h, 4)
        g_dma.inc(g.dma_start(out=m2_in[0:128], in_=g2row_sb[:, :]), 16)
        g.wait_ge(g_dma.h, g_dma.c)
        cc_m2.inc(g.collective_compute(
            "AllReduce", ALU.max, replica_groups=[list(range(N_CORES))],
            ins=[m2_in.ap().opt()], outs=[m2_out.ap().opt()]))
        g.wait_ge(cc_m2.h, 1)
        g_gl2.inc(g.dma_start(out=sx2_col[:, :],
                              in_=m2_out.ap().rearrange("(p o) -> p o", o=1)), 16)

    @blk.scalar
    def _(scalar):
        if STOP < 1:
            return
        for tc in range(TC):
            scalar.wait_ge(v_ve.h, tc + 1)
            scalar.activation(ln_tmp[:, tc:tc + 1], ve_all[:, tc:tc + 1], AF.Ln)
            scalar.drain()
            s_ln.inc(scalar.activation(rstd_all[:, tc:tc + 1],
                                       ln_tmp[:, tc:tc + 1], AF.Exp, scale=np.float32(-0.5)))
            if tc >= 1:
                scalar.wait_ge(v_xnb.h, tc)
                st_xn.gate(scalar, (tc - 1) % 3)
                st_xn.issue(scalar, (tc - 1) % 3, xn_sp.ap()[tc - 1],
                            x_tiles[:, ((tc - 1) % 3) * d:((tc - 1) % 3) * d + d])
        scalar.wait_ge(v_xnb.h, TC)
        st_xn.gate(scalar, (TC - 1) % 3)
        st_xn.issue(scalar, (TC - 1) % 3, xn_sp.ap()[TC - 1],
                    x_tiles[:, ((TC - 1) % 3) * d:((TC - 1) % 3) * d + d])

    @blk.vector
    def _(vec):
        if STOP < 1:
            return
        def norm(tc):
            vec.wait_ge(s_ln.h, tc + 1)
            xt = x_tiles[:, (tc % 3) * d:(tc % 3) * d + d]
            vec.tensor_scalar(xt[:, :], xt[:, :], mv_all[:, 2 * tc:2 * tc + 1],
                              rstd_all[:, tc:tc + 1], ALU.subtract, ALU.mult)
            vec.drain()
            vec.tensor_reduce(lmax_sb[:, tc:tc + 1], xt[:, :], AX.X, ALU.max,
                              apply_absolute_value=True)
            v_xnb.inc(vec.drain())
        def xn_evac(jt):
            for half in range(2):
                vec.wait_ge(t_xtr.h, 2 * jt + half + 1)
                for jj in range(FH):
                    fc = half * FH + jj
                    vec.tensor_copy(xn_fm[:, fc * T + jt * 128: fc * T + (jt + 1) * 128],
                                    pT[:, jj * 128:(jj + 1) * 128])
                v_fme.inc(vec.drain())
        for tc in range(TC):
            ld_x.wait(vec, tc % 3, 16 * (tc // 3 + 1))
            xt = x_tiles[:, (tc % 3) * d:(tc % 3) * d + d]
            for j in range(n_bn):
                vec.bn_stats(bn_scr[:, 6 * j:6 * j + 6], xt[:, 512 * j:512 * (j + 1)])
            vec.drain()
            vec.bn_aggr(mv_all[:, 2 * tc:2 * tc + 2], bn_scr[:, :])
            vec.drain()
            vec.tensor_scalar(ve_all[:, tc:tc + 1], mv_all[:, 2 * tc + 1:2 * tc + 2],
                              1e-5, None, ALU.add)
            v_ve.inc(vec.drain())
            if tc >= 1:
                norm(tc - 1)
            if tc >= 2:
                xn_evac(tc - 2)
        norm(TC - 1)
        for jt in range(max(0, TC - 2), TC):
            xn_evac(jt)
        vec.tensor_reduce(csc[:, 0:1], lmax_sb[:, :], AX.X, ALU.max)
        v_lm.inc(vec.drain())

    @blk.tensor
    def _(tensor):
        if STOP < 1:
            return
        tensor.wait_ge(ld_aux.h, ld_aux.c)
        for tc in range(TC):
            tensor.wait_ge(v_xnb.h, tc + 1)
            for half in range(2):
                if 2 * tc + half >= 1:
                    tensor.wait_ge(v_fme.h, 2 * tc + half)
                for j in range(FH):
                    fc = half * FH + j
                    mm = tensor.transpose(
                        pT[:, j * 128:(j + 1) * 128],
                        x_tiles[:, (tc % 3) * d + fc * 128:(tc % 3) * d + fc * 128 + 128],
                        idf_sb[:, :])
                t_xtr.inc(mm)
        tensor.wait_ge(v_lm.h, 1)
        tensor.wait_ge(v_pp1.h, ppi["w"])
        t_lm.inc(tensor.matmul(pP1[0:1, :], csc[:, 0:1], idf_sb[:, :],
                               start=True, stop=True))
        ppi["w"] += 1

    @blk.vector
    def _(vec):
        if STOP < 1:
            return
        vec.wait_ge(t_lm.h, 1)
        vec.tensor_reduce(gm1_sb[:, :], pP1[0:1, :], AX.X, ALU.max)
        vec.drain()
        vec.tensor_scalar(g1row_sb[:, :], pP1[0:1, :], gm1_sb[0:1, 0:1],
                          None, ALU.max)
        vec.drain()
        v_pp1.inc(vec.drain())
        v_lm.inc(vec.drain())

    def _unit_load(sync, kind, j):
        if kind == "w1":
            ld_w1.gate(sync, j % 2)
            if j >= 2:
                sync.wait_ge(v_wq1.h, j - 1)
            for q4 in range(4):
                fs = FC // 4
                ld_w1.issue(sync, j % 2,
                            w1p.rearrange("p (b f w) -> p b f w", b=2, f=FC)
                            [:, j % 2, q4 * fs:(q4 + 1) * fs],
                            w1r[:, q4 * fs:(q4 + 1) * fs, j * 128:(j + 1) * 128])
        elif kind == "w2m":
            ld_w2.gate(sync, j % 2)
            if j >= 2:
                sync.wait_ge(v_w2m.h, j - 1)
            ld_w2.issue(sync, j % 2,
                        w2p.rearrange("p (b f w) -> p b f w", b=2, f=W2C)[:, j % 2],
                        w2r[:, j * W2C:(j + 1) * W2C, :])
        elif kind == "w2q":
            ld_w2.gate(sync, j % 2)
            if j == 0:
                sync.wait_ge(v_mx1.h, NW1 + 1)
            if j >= 2:
                sync.wait_ge(v_wq2.h, j - 1)
            ld_w2.issue(sync, j % 2,
                        w2p.rearrange("p (b f w) -> p b f w", b=2, f=W2C)[:, j % 2],
                        w2r[:, j * W2C:(j + 1) * W2C, :])

    @blk.sync
    def _(sync):
        if STOP < 2:
            return
        sync.wait_ge(cc_a.h, 1)
        for kind, j in per_head[0]:
            _unit_load(sync, kind, j)
        for h in range(NH):
            if h >= 2:
                sync.wait_ge(t_p.h, 3 * (h - 1))
            base = (h % 2) * 3 * FC * 128
            r = (h * 128) // OSL
            co = (h * 128) % OSL
            ld_wq.gate(sync, h % 2)
            fs = FC // 4
            for pj in range(3):
                dst = wsl3[:, base + pj * FC * 128: base + (pj + 1) * FC * 128]
                for q4 in range(4):
                    ld_wq.issue(sync, h % 2,
                                dst.rearrange("p (f w) -> p f w", f=FC)
                                [:, q4 * fs:(q4 + 1) * fs],
                                agar[r, pj, :, q4 * fs:(q4 + 1) * fs, co:co + 128])
            if h + 1 < NH:
                for kind, j in per_head[h + 1]:
                    _unit_load(sync, kind, j)

    @blk.tensor
    def _(tensor):
        if STOP < 2:
            return
        nmx = 0
        for h in range(NH):
            for kind, j in per_head[h]:
                if kind == "w1":
                    tensor.wait_ge(v_mx1.h, nmx + 1)
                    tensor.wait_ge(v_pp1.h, ppi["w"])
                    t_mx.inc(tensor.transpose(pP1[:, :], mxw[:, 0:128], idf_sb[:, :]))
                    ppi["w"] += 1
                    nmx += 1
                    tensor.wait_ge(g_rr1.h, 16 * (j + 1))
                    tensor.wait_ge(v_pp1.h, ppi["w"])
                    t_pb.inc(tensor.matmul(pP1[:, :], onesf_sb[:, :], rr1[:, :],
                                            start=True, stop=True))
                    ppi["w"] += 1
                elif kind == "w2s":
                    tensor.wait_ge(v_mx1.h, nmx + 1)
                    tensor.wait_ge(v_pp1.h, ppi["w"])
                    t_mx.inc(tensor.transpose(pP1[:, :], mxw[:, j * 128:(j + 1) * 128],
                                               idf_sb[:, :]))
                    ppi["w"] += 1
                    nmx += 1
                elif kind == "pb2":
                    tensor.wait_ge(g_rr2.h, 16)
                    tensor.wait_ge(v_pp1.h, ppi["w"])
                    t_pb.inc(tensor.matmul(pP1[:, :], onesf_sb[:, :],
                                            rr2[:, j * 128:(j + 1) * 128],
                                            start=True, stop=True))
                    ppi["w"] += 1
            base = (h % 2) * 3 * FC * 128
            ld_wq.wait(tensor, h % 2, 192 * (h // 2 + 1))
            if h == 0:
                tensor.wait_ge(v_fme.h, 2 * TC)
            tensor.wait_ge(s_qb.h, 3 * h)
            for fc in range(FC):
                wt = wsl3[:, base + fc * 128: base + fc * 128 + 128]
                for (o, w) in PC:
                    mm = tensor.matmul(pX[:, o:o + w], wt,
                                       xn_fm[:, fc * T + o: fc * T + o + w],
                                       start=(fc == 0), stop=(fc == FC - 1))
            t_p.inc(mm)
            tensor.wait_ge(s_qb.h, 3 * h + 1)
            if h > 0:
                tensor.wait_ge(v_rope.h, 2 * h)
            for (o, w) in PC:
                mm = tensor.matmul(pY[:, o:o + w], rT_sb[:, :], qb_sb[:, o:o + w],
                                   start=True, stop=True)
            t_r.inc(mm)
            for fc in range(FC):
                wt = wsl3[:, base + FC * 128 + fc * 128: base + FC * 128 + fc * 128 + 128]
                for (o, w) in PC:
                    mm = tensor.matmul(pX[:, o:o + w], wt,
                                       xn_fm[:, fc * T + o: fc * T + o + w],
                                       start=(fc == 0), stop=(fc == FC - 1))
            t_p.inc(mm)
            tensor.wait_ge(s_qb.h, 3 * h + 2)
            tensor.wait_ge(v_rope.h, 2 * h + 1)
            for (o, w) in PC:
                mm = tensor.matmul(pY[:, o:o + w], rT_sb[:, :], qb_sb[:, o:o + w],
                                   start=True, stop=True)
            t_r.inc(mm)
            for fc in range(FC):
                wt = wsl3[:, base + 2 * FC * 128 + fc * 128: base + 2 * FC * 128 + fc * 128 + 128]
                for (o, w) in PC:
                    mm = tensor.matmul(pX[:, o:o + w], wt,
                                       xn_fm[:, fc * T + o: fc * T + o + w],
                                       start=(fc == 0), stop=(fc == FC - 1))
            t_p.inc(mm)
            tensor.wait_ge(s_qb.h, 3 * h + 3)
            tensor.wait_ge(v_vte.h, h)
            if h == 0:
                tensor.wait_ge(v_fme.h, 2 * TC)
            for tc in range(TC):
                mm = tensor.transpose(pT[:, tc * 128:(tc + 1) * 128],
                                      vb_sb[:, tc * 128:(tc + 1) * 128], idf_sb[:, :])
            t_vt.inc(mm)

    @blk.scalar
    def _(scalar):
        if STOP < 2:
            return
        scalar.wait_ge(ld_aux.h, ld_aux.c)
        for h in range(NH):
            scalar.wait_ge(t_p.h, 3 * h + 1)
            if h > 0:
                scalar.wait_ge(v_rope.h, 2 * h)
            s_qb.inc(scalar.activation(qb_sb[:, :], pX[:, :], AF.Identity,
                                       bias=bq_sb[:, h:h + 1]))
            scalar.wait_ge(t_p.h, 3 * h + 2)
            scalar.wait_ge(v_rope.h, 2 * h + 1)
            s_qb.inc(scalar.activation(qb_sb[:, :], pX[:, :], AF.Identity,
                                       bias=bk_sb[:, h:h + 1]))
            scalar.wait_ge(t_p.h, 3 * h + 3)
            if h > 0:
                scalar.wait_ge(t_vt.h, h)
            s_qb.inc(scalar.activation(vb_sb[:, :], pX[:, :], AF.Identity,
                                       bias=bv_sb[:, h:h + 1]))
            scalar.wait_ge(v_rope.h, 2 * h + 2)
            st_k.gate(scalar, 0)
            st_k.issue(scalar, 0,
                       agc_in.ap()[KOFF + h * 128 * T: KOFF + (h + 1) * 128 * T]
                       .rearrange("(p t) -> p t", p=128),
                       kr_sb[:, :])
            scalar.wait_ge(v_vte.h, h + 1)
            st_v.gate(scalar, 0)
            hh = TC // 2
            for h2 in range(2):
                st_v.issue(scalar, 0,
                           agc_in.ap()[VOFF + h * T * 128: VOFF + (h + 1) * T * 128]
                           .rearrange("(tc p dc) -> p tc dc", p=128, dc=128)
                           [:, h2 * hh:(h2 + 1) * hh],
                           vtm_sb.ap().rearrange("p (tc dc) -> p tc dc", dc=128)
                           [:, h2 * hh:(h2 + 1) * hh])

    @blk.vector
    def _(vec):
        if STOP < 2:
            return
        vec.wait_ge(ld_aux.h, ld_aux.c)
        nmx_v = 0
        w2fold = {"n": 0}
        for h in range(NH):
            for kind, j in per_head[h]:
                if kind == "w1":
                    ld_w1.wait(vec, j % 2, 64 * (j // 2 + 1))
                    src = w1p.rearrange("p (b f w) -> p b f w", b=2, f=FC)[:, j % 2]
                    vec.tensor_tensor(mxw[:, 0:128], src[:, 0], src[:, 1], ALU.max)
                    vec.tensor_tensor(mnw[:, 0:128], src[:, 0], src[:, 1], ALU.min)
                    for fc in range(2, FC):
                        vec.drain()
                        vec.tensor_tensor(mxw[:, 0:128], mxw[:, 0:128], src[:, fc],
                                          ALU.max)
                        vec.tensor_tensor(mnw[:, 0:128], mnw[:, 0:128], src[:, fc],
                                          ALU.min)
                    vec.drain()
                    vec.tensor_scalar(mnw[:, 0:128], mnw[:, 0:128], -1.0, None, ALU.mult)
                    vec.drain()
                    vec.tensor_tensor(mxw[:, 0:128], mxw[:, 0:128], mnw[:, 0:128],
                                      ALU.max)
                    v_mx1.inc(vec.drain())
                    nmx_v += 1
                    vec.wait_ge(t_mx.h, nmx_v)
                    vec.tensor_reduce(csc[:, 1:2], pP1[:, :], AX.X, ALU.max)
                    v_pp1.inc(vec.drain())
                    if j >= 2:
                        vec.wait_ge(g_csc.h, 32 * (j - 1))
                    vec.tensor_scalar(csc[:, 4 + (j % 2):5 + (j % 2)], csc[:, 1:2],
                                      1.0 / 127.0, 1e-8, ALU.mult, ALU.max)
                    vec.drain()
                    vec.reciprocal(csc[:, 2 + (j % 2):3 + (j % 2)],
                                   csc[:, 4 + (j % 2):5 + (j % 2)])
                    v_mxr.inc(vec.drain())
                    vec.wait_ge(t_pb.h, j + 1)
                    vec.tensor_copy(rep1[:, :], pP1[:, :])
                    vec.drain()
                    v_pp1.inc(vec.drain())
                    if j >= 2:
                        g_w1.wait(vec, j % 2, 64 * (j // 2))
                    dstv = w1q_bf.rearrange("p (b f w) -> p b f w", b=2, f=FC)[:, j % 2]
                    for fc in range(FC):
                        vec.tensor_tensor(mxq[:, 0:128], src[:, fc], rep1[:, :], ALU.mult)
                        vec.drain()
                        vec.tensor_scalar(mxq[:, 0:128], mxq[:, 0:128],
                                          RND_C, RND_C, ALU.add, ALU.subtract)
                        vec.drain()
                        vec.tensor_scalar(dstv[:, fc], mxq[:, 0:128],
                                          127.0, -127.0, ALU.min, ALU.max)
                        vec.drain()
                    v_wq1.inc(vec.drain())
                elif kind == "w2m":
                    ld_w2.wait(vec, j % 2, 16 * (j // 2 + 1))
                    if j == 0 and NW1 > 0:
                        vec.wait_ge(t_mx.h, NW1)
                    src = w2p.rearrange("p (b f w) -> p b f w", b=2, f=W2C)[:, j % 2]
                    for f in range(W2C):
                        if j == 0 and f == 0:
                            vec.tensor_tensor(mxw[:, 0:OSL], src[:, 0],
                                              src[:, min(1, W2C - 1)], ALU.max)
                            vec.tensor_tensor(mnw[:, 0:OSL], src[:, 0],
                                              src[:, min(1, W2C - 1)], ALU.min)
                            vec.drain()
                            continue
                        if j == 0 and f == 1 and W2C > 1:
                            continue
                        vec.tensor_tensor(mxw[:, 0:OSL], mxw[:, 0:OSL], src[:, f],
                                          ALU.max)
                        vec.tensor_tensor(mnw[:, 0:OSL], mnw[:, 0:OSL], src[:, f],
                                          ALU.min)
                        vec.drain()
                    v_w2m.inc(vec.drain())
                    w2fold["n"] += 1
                    if w2fold["n"] == N2C:
                        vec.tensor_scalar(mnw[:, 0:OSL], mnw[:, 0:OSL], -1.0,
                                          None, ALU.mult)
                        vec.drain()
                        vec.tensor_tensor(mxw[:, 0:OSL], mxw[:, 0:OSL],
                                          mnw[:, 0:OSL], ALU.max)
                        v_mx1.inc(vec.drain())
                elif kind == "w2s":
                    if j > 0:
                        v_mx1.inc(vec.drain())
                    nmx_v += 1
                    vec.wait_ge(t_mx.h, nmx_v)
                    vec.tensor_reduce(csc[:, 1:2], pP1[:, :], AX.X, ALU.max)
                    v_pp1.inc(vec.drain())
                    vec.wait_ge(g_csc.h, 32 * NW1)
                    vec.tensor_scalar(csc[:, 4 + (j % 2):5 + (j % 2)], csc[:, 1:2],
                                      1.0 / 127.0, 1e-8, ALU.mult, ALU.max)
                    vec.drain()
                    vec.reciprocal(csc[:, 2 + (j % 2):3 + (j % 2)],
                                   csc[:, 4 + (j % 2):5 + (j % 2)])
                    v_mxr.inc(vec.drain())
                elif kind == "pb2":
                    vec.wait_ge(t_pb.h, NW1 + j + 1)
                    vec.tensor_copy(rep2[:, j * 128:(j + 1) * 128], pP1[:, :])
                    vec.drain()
                    v_pp1.inc(vec.drain())
                else:
                    ld_w2.wait(vec, j % 2, 16 * (N2C // 2 + j // 2 + 1))
                    if j >= 2:
                        g_w2.wait(vec, j % 2, 16 * (j // 2))
                    src = w2p.rearrange("p (b f w) -> p b f w", b=2, f=W2C)[:, j % 2]
                    dstv = w2q_bf.rearrange("p (b f w) -> p b f w", b=2, f=W2C)[:, j % 2]
                    for f in range(W2C):
                        vec.tensor_tensor(mxq[:, 0:OSL], src[:, f], rep2[:, :], ALU.mult)
                        vec.drain()
                        vec.tensor_scalar(mxq[:, 0:OSL], mxq[:, 0:OSL],
                                          RND_C, RND_C, ALU.add, ALU.subtract)
                        vec.drain()
                        vec.tensor_scalar(dstv[:, f], mxq[:, 0:OSL],
                                          127.0, -127.0, ALU.min, ALU.max)
                        vec.drain()
                    v_wq2.inc(vec.drain())
            vec.wait_ge(s_qb.h, 3 * h + 1)
            vec.wait_ge(t_r.h, 2 * h + 1)
            vec.tensor_tensor(m1r_sb[:, :], qb_sb[:, :], cos_sb[:, :], ALU.mult)
            vec.tensor_tensor(m2r_sb[:, :], pY[:, :], sin_sb[:, :], ALU.mult)
            vec.drain()
            vec.tensor_tensor(q_rope[:, h * T:(h + 1) * T], m1r_sb[:, :],
                              m2r_sb[:, :], ALU.add)
            v_rope.inc(vec.drain())
            vec.wait_ge(s_qb.h, 3 * h + 2)
            vec.wait_ge(t_r.h, 2 * h + 2)
            if h >= 1:
                st_k.wait(vec, 0, 16 * h)
            vec.tensor_tensor(m1r_sb[:, :], qb_sb[:, :], cos_sb[:, :], ALU.mult)
            vec.tensor_tensor(m2r_sb[:, :], pY[:, :], sin_sb[:, :], ALU.mult)
            vec.drain()
            vec.tensor_tensor(kr_sb[:, :], m1r_sb[:, :], m2r_sb[:, :], ALU.add)
            v_rope.inc(vec.drain())
            vec.wait_ge(t_vt.h, h + 1)
            if h >= 1:
                st_v.wait(vec, 0, 32 * h)
            vec.tensor_copy(vtm_sb[:, :], pT[:, 0:T])
            v_vte.inc(vec.drain())

    esA1b.close()
    esP2.close()
    esP3 = contextlib.ExitStack()
    q1slab = esP3.enter_context(nc.sbuf_tensor("q1slab", [128, 4 * HSL], BF16))
    h_sb = esP3.enter_context(nc.sbuf_tensor("h_sb", [128, 2 * T], F32))

    K1 = ISL // 128
    K2 = OSL // 128

    @blk.sync
    def _(sync):
        if STOP < 3:
            return
        sync.wait_ge(cc_s.h, 1)
        for r in range(2 * N_CORES):
            if r >= 1:
                sync.wait_ge(t_sw.h, r)
            if r < N_CORES:
                src = ags_out.ap()[r * ssz: r * ssz + ISL].rearrange("(k p) -> k p", p=128)
                ld_sw.inc(sync.dma_start(out=swsl[0:K1, :], in_=src), 16)
            else:
                rr = r - N_CORES
                src = ags_out.ap()[rr * ssz + ISL: (rr + 1) * ssz].rearrange("(k p) -> k p", p=128)
                ld_sw.inc(sync.dma_start(out=swsl[0:K2, :], in_=src), 16)

    @blk.tensor
    def _(tensor):
        if STOP < 3:
            return
        for r in range(2 * N_CORES):
            k = K1 if r < N_CORES else K2
            tensor.wait_ge(ld_sw.h, 16 * (r + 1))
            tensor.wait_ge(v_pp1.h, ppi["w"])
            t_sw.inc(tensor.transpose(pP1[:, 0:k], swsl[0:k, :], idf_sb[0:k, 0:k]))
            ppi["w"] += 1

    @blk.vector
    def _(vec):
        if STOP < 3:
            return
        for r in range(2 * N_CORES):
            vec.wait_ge(t_sw.h, r + 1)
            if r < N_CORES:
                vec.tensor_copy(sw1_col[:, r * K1:(r + 1) * K1], pP1[:, 0:K1])
            else:
                rr = r - N_CORES
                vec.tensor_copy(sw2_col[:, rr * K2:(rr + 1) * K2], pP1[:, 0:K2])
            vec.drain()
            v_swc.inc(vec.drain())
            v_pp1.inc(vec.drain())
        vec.wait_ge(g_gl1.h, 16)
        vec.tensor_scalar(csc[:, 6:7], sx1_col[:, :], 1.0 / 127.0, 1e-8,
                          ALU.mult, ALU.max)
        vec.drain()
        vec.reciprocal(r1_col[:, :], csc[:, 6:7])
        vec.drain()
        vec.tensor_scalar(s1_col[:, :], sw1_col[:, :], csc[:, 6:7], None, ALU.mult)
        v_sx.inc(vec.drain())

    @blk.sync
    def _(sync):
        if STOP < 3:
            return
        st_xn.wait_all(sync)
        sync.wait_ge(t_xtr.h, 2 * TC)
        for tc in range(TC):
            ld_xq.gate(sync, tc % 3)
            if tc >= 3:
                sync.wait_ge(t_xtr.h, 2 * TC + 2 * (tc - 2))
            ld_xq.issue(sync, tc % 3,
                        x_tiles[:, (tc % 3) * d:(tc % 3) * d + d],
                        xn_sp.ap()[tc])

    @blk.vector
    def _(vec):
        if STOP < 3:
            return
        def qx_evac(jt):
            for half in range(2):
                vec.wait_ge(t_xtr.h, 2 * TC + 2 * jt + half + 1)
                for jj in range(FH):
                    fc = half * FH + jj
                    vec.tensor_copy(xn_fm[:, fc * T + jt * 128: fc * T + (jt + 1) * 128],
                                    pT[:, jj * 128:(jj + 1) * 128])
                v_fme.inc(vec.drain())
        vec.wait_ge(t_p.h, 3 * NH)
        for tc in range(TC):
            ld_xq.wait(vec, tc % 3, 16 * (tc // 3 + 1))
            xo = x_tiles[:, (tc % 3) * d:(tc % 3) * d + d]
            vec.tensor_scalar(xo[:, :], xo[:, :], r1_col[:, :], RND_C,
                              ALU.mult, ALU.add)
            vec.drain()
            vec.tensor_scalar(xo[:, :], xo[:, :], RND_C, 127.0,
                              ALU.subtract, ALU.min)
            vec.drain()
            vec.tensor_scalar(xo[:, :], xo[:, :], -127.0, None, ALU.max)
            v_qx.inc(vec.drain())
            if tc >= 1:
                qx_evac(tc - 1)
        qx_evac(TC - 1)

    @blk.tensor
    def _(tensor):
        if STOP < 3:
            return
        tensor.wait_ge(v_vte.h, NH)
        for tc in range(TC):
            tensor.wait_ge(v_qx.h, tc + 1)
            for half in range(2):
                tensor.wait_ge(v_fme.h, 2 * TC + 2 * tc + half)
                for j in range(FH):
                    fc = half * FH + j
                    mm = tensor.transpose(
                        pT[:, j * 128:(j + 1) * 128],
                        x_tiles[:, (tc % 3) * d + fc * 128:(tc % 3) * d + fc * 128 + 128],
                        idf_sb[:, :])
                t_xtr.inc(mm)

    @blk.sync
    def _(sync):
        if STOP < 3:
            return
        sync.wait_ge(cc_b1.h, 1)
        sync.wait_ge(v_wq2.h, N2C)
        sync.wait_ge(v_rope.h, 2 * NH)
        sync.wait_ge(v_vte.h, NH)
        st_k.wait(sync, 0, 16 * NH)
        st_v.wait(sync, 0, 32 * NH)
        for hf in range(2 * N_CORES):
            r, hh = hf // 2, hf % 2
            ld_q1.gate(sync, hf % 4)
            if hf >= 4:
                sync.wait_ge(t_f1.h, min(((hf - 4) // 2 + 1) * NW1, IC))
            ld_q1.issue(sync, hf % 4,
                        q1slab[:, (hf % 4) * HSL:(hf % 4 + 1) * HSL]
                        .rearrange("p (f i) -> p f i", f=FC // 2),
                        agb1r[r, :, hh * (FC // 2):(hh + 1) * (FC // 2), :])

    @blk.tensor
    def _(tensor):
        if STOP < 3:
            return
        tensor.wait_ge(v_fme.h, 4 * TC)
        tensor.wait_ge(s_qb.h, 3 * NH)
        tensor.wait_ge(v_rope.h, 2 * NH)
        for ic in range(IC):
            r = ic // NW1
            for hf in (2 * r, 2 * r + 1):
                ld_q1.wait(tensor, hf % 4, 16 * (hf // 4 + 1))
            if ic >= 2:
                tensor.wait_ge(s_g.h, ic - 1)
            pp = pX if ic % 2 == 0 else pY
            for fc in range(FC):
                slot = (2 * r + fc // (FC // 2)) % 4
                off = slot * HSL + (fc % (FC // 2)) * ISL + (ic % NW1) * 128
                wt = q1slab[:, off:off + 128]
                for (o, w) in PC:
                    mm = tensor.matmul(pp[:, o:o + w], wt,
                                       xn_fm[:, fc * T + o: fc * T + o + w],
                                       start=(fc == 0), stop=(fc == FC - 1))
            t_f1.inc(mm)

    @blk.scalar
    def _(scalar):
        if STOP < 3:
            return
        for ic in range(IC):
            scalar.wait_ge(t_f1.h, ic + 1)
            if ic == 0:
                scalar.wait_ge(v_sx.h, 1)
            if ic >= 2:
                scalar.wait_ge(v_hx.h, ic - 1)
            st_h.gate(scalar, ic % 2)
            pp = pX if ic % 2 == 0 else pY
            scalar.activation(h_sb[:, (ic % 2) * T:(ic % 2) * T + T], pp[:, :],
                              AF.Gelu, bias=b1_sb[:, ic:ic + 1],
                              scale=s1_col[:, ic:ic + 1])
            s_g.inc(scalar.drain())
            st_h.issue(scalar, ic % 2, h_sp.ap()[ic],
                       h_sb[:, (ic % 2) * T:(ic % 2) * T + T])

    @blk.vector
    def _(vec):
        if STOP < 3:
            return
        for ic in range(IC):
            vec.wait_ge(s_g.h, ic + 1)
            vec.tensor_reduce(hmax_sb[:, ic:ic + 1],
                              h_sb[:, (ic % 2) * T:(ic % 2) * T + T], AX.X,
                              ALU.max, apply_absolute_value=True)
            v_hx.inc(vec.drain())
        vec.wait_ge(t_lm.h, 1)
        vec.tensor_reduce(csc[:, 0:1], hmax_sb[:, :], AX.X, ALU.max)
        v_lm.inc(vec.drain())

    @blk.tensor
    def _(tensor):
        if STOP < 3:
            return
        tensor.wait_ge(v_lm.h, 3)
        tensor.wait_ge(v_pp1.h, ppi["w"])
        t_lm.inc(tensor.matmul(pP1[0:1, :], csc[:, 0:1], idf_sb[:, :],
                               start=True, stop=True))
        ppi["w"] += 1

    @blk.vector
    def _(vec):
        if STOP < 3:
            return
        vec.wait_ge(t_lm.h, 2)
        vec.tensor_reduce(gm2_sb[:, :], pP1[0:1, :], AX.X, ALU.max)
        vec.drain()
        vec.tensor_scalar(g2row_sb[:, :], pP1[0:1, :], gm2_sb[0:1, 0:1],
                          None, ALU.max)
        vec.drain()
        v_pp1.inc(vec.drain())
        v_lm.inc(vec.drain())

    esT.close()
    esP3.close()
    esA1a.close()
    esP4 = contextlib.ExitStack()
    P4 = lambda n, s, dt: esP4.enter_context(nc.sbuf_tensor(n, s, dt))
    k_sb = P4("k_sb", [128, 2 * S], BF16)
    v_sb = P4("v_sb", [128, 2 * S], BF16)
    exp_sb = P4("exp_sb", [128, 4 * T], BF16)
    recip_sb = P4("recip_sb", [128, T], F32)
    acc2 = P4("acc2", [128, 2 * T], F32)
    hrd_sb = P4("hrd_sb", [128, 2 * T], F32) if QA else None
    es_qha = contextlib.ExitStack()
    qh_a = es_qha.enter_context(
        nc.sbuf_tensor("qh_a", [128, QA * T], BF16, side="right")) if QA else None
    es_ao = contextlib.ExitStack()
    attn_out = es_ao.enter_context(
        nc.sbuf_tensor("attn_out", [128, NH * T], BF16, side="right"))

    _qh_b_ref = []

    def qh_sl(ic, o, w):
        if ic < QA:
            return qh_a[:, ic * T + o: ic * T + o + w]
        qb = _qh_b_ref[0]
        return qb[:, (ic - QA) * T + o: (ic - QA) * T + o + w]
    esPO = contextlib.ExitStack()
    pO = esPO.enter_context(nc.psum_tensor("pO", [128, T], F32))
    pU = [esPO.enter_context(nc.psum_tensor(f"pU{i}", [128, PC[0][1]], F32))
          for i in range(NP)]

    agcr = agc_out.ap().rearrange("(r e) -> r e", r=GROUP)
    accb = v_acc.c
    p1b = s_p1.c

    @blk.sync
    def _(sync):
        if STOP < 4:
            return
        sync.wait_ge(cc_kv.h, 1)
        sync.wait_ge(t_f1.h, IC)
        st_h.wait_all(sync)
        sync.wait_ge(v_hx.h, IC)
        for h in range(NH):
            slot = h % 2
            ld_kv.gate(sync, slot)
            if h >= 2:
                sync.wait_ge(t_O.h, h - 1)
            for r in range(GROUP):
                src = agcr[r, KOFF + h * 128 * T: KOFF + (h + 1) * 128 * T] \
                    .rearrange("(p t) -> p t", p=128)
                ld_kv.issue(sync, slot,
                            k_sb[:, slot * S + r * T: slot * S + (r + 1) * T], src)
            for r in range(GROUP):
                src = agcr[r, VOFF + h * T * 128: VOFF + (h + 1) * T * 128] \
                    .rearrange("(tc p dc) -> p tc dc", p=128, dc=128)
                dstv = v_sb[:, slot * S + r * T: slot * S + (r + 1) * T] \
                    .rearrange("p (tc dc) -> p tc dc", dc=128)
                hh = TC // 2
                for h2 in range(2):
                    ld_kv.issue(sync, slot,
                                dstv[:, h2 * hh:(h2 + 1) * hh],
                                src[:, h2 * hh:(h2 + 1) * hh])

    @blk.tensor
    def _(tensor):
        if STOP < 4:
            return
        pS = [pX, pY]
        flat = [(h, kc) for h in range(NH) for kc in range(KC)]

        def emit_scores(h, kc):
            idx = h * KC + kc
            if kc == 0:
                ld_kv.wait(tensor, h % 2, 16 * 3 * GROUP * (h // 2 + 1))
            if idx == 0:
                tensor.wait_ge(s_g.h, IC)
                tensor.wait_ge(v_pp1.h, ppi["w"])
                tensor.wait_ge(v_fme.h, 4 * TC)
            if idx >= 2:
                tensor.wait_ge(s_exp.h, idx - 1)
            kt = k_sb[:, (h % 2) * S + kc * 128: (h % 2) * S + (kc + 1) * 128]
            for (o, w) in PC:
                mm = tensor.matmul(pS[idx % 2][:, o:o + w], kt,
                                   q_rope[:, h * T + o: h * T + o + w],
                                   start=True, stop=True)
            t_S.inc(mm)

        def emit_av(h, kc):
            idx = h * KC + kc
            tensor.wait_ge(s_exp.h, idx + 1)
            if kc == 0:
                tensor.wait_ge(v_att.h, h)
                tensor.wait_ge(s_rec.h, h)
            vt = v_sb[:, (h % 2) * S + kc * 128: (h % 2) * S + (kc + 1) * 128]
            ex = exp_sb[:, (idx % 4) * T:(idx % 4) * T + T]
            for (o, w) in PC:
                mm = tensor.matmul(pO[:, o:o + w], vt, ex[:, o:o + w],
                                   start=(kc == 0), stop=(kc == KC - 1))
            t_A.inc(mm)
            if kc == KC - 1:
                tensor.wait_ge(v_acc.h, accb + (h + 1) * KC)
                for pi, (o, w) in enumerate(PC):
                    tensor.matmul(pU[pi][:, :], onesf32_sb[:, :],
                                  acc2[:, (h % 2) * T + o:(h % 2) * T + o + w],
                                  start=True, stop=True)
                t_O.inc(tensor.drain())

        emit_scores(*flat[0])
        emit_scores(*flat[1])
        for i, (h, kc) in enumerate(flat):
            if i + 2 < len(flat):
                emit_scores(*flat[i + 2])
            emit_av(h, kc)

    @blk.scalar
    def _(scalar):
        if STOP < 4:
            return
        st_h.wait_all(scalar)
        for h in range(NH):
            for kc in range(KC):
                scalar.wait_ge(t_S.h, h * KC + kc + 1)
                if h * KC + kc >= 4:
                    scalar.wait_ge(t_A.h, h * KC + kc - 3)
                    scalar.wait_ge(v_acc.h, accb + h * KC + kc - 3)
                idx = h * KC + kc
                s_exp.inc(scalar.activation(
                    exp_sb[:, (idx % 4) * T:(idx % 4) * T + T],
                    (pX if idx % 2 == 0 else pY)[:, :],
                    AF.Exp, scale=EXPS, bias=EXPB))
            scalar.wait_ge(t_O.h, h + 1)
            if h > 0:
                scalar.wait_ge(v_att.h, h)
            for pi, (o, w) in enumerate(PC):
                scalar.activation(recip_sb[:, o:o + w], pU[pi][:, :], AF.Ln)
            scalar.drain()
            s_rec.inc(scalar.activation(recip_sb[:, :], recip_sb[:, :],
                                        AF.Exp, scale=np.float32(-1.0)))

    @blk.vector
    def _(vec):
        if STOP < 4:
            return
        for h in range(NH):
            acc = acc2[:, (h % 2) * T:(h % 2) * T + T]
            for kc in range(KC):
                idx = h * KC + kc
                vec.wait_ge(s_exp.h, idx + 1)
                ex = exp_sb[:, (idx % 4) * T:(idx % 4) * T + T]
                if kc == 0:
                    vec.tensor_scalar(acc[:, :], ex[:, :], 0.0,
                                      None, ALU.add)
                else:
                    vec.tensor_tensor(acc[:, :], acc[:, :], ex[:, :], ALU.add)
                v_acc.inc(vec.drain())
            vec.wait_ge(s_rec.h, h + 1)
            vec.tensor_tensor(attn_out[:, h * T:(h + 1) * T], pO[:, :],
                              recip_sb[:, :], ALU.mult)
            v_att.inc(vec.drain())

    esP4.close()
    esA2.close()
    esQR.close()
    esPO.close()
    esXY.close()
    esQH = contextlib.ExitStack()
    qh_b = esQH.enter_context(nc.sbuf_tensor("qh_b", [128, max(QB, 1) * T], BF16))
    _qh_b_ref.append(qh_b)
    hrd2_sb = esQH.enter_context(nc.sbuf_tensor("hrd2_sb", [128, 2 * T], F32))
    h2q_sb = esQH.enter_context(nc.sbuf_tensor("h2q_sb", [128, 2 * T], F32))

    esP5 = contextlib.ExitStack()
    P5 = lambda n, s, dt: esP5.enter_context(nc.sbuf_tensor(n, s, dt))
    wo_sb = P5("wo_sb", [128, 2 * FC * 128], BF16)
    asp_sb = P5("asp_sb", [128, 2 * T], F32)
    esW = contextlib.ExitStack()
    pW0 = esW.enter_context(nc.psum_tensor("pW0", [128, T], F32))
    pW1 = esW.enter_context(nc.psum_tensor("pW1", [128, T], F32))
    pF = esW.enter_context(nc.psum_tensor("pF", [128, 1024], F32))

    def pF_sl(s):
        c = (s % 2) * 512 + (s // 2) * 128
        return pF[:, c:c + 128]

    @blk.sync
    def _(sync):
        if STOP < 5:
            return
        sync.wait_ge(t_O.h, NH)
        st_h.wait_all(sync)
        for oc in range(FC):
            ld_wo.gate(sync, oc % 2)
            if oc >= 2:
                sync.wait_ge(t_wo.h, oc - 1)
            r = (oc * 128) // OSL
            co = (oc * 128) % OSL
            dst = wo_sb[:, (oc % 2) * FC * 128:(oc % 2 + 1) * FC * 128]
            fs = FC // 4
            for q4 in range(4):
                ld_wo.issue(sync, oc % 2,
                            dst.rearrange("p (f w) -> p f w", f=FC)
                            [:, q4 * fs:(q4 + 1) * fs],
                            agar[r, 3, :, q4 * fs:(q4 + 1) * fs, co:co + 128])


    @blk.gpsimd
    def _(g):
        if STOP < 5:
            return
        g.wait_ge(t_O.h, NH)
        g.wait_ge(s_rec.h, NH)
        g.wait_ge(v_att.h, NH)
        st_h.wait_all(g)
        for ic in range(IC):
            ld_h.gate(g, ic % 2)
            if ic >= 2:
                g.wait_ge(s_p1.h, p1b + ic - 1)
            ld_h.issue(g, ic % 2,
                       hrd2_sb[:, (ic % 2) * T:(ic % 2) * T + T],
                       h_sp.ap()[ic])

    @blk.tensor
    def _(tensor):
        if STOP < 5:
            return
        for oc in range(FC):
            ld_wo.wait(tensor, oc % 2, 64 * (oc // 2 + 1))
            if oc == 0:
                tensor.wait_ge(v_att.h, NH)
                tensor.wait_ge(s_rec.h, NH)
                tensor.wait_ge(s_exp.h, NH * KC)
            if oc >= 2:
                tensor.wait_ge(s_wo.h, oc - 1)
            pp = pW0 if oc % 2 == 0 else pW1
            for fc in range(FC):
                wt = wo_sb[:, (oc % 2) * FC * 128 + fc * 128:(oc % 2) * FC * 128 + fc * 128 + 128]
                for (o, w) in PC:
                    mm = tensor.matmul(pp[:, o:o + w], wt,
                                       attn_out[:, fc * T + o: fc * T + o + w],
                                       start=(fc == 0), stop=(fc == FC - 1))
            t_wo.inc(mm)

    @blk.scalar
    def _(scalar):
        if STOP < 5:
            return
        npo = IC // FC
        for oc in range(FC):
            scalar.wait_ge(t_wo.h, oc + 1)
            st_w.gate(scalar, oc % 2)
            scalar.activation(asp_sb[:, (oc % 2) * T:(oc % 2) * T + T],
                              (pW0 if oc % 2 == 0 else pW1)[:, :], AF.Copy)
            s_wo.inc(scalar.drain())
            st_w.issue(scalar, oc % 2, attn_sp.ap()[oc],
                       asp_sb[:, (oc % 2) * T:(oc % 2) * T + T])
            for ic in range(oc * npo, (oc + 1) * npo):
                if ic == 0:
                    scalar.wait_ge(v_sx.h, 2)
                if ic >= 2:
                    scalar.wait_ge(v_qh.h, ic - 1)
                ld_h.wait(scalar, ic % 2, 16 * (ic // 2 + 1))
                s_p1.inc(scalar.activation(
                    h2q_sb[:, (ic % 2) * T:(ic % 2) * T + T],
                    hrd2_sb[:, (ic % 2) * T:(ic % 2) * T + T],
                    AF.Identity, bias=np.float32(RND_C), scale=r2_col[:, 0:1]))

    @blk.vector
    def _(vec):
        if STOP < 5:
            return
        vec.wait_ge(g_gl2.h, 16)
        vec.tensor_scalar(csc[:, 7:8], sx2_col[:, :], 1.0 / 127.0, 1e-8,
                          ALU.mult, ALU.max)
        vec.drain()
        vec.reciprocal(r2_col[:, :], csc[:, 7:8])
        vec.drain()
        vec.tensor_scalar(s2_col[:, :], sw2_col[:, :], csc[:, 7:8], None, ALU.mult)
        v_sx.inc(vec.drain())
        for ic in range(IC):
            vec.wait_ge(s_p1.h, p1b + ic + 1)
            hs = h2q_sb[:, (ic % 2) * T:(ic % 2) * T + T]
            vec.tensor_scalar(hs[:, :], hs[:, :], RND_C, 127.0,
                              ALU.subtract, ALU.min)
            vec.drain()
            vec.tensor_scalar(qh_sl(ic, 0, T), hs[:, :], -127.0, None, ALU.max)
            v_qh.inc(vec.drain())

    es_ao.close()
    esP5.close()
    esP6 = contextlib.ExitStack()
    P6 = lambda n, s, dt: esP6.enter_context(nc.sbuf_tensor(n, s, dt))
    q2_sb = P6("q2_sb", [128, 2 * IC * 128], BF16)
    f2_sb = P6("f2_sb", [128, 2 * T], F32)
    asp2_sb = P6("asp2_sb", [128, T], F32)
    fin_sb = P6("fin_sb", [128, T], F32)
    xcol_sb = P6("xcol_sb", [128, 2 * T], F32)
    outt_sb = P6("outt_sb", [128, 4 * 128], F32)

    @blk.sync
    def _(sync):
        if STOP < 6:
            return
        sync.wait_ge(s_wo.h, FC)
        st_w.wait_all(sync)
        for oc in range(FC):
            ld_q2.gate(sync, oc % 2)
            if oc >= 2:
                sync.wait_ge(t_f2.h, oc - 1)
            r = (oc * 128) // OSL
            co = (oc * 128) % OSL
            i16 = max(1, IC // 16)
            for q16 in range(IC // i16):
                ld_q2.issue(sync, oc % 2,
                            q2_sb.rearrange("p (b f w) -> p b f w", b=2, f=IC)
                            [:, oc % 2, q16 * i16:(q16 + 1) * i16],
                            agb2r[r, :, q16 * i16:(q16 + 1) * i16, co:co + 128])
            if oc >= 1:
                sync.wait_ge(v_fin.h, oc)
            ld_fa.inc(sync.dma_start(
                out=asp2_sb[:, :], in_=attn_sp.ap()[oc]), 16)
            ld_fx.gate(sync, oc % 2)
            if oc >= 2:
                sync.wait_ge(v_ot.h, (oc - 1) * TC)
            hh = TC // 2
            for h2 in range(2):
                ld_fx.issue(sync, oc % 2,
                            xcol_sb.rearrange("p (b tc w) -> p b tc w", b=2, tc=TC)
                            [:, oc % 2, h2 * hh:(h2 + 1) * hh],
                            xcolr[:, h2 * hh:(h2 + 1) * hh, oc * 128:(oc + 1) * 128])
        st_out.wait_all(sync)

    def _btr(tensor, oc):
        tensor.wait_ge(v_fin.h, oc + 1)
        for tc in range(TC):
            ti = oc * TC + tc
            if ti >= 2:
                tensor.wait_ge(v_ot.h, ti - 1)
            mm = tensor.transpose(pF_sl(ti % 4),
                                  fin_sb[:, tc * 128:(tc + 1) * 128],
                                  idf_sb[:, :])
            t_F.inc(mm)

    @blk.tensor
    def _(tensor):
        if STOP < 6:
            return
        tensor.wait_ge(cc_b2.h, 1)
        boff = IC // 4
        bstep = max(1, (IC - boff) // TC)
        bt = {boff + bstep * tc: tc for tc in range(TC)}
        for oc in range(FC):
            ld_q2.wait(tensor, oc % 2, 16 * (IC // max(1, IC // 16)) * (oc // 2 + 1))
            if oc == 0:
                tensor.wait_ge(s_wo.h, FC)
            if oc >= 2:
                tensor.wait_ge(s_f2.h, oc - 1)
            pp = pW0 if oc % 2 == 0 else pW1
            for ic in range(IC):
                if oc == 0:
                    tensor.wait_ge(v_qh.h, ic + 1)
                if oc >= 1 and ic in bt:
                    tc = bt[ic]
                    ti = (oc - 1) * TC + tc
                    if tc == 0:
                        tensor.wait_ge(v_fin.h, oc)
                    if ti >= 2:
                        tensor.wait_ge(v_ot.h, ti - 1)
                    t_F.inc(tensor.transpose(pF_sl(ti % 4),
                                             fin_sb[:, tc * 128:(tc + 1) * 128],
                                             idf_sb[:, :]))
                wt = q2_sb[:, (oc % 2) * IC * 128 + ic * 128:(oc % 2) * IC * 128 + ic * 128 + 128]
                for (o, w) in PC:
                    mm = tensor.matmul(pp[:, o:o + w], wt, qh_sl(ic, o, w),
                                       start=(ic == 0), stop=(ic == IC - 1))
            t_f2.inc(mm)
        _btr(tensor, FC - 1)

    @blk.scalar
    def _(scalar):
        if STOP < 6:
            return
        for oc in range(FC):
            scalar.wait_ge(t_f2.h, oc + 1)
            if oc == 0:
                scalar.wait_ge(v_sx.h, 2)
            if oc >= 2:
                scalar.wait_ge(v_fin.h, oc - 1)
            s_f2.inc(scalar.activation(
                f2_sb[:, (oc % 2) * T:(oc % 2) * T + T],
                (pW0 if oc % 2 == 0 else pW1)[:, :], AF.Identity,
                bias=b2_sb[:, oc:oc + 1], scale=s2_col[:, oc:oc + 1]))

    @blk.vector
    def _(vec):
        if STOP < 6:
            return
        for oc in range(FC):
            vec.wait_ge(s_f2.h, oc + 1)
            vec.wait_ge(ld_fa.h, 16 * (oc + 1))
            if oc >= 1:
                vec.wait_ge(t_F.h, oc * TC)
            vec.tensor_tensor(fin_sb[:, :],
                              f2_sb[:, (oc % 2) * T:(oc % 2) * T + T],
                              asp2_sb[:, :], ALU.add)
            v_fin.inc(vec.drain())
            ld_fx.wait(vec, oc % 2, 32 * (oc // 2 + 1))
            for tc in range(TC):
                ti = oc * TC + tc
                vec.wait_ge(t_F.h, ti + 1)
                if ti >= 4:
                    st_out.wait(vec, ti % 4, 16 * ((ti - 4) // 4 + 1))
                vec.tensor_tensor(
                    outt_sb[:, (ti % 4) * 128:(ti % 4) * 128 + 128],
                    pF_sl(ti % 4),
                    xcol_sb[:, (oc % 2) * T + tc * 128:(oc % 2) * T + (tc + 1) * 128],
                    ALU.add)
                v_ot.inc(vec.drain())

    @blk.gpsimd
    def _(g):
        if STOP < 6:
            return
        for oc in range(FC):
            for tc in range(TC):
                ti = oc * TC + tc
                g.wait_ge(v_ot.h, ti + 1)
                st_out.gate(g, ti % 4)
                st_out.issue(g, ti % 4,
                             out_t[tc * 128:(tc + 1) * 128, oc * 128:(oc + 1) * 128],
                             outt_sb[:, (ti % 4) * 128:(ti % 4) * 128 + 128])

    if STOP < 6:
        @blk.sync
        def _(sync):
            if STOP >= 1:
                st_xn.wait_all(sync)
                sync.wait_ge(v_lm.h, 2)
                sync.wait_ge(v_fme.h, 2 * TC)
            if STOP >= 2:
                st_k.wait(sync, 0, 16 * NH)
                st_v.wait(sync, 0, 32 * NH)
                sync.wait_ge(v_wq2.h, N2C)
                sync.wait_ge(v_vte.h, NH)
            if STOP >= 3:
                st_h.wait_all(sync)
                sync.wait_ge(v_lm.h, 4)
                sync.wait_ge(v_fme.h, 4 * TC)
            if STOP >= 4:
                sync.wait_ge(v_att.h, NH)
            if STOP >= 5:
                st_w.wait_all(sync)
                sync.wait_ge(v_qh.h, IC)

    esP6.close()
    esQH.close()
    es_qha.close()
    esW.close()
    es.close()
    return nc



_CFG_FULL = {"d": 2048, "I": 8192, "T": 1024, "qa": 0}
_NC_CACHE = {}


def _get_nc(key, cfg):
    if key not in _NC_CACHE:
        _NC_CACHE[key] = build_nc(dict(cfg))
    return _NC_CACHE[key]


def host_inputs(core, x_full, Wq, bq, Wk, bk, Wv, bv, Wo, w1, b1, w2, b2, cfg):
    d, I, T = cfg["d"], cfg["I"], cfg["T"]
    NH, FC, IC = d // 128, d // 128, I // 128
    OSL, ISL = d // N_CORES, I // N_CORES
    b = core // GROUP
    q0 = (core % GROUP) * T
    f32 = np.float32
    bf16 = ml_dtypes.bfloat16

    m = {}
    m["x_in"] = np.ascontiguousarray(x_full[b, q0:q0 + T, :], dtype=f32)
    m["wqT_sl"] = np.ascontiguousarray(Wq.T[:, core * OSL:(core + 1) * OSL], dtype=f32)
    m["wkT_sl"] = np.ascontiguousarray(Wk.T[:, core * OSL:(core + 1) * OSL], dtype=f32)
    m["wvT_sl"] = np.ascontiguousarray(Wv.T[:, core * OSL:(core + 1) * OSL], dtype=f32)
    m["woT_sl"] = np.ascontiguousarray(Wo.T[:, core * OSL:(core + 1) * OSL], dtype=f32)
    m["w1T_sl"] = np.ascontiguousarray(w1.T[:, core * ISL:(core + 1) * ISL], dtype=f32)
    m["w2T_sl"] = np.ascontiguousarray(w2.T[:, core * OSL:(core + 1) * OSL], dtype=f32)
    m["bq_t"] = np.ascontiguousarray(bq.reshape(NH, 128).T, dtype=f32)
    m["bk_t"] = np.ascontiguousarray(bk.reshape(NH, 128).T, dtype=f32)
    m["bv_t"] = np.ascontiguousarray(bv.reshape(NH, 128).T, dtype=f32)
    m["b1_t"] = np.ascontiguousarray(b1.reshape(IC, 128).T, dtype=f32)
    m["b2_t"] = np.ascontiguousarray(b2.reshape(FC, 128).T, dtype=f32)
    hd = 128
    inv = (1.0 / (np.float32(1000000.0) **
                  (np.arange(0, hd, 2, dtype=f32) / f32(hd)))).astype(f32)
    pos = np.arange(q0, q0 + T, dtype=f32)
    fr = np.outer(pos, inv).astype(f32)
    emb = np.concatenate([fr, fr], axis=1)
    m["cos_t"] = np.ascontiguousarray(np.cos(emb).T, dtype=f32)
    m["sin_t"] = np.ascontiguousarray(np.sin(emb).T, dtype=f32)
    m["id_bf"] = np.eye(128, dtype=bf16)
    m["id_f32"] = np.eye(128, dtype=f32)
    R = np.zeros((128, 128), dtype=f32)
    for i in range(64):
        R[i, i + 64] = -1.0
        R[i + 64, i] = 1.0
    m["rT_bf"] = np.ascontiguousarray(R.T).astype(bf16)
    m["ones_bf"] = np.ones((128, 128), dtype=bf16)
    m["ones_f32"] = np.ones((128, 128), dtype=f32)
    m["onesf"] = np.ones((1, 128), dtype=f32)
    return m


def kernel(L0_attn_hidden_states, L0_attn_Wq, L0_attn_bq, L0_attn_Wk, L0_attn_bk,
           L0_attn_Wv, L0_attn_bv, L0_attn_Wo, L0_ffn_w1, L0_ffn_b1,
           L0_ffn_w2, L0_ffn_b2):
    cfg = _CFG_FULL
    x = np.asarray(L0_attn_hidden_states, dtype=np.float32)
    args = [np.asarray(a, dtype=np.float32) for a in
            (L0_attn_Wq, L0_attn_bq, L0_attn_Wk, L0_attn_bk, L0_attn_Wv,
             L0_attn_bv, L0_attn_Wo, L0_ffn_w1, L0_ffn_b1, L0_ffn_w2, L0_ffn_b2)]
    nc = _get_nc("full", cfg)
    in_maps = [host_inputs(c, x, *args, cfg) for c in range(N_CORES)]
    res = run_bass_kernel_spmd(nc, in_maps, core_ids=list(range(N_CORES)))
    T, d = cfg["T"], cfg["d"]
    B, Sq = x.shape[0], x.shape[1]
    out = np.empty((B, Sq, d), dtype=np.float32)
    for c in range(N_CORES):
        b = c // GROUP
        q0 = (c % GROUP) * T
        out[b, q0:q0 + T, :] = np.asarray(res.results[c]["out"])
    return out


# version 16
# speedup vs baseline: 1.0777x; 1.0118x over previous
import contextlib

import numpy as np
import ml_dtypes

import concourse.bass as bass
import concourse.mybir as mybir
from concourse.bass_utils import run_bass_kernel_spmd

F32 = mybir.dt.float32
BF16 = mybir.dt.bfloat16
AF = mybir.ActivationFunctionType
ALU = mybir.AluOpType
AX = mybir.AxisListType

RND_C = 3.0 * 2.0**22
N_CORES = 8
GROUP = 4


class Sem:
    def __init__(self, nc, es, name):
        self.h = es.enter_context(nc.semaphore(name))
        self.c = 0

    def inc(self, instr, n=1):
        instr.then_inc(self.h, n)
        self.c += n
        return self.c


class DSem:

    def __init__(self, nc, es, name, nslots):
        self.s = [Sem(nc, es, f"{name}{i}") for i in range(nslots)]

    def gate(self, eng, slot):
        sm = self.s[slot]
        if sm.c > 0:
            eng.wait_ge(sm.h, sm.c)

    def issue(self, eng, slot, out, in_):
        sm = self.s[slot]
        sm.inc(eng.dma_start(out=out, in_=in_), 16)
        return sm.c

    def cnt(self, slot):
        return self.s[slot].c

    def wait(self, eng, slot, cnt):
        eng.wait_ge(self.s[slot].h, cnt)

    def wait_all(self, eng):
        for sm in self.s:
            if sm.c:
                eng.wait_ge(sm.h, sm.c)


def _pieces(T):
    n = (T + 511) // 512
    w = T // n
    assert w * n == T
    return [(i * w, w) for i in range(n)]


def build_nc(cfg):
    d = cfg["d"]
    I = cfg["I"]
    T = cfg["T"]
    NH = d // 128
    FC = d // 128
    IC = I // 128
    TC = T // 128
    S = GROUP * T
    KC = S // 128
    OSL = d // N_CORES
    ISL = I // N_CORES
    PC = _pieces(T)
    NP = len(PC)
    assert OSL % 128 == 0 and ISL % 128 == 0
    NW1 = ISL // 128
    NW2 = OSL // 128
    W2C = max(1, IC // 16)
    N2C = IC // W2C
    n_bn = d // 512
    assert n_bn * 512 == d and T % 128 == 0
    QA = min(IC, cfg.get("qa", 20))
    QB = IC - QA
    EXPS = np.float32(np.float32(1.0) / np.sqrt(np.float32(128.0)))
    EXPB = np.float32(-20.0)
    HSL = (FC // 2) * ISL
    FH = FC // 2

    STOP = cfg.get("stop", 99)
    nc = bass.Bass("TRN2", target_bir_lowering=False)

    x_in = nc.dram_tensor("x_in", [T, d], F32, kind="ExternalInput").ap()
    wqT = nc.dram_tensor("wqT_sl", [d, OSL], F32, kind="ExternalInput").ap()
    wkT = nc.dram_tensor("wkT_sl", [d, OSL], F32, kind="ExternalInput").ap()
    wvT = nc.dram_tensor("wvT_sl", [d, OSL], F32, kind="ExternalInput").ap()
    woT = nc.dram_tensor("woT_sl", [d, OSL], F32, kind="ExternalInput").ap()
    w1T = nc.dram_tensor("w1T_sl", [d, ISL], F32, kind="ExternalInput").ap()
    w2T = nc.dram_tensor("w2T_sl", [I, OSL], F32, kind="ExternalInput").ap()
    bq_t = nc.dram_tensor("bq_t", [128, NH], F32, kind="ExternalInput").ap()
    bk_t = nc.dram_tensor("bk_t", [128, NH], F32, kind="ExternalInput").ap()
    bv_t = nc.dram_tensor("bv_t", [128, NH], F32, kind="ExternalInput").ap()
    b1_t = nc.dram_tensor("b1_t", [128, IC], F32, kind="ExternalInput").ap()
    b2_t = nc.dram_tensor("b2_t", [128, FC], F32, kind="ExternalInput").ap()
    cos_t = nc.dram_tensor("cos_t", [128, T], F32, kind="ExternalInput").ap()
    sin_t = nc.dram_tensor("sin_t", [128, T], F32, kind="ExternalInput").ap()
    id_bf = nc.dram_tensor("id_bf", [128, 128], BF16, kind="ExternalInput").ap()
    id_f32 = nc.dram_tensor("id_f32", [128, 128], F32, kind="ExternalInput").ap()
    rT_bf = nc.dram_tensor("rT_bf", [128, 128], BF16, kind="ExternalInput").ap()
    ones_bf = nc.dram_tensor("ones_bf", [128, 128], BF16, kind="ExternalInput").ap()
    ones_f32 = nc.dram_tensor("ones_f32", [128, 128], F32, kind="ExternalInput").ap()
    onesf_t = nc.dram_tensor("onesf", [1, 128], F32, kind="ExternalInput").ap()
    out_t = nc.dram_tensor("out", [T, d], F32, kind="ExternalOutput").ap()

    aga_in = nc.dram_tensor("aga_in", [4 * d, OSL], BF16)
    aga_out = nc.dram_tensor("aga_out", [N_CORES * 4 * d, OSL], BF16, addr_space="Shared")
    agb1_in = nc.dram_tensor("agb1_in", [d, ISL], BF16)
    agb1_out = nc.dram_tensor("agb1_out", [N_CORES * d, ISL], BF16, addr_space="Shared")
    agb2_in = nc.dram_tensor("agb2_in", [I, OSL], BF16)
    agb2_out = nc.dram_tensor("agb2_out", [N_CORES * I, OSL], BF16, addr_space="Shared")
    ssz = ISL + OSL
    ags_in = nc.dram_tensor("ags_in", [ssz], F32)
    ags_out = nc.dram_tensor("ags_out", [N_CORES * ssz], F32, addr_space="Shared")
    m1_in = nc.dram_tensor("m1_in", [128], F32)
    m1_out = nc.dram_tensor("m1_out", [128], F32, addr_space="Shared")
    m2_in = nc.dram_tensor("m2_in", [128], F32)
    m2_out = nc.dram_tensor("m2_out", [128], F32, addr_space="Shared")
    agc_sz = NH * 128 * T * 2
    agc_in = nc.dram_tensor("agc_in", [agc_sz], BF16)
    agc_out = nc.dram_tensor("agc_out", [GROUP * agc_sz], BF16)
    xn_sp = nc.dram_tensor("xn_sp", [TC, 128, d], F32)
    h_sp = nc.dram_tensor("h_sp", [IC, 128, T], F32)
    attn_sp = nc.dram_tensor("attn_sp", [FC, 128, T], F32)
    rs_scr = nc.dram_tensor("rs_scr", [max(ISL, OSL)], F32)
    KOFF = 0
    VOFF = NH * 128 * T

    es = contextlib.ExitStack()
    SM = lambda name: Sem(nc, es, name)
    DS = lambda name, n: DSem(nc, es, name, n)
    ld_aux = SM("ld_aux")
    ld_x = DS("ld_x", 3); ld_w1 = DS("ld_w1", 2); ld_w2 = DS("ld_w2", 2)
    ld_wq = DS("ld_wq", 2); ld_wo = DS("ld_wo", 2); ld_q1 = DS("ld_q1", 4)
    ld_kv = DS("ld_kv", 2); ld_h = DS("ld_h", 2); ld_q2 = DS("ld_q2", 2)
    ld_fa = SM("ld_fa"); ld_fx = DS("ld_fx", 2); ld_sw = SM("ld_sw")
    ld_xq = DS("ld_xq", 3)
    st_out = DS("st_out", 4)
    st_xn = DS("st_xn", 3); st_k = DS("st_k", 1); st_v = DS("st_v", 1)
    st_h = DS("st_h", 2); st_w = DS("st_w", 2)
    g_dma = SM("g_dma"); g_csc = SM("g_csc")
    g_w1 = DS("g_w1", 2); g_w2 = DS("g_w2", 2)
    cc_a = SM("cc_a"); cc_b1 = SM("cc_b1"); cc_b2 = SM("cc_b2")
    cc_s = SM("cc_s"); cc_m1 = SM("cc_m1"); cc_m2 = SM("cc_m2"); cc_kv = SM("cc_kv")
    t_xtr = SM("t_xtr"); t_mx = SM("t_mx"); t_sw = SM("t_sw")
    t_p = SM("t_p"); t_r = SM("t_r"); t_vt = SM("t_vt"); t_f1 = SM("t_f1")
    t_S = SM("t_S"); t_A = SM("t_A"); t_O = SM("t_O"); t_wo = SM("t_wo")
    t_f2 = SM("t_f2"); t_F = SM("t_F")
    v_ve = SM("v_ve"); v_xnb = SM("v_xnb"); v_fme = SM("v_fme")
    v_lm = SM("v_lm"); v_mx1 = SM("v_mx1"); v_w2m = SM("v_w2m")
    v_mxr = SM("v_mxr"); v_wq1 = SM("v_wq1"); v_wq2 = SM("v_wq2")
    v_swc = SM("v_swc"); v_sx = SM("v_sx"); v_rope = SM("v_rope")
    v_vte = SM("v_vte"); v_qx = SM("v_qx"); v_hx = SM("v_hx"); v_att = SM("v_att")
    v_qh = SM("v_qh"); v_fin = SM("v_fin"); v_ot = SM("v_ot"); v_pp1 = SM("v_pp1")
    s_ln = SM("s_ln"); s_qb = SM("s_qb"); s_g = SM("s_g"); s_exp = SM("s_exp")
    s_rec = SM("s_rec"); s_wo = SM("s_wo"); s_f2 = SM("s_f2")
    v_acc = v_vte; s_p1 = s_qb
    t_pb = SM("t_pb"); t_lm = SM("t_lm")
    g_rr1 = SM("g_rr1"); g_rr2 = SM("g_rr2"); g_gl1 = SM("g_gl1"); g_gl2 = SM("g_gl2")


    sb = lambda name, shape, dt: es.enter_context(nc.sbuf_tensor(name, shape, dt))
    id_sb = sb("id_sb", [128, 128], BF16)
    idf_sb = sb("idf_sb", [128, 128], F32)
    rT_sb = sb("rT_sb", [128, 128], BF16)
    ones_sb = sb("ones_sb", [128, 128], BF16)
    onesf32_sb = sb("onesf32_sb", [128, 128], F32)
    bq_sb = sb("bq_sb", [128, NH], F32)
    bk_sb = sb("bk_sb", [128, NH], F32)
    bv_sb = sb("bv_sb", [128, NH], F32)
    b1_sb = sb("b1_sb", [128, IC], F32)
    b2_sb = sb("b2_sb", [128, FC], F32)
    sw1_col = sb("sw1_col", [128, IC], F32)
    sw2_col = sb("sw2_col", [128, FC], F32)
    s1_col = sb("s1_col", [128, IC], F32)
    s2_col = sb("s2_col", [128, FC], F32)
    onesf_sb = sb("onesf_sb", [1, 128], F32)
    g1row_sb = sb("g1row_sb", [1, 128], F32)
    g2row_sb = sb("g2row_sb", [1, 128], F32)
    gm1_sb = sb("gm1_sb", [1, 1], F32)
    gm2_sb = sb("gm2_sb", [1, 1], F32)
    sx1_col = sb("sx1_col", [128, 1], F32)
    sx2_col = sb("sx2_col", [128, 1], F32)
    r1_col = sb("r1_col", [128, 1], F32)
    r2_col = sb("r2_col", [128, 1], F32)
    csc = sb("csc", [128, 8], F32)
    mv_all = sb("mv_all", [128, 2 * TC], F32)
    ve_all = sb("ve_all", [128, TC], F32)
    rstd_all = sb("rstd_all", [128, TC], F32)
    ln_tmp = sb("ln_tmp", [128, TC], F32)
    lmax_sb = sb("lmax_sb", [128, TC], F32)
    hmax_sb = sb("hmax_sb", [128, IC], F32)

    blk = es.enter_context(nc.Block())
    ppi = {"w": 0}

    esQR = contextlib.ExitStack()
    q_rope = esQR.enter_context(nc.sbuf_tensor("q_rope", [128, NH * T], BF16))
    esA2 = contextlib.ExitStack()
    A2 = lambda n, s, dt: esA2.enter_context(nc.sbuf_tensor(n, s, dt))
    xn_fm = A2("xn_fm", [128, FC * T], BF16)
    cos_sb = A2("cos_sb", [128, T], F32)
    sin_sb = A2("sin_sb", [128, T], F32)
    esA1a = contextlib.ExitStack()
    A1a = lambda n, s, dt: esA1a.enter_context(nc.sbuf_tensor(n, s, dt))
    x_tiles = A1a("x_tiles", [128, 3 * d], F32)
    bn_scr = A1a("bn_scr", [128, 6 * n_bn], F32)
    swsl = A1a("swsl", [max(ISL, OSL) // 128, 128], F32)
    esP2 = contextlib.ExitStack()
    P2 = lambda n, s, dt: esP2.enter_context(nc.sbuf_tensor(n, s, dt))
    wsl3 = P2("wsl3", [128, 2 * 3 * FC * 128], BF16)
    qb_sb = P2("qb_sb", [128, T], BF16)
    vb_sb = P2("vb_sb", [128, T], F32)
    vtm_sb = P2("vtm_sb", [128, T], BF16)
    kr_sb = P2("kr_sb", [128, T], BF16)
    m1r_sb = P2("m1r_sb", [128, T], F32)
    m2r_sb = P2("m2r_sb", [128, T], F32)
    esA1b = contextlib.ExitStack()
    A1b = lambda n, s, dt: esA1b.enter_context(nc.sbuf_tensor(n, s, dt))
    w1p = A1b("w1p", [128, 2 * FC * 128], F32)
    w1q_bf = A1b("w1q_bf", [128, 2 * FC * 128], BF16)
    w2p = A1b("w2p", [128, 2 * W2C * OSL], F32)
    w2q_bf = A1b("w2q_bf", [128, 2 * W2C * OSL], BF16)
    mxw = A1b("mxw", [128, max(128, OSL)], F32)
    mnw = A1b("mnw", [128, max(128, OSL)], F32)
    mxq = A1b("mxq", [128, max(128, OSL)], F32)
    rep1 = A1b("rep1", [128, 128], F32)
    rep2 = A1b("rep2", [128, OSL], F32)
    rr1 = A1b("rr1", [1, 128], F32)
    rr2 = A1b("rr2", [1, OSL], F32)
    esXY = contextlib.ExitStack()
    pX = esXY.enter_context(nc.psum_tensor("pX", [128, T], F32))
    pY = esXY.enter_context(nc.psum_tensor("pY", [128, T], F32))
    esT = contextlib.ExitStack()
    pT = esT.enter_context(nc.psum_tensor("pT", [128, 1024], F32))
    pP1 = esT.enter_context(nc.psum_tensor("pP1", [128, 128], F32))

    w1r = w1T.rearrange("(f p) i -> p f i", p=128)
    w2r = w2T.rearrange("(f p) o -> p f o", p=128)
    agar = aga_out.ap().rearrange("(r j f p) o -> r j p f o", j=4, f=FC, p=128)
    agb1r = agb1_out.ap().rearrange("(r f p) i -> r p f i", f=FC, p=128)
    agb2r = agb2_out.ap().rearrange("(r f p) o -> r p f o", f=IC, p=128)
    agb1w = agb1_in.ap().rearrange("(f p) i -> p f i", p=128)
    agb2w = agb2_in.ap().rearrange("(c f p) o -> c p f o", f=W2C, p=128)
    xcolr = x_in.rearrange("(tc p) o -> p tc o", p=128)

    units = ([("w1", j) for j in range(NW1)]
             + [("w2m", j) for j in range(N2C)]
             + [("w2s", j) for j in range(NW2)]
             + [("pb2", j) for j in range(NW2)]
             + [("w2q", j) for j in range(N2C)])
    NHP = max(1, NH // 2)
    per_head = [units[(len(units) * h) // NHP:(len(units) * (h + 1)) // NHP]
                if h < NHP else [] for h in range(NH)]


    @blk.sync
    def _(sync):
        if STOP < 1:
            return
        for src, dst in [(cos_t, cos_sb), (sin_t, sin_sb), (id_bf, id_sb),
                         (id_f32, idf_sb), (rT_bf, rT_sb), (ones_bf, ones_sb),
                         (ones_f32, onesf32_sb),
                         (bq_t, bq_sb), (bk_t, bk_sb), (bv_t, bv_sb),
                         (b1_t, b1_sb), (b2_t, b2_sb), (onesf_t, onesf_sb)]:
            ld_aux.inc(sync.dma_start(out=dst[:, :], in_=src), 16)
        for tc in range(TC):
            sl = tc % 3
            ld_x.gate(sync, sl)
            if tc >= 3:
                st_xn.wait(sync, (tc - 3) % 3, 16 * ((tc - 3) // 3 + 1))
                sync.wait_ge(t_xtr.h, 2 * (tc - 2))
            ld_x.issue(sync, sl,
                       x_tiles[:, sl * d:sl * d + d],
                       x_in[tc * 128:(tc + 1) * 128, :])

    @blk.gpsimd
    def _(g):
        if STOP < 2:
            return
        for j, w in enumerate([wqT, wkT, wvT, woT]):
            g_dma.inc(g.dma_start(out=aga_in[j * d:j * d + d // 2, :],
                                  in_=w[0:d // 2, :]), 16)
            g_dma.inc(g.dma_start(out=aga_in[j * d + d // 2:(j + 1) * d, :],
                                  in_=w[d // 2:d, :]), 16)
        g.wait_ge(g_dma.h, g_dma.c)
        cc_a.inc(g.collective_compute(
            "AllGather", ALU.bypass, replica_groups=[list(range(N_CORES))],
            ins=[aga_in.ap().opt()], outs=[aga_out.ap().opt()]))
        for j in range(NW1):
            g.wait_ge(v_mxr.h, j + 1)
            g_csc.inc(g.dma_start(out=rs_scr[j * 128:(j + 1) * 128],
                                  in_=csc[:, 2 + (j % 2):3 + (j % 2)]), 16)
            g_csc.inc(g.dma_start(out=ags_in[j * 128:(j + 1) * 128],
                                  in_=csc[:, 4 + (j % 2):5 + (j % 2)]), 16)
            g.wait_ge(g_csc.h, g_csc.c)
            if j >= 1:
                g.wait_ge(t_pb.h, j)
            g_rr1.inc(g.dma_start(out=rr1[:, :],
                                  in_=rs_scr[j * 128:(j + 1) * 128]), 16)
            g.wait_ge(v_wq1.h, j + 1)
            for q4 in range(4):
                fs = FC // 4
                g_w1.issue(g, j % 2,
                           agb1w[:, q4 * fs:(q4 + 1) * fs, j * 128:(j + 1) * 128],
                           w1q_bf.rearrange("p (b f w) -> p b f w", b=2, f=FC)
                           [:, j % 2, q4 * fs:(q4 + 1) * fs])
        g.wait_ge(g_dma.h, g_dma.c)
        g_w1.wait_all(g)
        cc_b1.inc(g.collective_compute(
            "AllGather", ALU.bypass, replica_groups=[list(range(N_CORES))],
            ins=[agb1_in.ap().opt()], outs=[agb1_out.ap().opt()]))
        for j in range(NW2):
            g.wait_ge(v_mxr.h, NW1 + j + 1)
            g_csc.inc(g.dma_start(out=rs_scr[j * 128:(j + 1) * 128],
                                  in_=csc[:, 2 + (j % 2):3 + (j % 2)]), 16)
            g_csc.inc(g.dma_start(out=ags_in[ISL + j * 128:ISL + (j + 1) * 128],
                                  in_=csc[:, 4 + (j % 2):5 + (j % 2)]), 16)
        g.wait_ge(g_csc.h, g_csc.c)
        g_rr2.inc(g.dma_start(out=rr2[:, :], in_=rs_scr[0:OSL]), 16)
        for j in range(N2C):
            g.wait_ge(v_wq2.h, j + 1)
            g_w2.issue(g, j % 2,
                       agb2w[j],
                       w2q_bf.rearrange("p (b f w) -> p b f w", b=2, f=W2C)[:, j % 2])
        g_w2.wait_all(g)
        cc_b2.inc(g.collective_compute(
            "AllGather", ALU.bypass, replica_groups=[list(range(N_CORES))],
            ins=[agb2_in.ap().opt()], outs=[agb2_out.ap().opt()]))
        cc_s.inc(g.collective_compute(
            "AllGather", ALU.bypass, replica_groups=[list(range(N_CORES))],
            ins=[ags_in.ap().opt()], outs=[ags_out.ap().opt()]))
        if STOP < 3:
            return
        g.wait_ge(v_lm.h, 2)
        g_dma.inc(g.dma_start(out=m1_in[0:128], in_=g1row_sb[:, :]), 16)
        g.wait_ge(g_dma.h, g_dma.c)
        cc_m1.inc(g.collective_compute(
            "AllReduce", ALU.max, replica_groups=[list(range(N_CORES))],
            ins=[m1_in.ap().opt()], outs=[m1_out.ap().opt()]))
        g.wait_ge(cc_m1.h, 1)
        g_gl1.inc(g.dma_start(out=sx1_col[:, :],
                              in_=m1_out.ap().rearrange("(p o) -> p o", o=1)), 16)
        if STOP < 4:
            pass
        st_k.wait(g, 0, 16 * NH)
        st_v.wait(g, 0, 32 * NH)
        cc_kv.inc(g.collective_compute(
            "AllGather", ALU.bypass,
            replica_groups=[list(range(G * GROUP, (G + 1) * GROUP))
                            for G in range(N_CORES // GROUP)],
            ins=[agc_in.ap().opt()], outs=[agc_out.ap().opt()]))
        g.wait_ge(v_lm.h, 4)
        g_dma.inc(g.dma_start(out=m2_in[0:128], in_=g2row_sb[:, :]), 16)
        g.wait_ge(g_dma.h, g_dma.c)
        cc_m2.inc(g.collective_compute(
            "AllReduce", ALU.max, replica_groups=[list(range(N_CORES))],
            ins=[m2_in.ap().opt()], outs=[m2_out.ap().opt()]))
        g.wait_ge(cc_m2.h, 1)
        g_gl2.inc(g.dma_start(out=sx2_col[:, :],
                              in_=m2_out.ap().rearrange("(p o) -> p o", o=1)), 16)

    @blk.scalar
    def _(scalar):
        if STOP < 1:
            return
        for tc in range(TC):
            scalar.wait_ge(v_ve.h, tc + 1)
            scalar.activation(ln_tmp[:, tc:tc + 1], ve_all[:, tc:tc + 1], AF.Ln)
            scalar.drain()
            s_ln.inc(scalar.activation(rstd_all[:, tc:tc + 1],
                                       ln_tmp[:, tc:tc + 1], AF.Exp, scale=np.float32(-0.5)))
            if tc >= 1:
                scalar.wait_ge(v_xnb.h, tc)
                st_xn.gate(scalar, (tc - 1) % 3)
                st_xn.issue(scalar, (tc - 1) % 3, xn_sp.ap()[tc - 1],
                            x_tiles[:, ((tc - 1) % 3) * d:((tc - 1) % 3) * d + d])
        scalar.wait_ge(v_xnb.h, TC)
        st_xn.gate(scalar, (TC - 1) % 3)
        st_xn.issue(scalar, (TC - 1) % 3, xn_sp.ap()[TC - 1],
                    x_tiles[:, ((TC - 1) % 3) * d:((TC - 1) % 3) * d + d])

    @blk.vector
    def _(vec):
        if STOP < 1:
            return
        def norm(tc):
            vec.wait_ge(s_ln.h, tc + 1)
            xt = x_tiles[:, (tc % 3) * d:(tc % 3) * d + d]
            vec.tensor_scalar(xt[:, :], xt[:, :], mv_all[:, 2 * tc:2 * tc + 1],
                              rstd_all[:, tc:tc + 1], ALU.subtract, ALU.mult)
            vec.drain()
            vec.tensor_reduce(lmax_sb[:, tc:tc + 1], xt[:, :], AX.X, ALU.max,
                              apply_absolute_value=True)
            v_xnb.inc(vec.drain())
        def xn_evac(jt):
            for half in range(2):
                vec.wait_ge(t_xtr.h, 2 * jt + half + 1)
                for jj in range(FH):
                    fc = half * FH + jj
                    vec.tensor_copy(xn_fm[:, fc * T + jt * 128: fc * T + (jt + 1) * 128],
                                    pT[:, jj * 128:(jj + 1) * 128])
                v_fme.inc(vec.drain())
        for tc in range(TC):
            ld_x.wait(vec, tc % 3, 16 * (tc // 3 + 1))
            xt = x_tiles[:, (tc % 3) * d:(tc % 3) * d + d]
            for j in range(n_bn):
                vec.bn_stats(bn_scr[:, 6 * j:6 * j + 6], xt[:, 512 * j:512 * (j + 1)])
            vec.drain()
            vec.bn_aggr(mv_all[:, 2 * tc:2 * tc + 2], bn_scr[:, :])
            vec.drain()
            vec.tensor_scalar(ve_all[:, tc:tc + 1], mv_all[:, 2 * tc + 1:2 * tc + 2],
                              1e-5, None, ALU.add)
            v_ve.inc(vec.drain())
            if tc >= 1:
                norm(tc - 1)
            if tc >= 2:
                xn_evac(tc - 2)
        norm(TC - 1)
        for jt in range(max(0, TC - 2), TC):
            xn_evac(jt)
        vec.tensor_reduce(csc[:, 0:1], lmax_sb[:, :], AX.X, ALU.max)
        v_lm.inc(vec.drain())

    @blk.tensor
    def _(tensor):
        if STOP < 1:
            return
        tensor.wait_ge(ld_aux.h, ld_aux.c)
        for tc in range(TC):
            tensor.wait_ge(v_xnb.h, tc + 1)
            for half in range(2):
                if 2 * tc + half >= 1:
                    tensor.wait_ge(v_fme.h, 2 * tc + half)
                for j in range(FH):
                    fc = half * FH + j
                    mm = tensor.transpose(
                        pT[:, j * 128:(j + 1) * 128],
                        x_tiles[:, (tc % 3) * d + fc * 128:(tc % 3) * d + fc * 128 + 128],
                        idf_sb[:, :])
                t_xtr.inc(mm)
        tensor.wait_ge(v_lm.h, 1)
        tensor.wait_ge(v_pp1.h, ppi["w"])
        t_lm.inc(tensor.matmul(pP1[0:1, :], csc[:, 0:1], idf_sb[:, :],
                               start=True, stop=True))
        ppi["w"] += 1

    @blk.vector
    def _(vec):
        if STOP < 1:
            return
        vec.wait_ge(t_lm.h, 1)
        vec.tensor_reduce(gm1_sb[:, :], pP1[0:1, :], AX.X, ALU.max)
        vec.drain()
        vec.tensor_scalar(g1row_sb[:, :], pP1[0:1, :], gm1_sb[0:1, 0:1],
                          None, ALU.max)
        vec.drain()
        v_pp1.inc(vec.drain())
        v_lm.inc(vec.drain())

    def _unit_load(sync, kind, j):
        if kind == "w1":
            ld_w1.gate(sync, j % 2)
            if j >= 2:
                sync.wait_ge(v_wq1.h, j - 1)
            for q4 in range(4):
                fs = FC // 4
                ld_w1.issue(sync, j % 2,
                            w1p.rearrange("p (b f w) -> p b f w", b=2, f=FC)
                            [:, j % 2, q4 * fs:(q4 + 1) * fs],
                            w1r[:, q4 * fs:(q4 + 1) * fs, j * 128:(j + 1) * 128])
        elif kind == "w2m":
            ld_w2.gate(sync, j % 2)
            if j >= 2:
                sync.wait_ge(v_w2m.h, j - 1)
            ld_w2.issue(sync, j % 2,
                        w2p.rearrange("p (b f w) -> p b f w", b=2, f=W2C)[:, j % 2],
                        w2r[:, j * W2C:(j + 1) * W2C, :])
        elif kind == "w2q":
            ld_w2.gate(sync, j % 2)
            if j == 0:
                sync.wait_ge(v_mx1.h, NW1 + 1)
            if j >= 2:
                sync.wait_ge(v_wq2.h, j - 1)
            ld_w2.issue(sync, j % 2,
                        w2p.rearrange("p (b f w) -> p b f w", b=2, f=W2C)[:, j % 2],
                        w2r[:, j * W2C:(j + 1) * W2C, :])

    @blk.sync
    def _(sync):
        if STOP < 2:
            return
        sync.wait_ge(cc_a.h, 1)
        for kind, j in per_head[0]:
            _unit_load(sync, kind, j)
        for h in range(NH):
            if h >= 2:
                sync.wait_ge(t_p.h, 3 * NP * (h - 1))
            base = (h % 2) * 3 * FC * 128
            r = (h * 128) // OSL
            co = (h * 128) % OSL
            ld_wq.gate(sync, h % 2)
            fs = FC // 4
            for pj in range(3):
                dst = wsl3[:, base + pj * FC * 128: base + (pj + 1) * FC * 128]
                for q4 in range(4):
                    ld_wq.issue(sync, h % 2,
                                dst.rearrange("p (f w) -> p f w", f=FC)
                                [:, q4 * fs:(q4 + 1) * fs],
                                agar[r, pj, :, q4 * fs:(q4 + 1) * fs, co:co + 128])
            if h + 1 < NH:
                for kind, j in per_head[h + 1]:
                    _unit_load(sync, kind, j)

    @blk.tensor
    def _(tensor):
        if STOP < 2:
            return
        nmx = 0
        for h in range(NH):
            for kind, j in per_head[h]:
                if kind == "w1":
                    tensor.wait_ge(v_mx1.h, nmx + 1)
                    tensor.wait_ge(v_pp1.h, ppi["w"])
                    t_mx.inc(tensor.transpose(pP1[:, :], mxw[:, 0:128], idf_sb[:, :]))
                    ppi["w"] += 1
                    nmx += 1
                    tensor.wait_ge(g_rr1.h, 16 * (j + 1))
                    tensor.wait_ge(v_pp1.h, ppi["w"])
                    t_pb.inc(tensor.matmul(pP1[:, :], onesf_sb[:, :], rr1[:, :],
                                            start=True, stop=True))
                    ppi["w"] += 1
                elif kind == "w2s":
                    tensor.wait_ge(v_mx1.h, nmx + 1)
                    tensor.wait_ge(v_pp1.h, ppi["w"])
                    t_mx.inc(tensor.transpose(pP1[:, :], mxw[:, j * 128:(j + 1) * 128],
                                               idf_sb[:, :]))
                    ppi["w"] += 1
                    nmx += 1
                elif kind == "pb2":
                    tensor.wait_ge(g_rr2.h, 16)
                    tensor.wait_ge(v_pp1.h, ppi["w"])
                    t_pb.inc(tensor.matmul(pP1[:, :], onesf_sb[:, :],
                                            rr2[:, j * 128:(j + 1) * 128],
                                            start=True, stop=True))
                    ppi["w"] += 1
            base = (h % 2) * 3 * FC * 128
            ld_wq.wait(tensor, h % 2, 192 * (h // 2 + 1))
            if h == 0:
                tensor.wait_ge(v_fme.h, 2 * TC)
            for m in range(3):
                woff = base + m * FC * 128
                for pi, (o, w) in enumerate(PC):
                    k = (3 * h + m) * NP + pi
                    if k >= 1:
                        tensor.wait_ge(s_qb.h, k - 1)
                    for fc in range(FC):
                        wt = wsl3[:, woff + fc * 128: woff + fc * 128 + 128]
                        mm = tensor.matmul(pX[:, o:o + w], wt,
                                           xn_fm[:, fc * T + o: fc * T + o + w],
                                           start=(fc == 0), stop=(fc == FC - 1))
                    t_p.inc(mm)
                if m < 2:
                    tensor.wait_ge(s_qb.h, (3 * h + m + 1) * NP)
                    if m == 0 and h > 0:
                        tensor.wait_ge(v_rope.h, 2 * h)
                    if m == 1:
                        tensor.wait_ge(v_rope.h, 2 * h + 1)
                    for (o, w) in PC:
                        mm = tensor.matmul(pY[:, o:o + w], rT_sb[:, :],
                                           qb_sb[:, o:o + w],
                                           start=True, stop=True)
                    t_r.inc(mm)
            tensor.wait_ge(s_qb.h, (3 * h + 3) * NP)
            tensor.wait_ge(v_vte.h, h)
            if h == 0:
                tensor.wait_ge(v_fme.h, 2 * TC)
            for tc in range(TC):
                mm = tensor.transpose(pT[:, tc * 128:(tc + 1) * 128],
                                      vb_sb[:, tc * 128:(tc + 1) * 128], idf_sb[:, :])
            t_vt.inc(mm)

    @blk.scalar
    def _(scalar):
        if STOP < 2:
            return
        scalar.wait_ge(ld_aux.h, ld_aux.c)
        for h in range(NH):
            for m in range(3):
                bias = (bq_sb, bk_sb, bv_sb)[m]
                dst = vb_sb if m == 2 else qb_sb
                for pi, (o, w) in enumerate(PC):
                    scalar.wait_ge(t_p.h, (3 * h + m) * NP + pi + 1)
                    if pi == 0:
                        if m == 0 and h > 0:
                            scalar.wait_ge(v_rope.h, 2 * h)
                        if m == 1:
                            scalar.wait_ge(v_rope.h, 2 * h + 1)
                        if m == 2 and h > 0:
                            scalar.wait_ge(t_vt.h, h)
                    s_qb.inc(scalar.activation(dst[:, o:o + w], pX[:, o:o + w],
                                               AF.Identity,
                                               bias=bias[:, h:h + 1]))
            scalar.wait_ge(v_rope.h, 2 * h + 2)
            st_k.gate(scalar, 0)
            st_k.issue(scalar, 0,
                       agc_in.ap()[KOFF + h * 128 * T: KOFF + (h + 1) * 128 * T]
                       .rearrange("(p t) -> p t", p=128),
                       kr_sb[:, :])
            scalar.wait_ge(v_vte.h, h + 1)
            st_v.gate(scalar, 0)
            hh = TC // 2
            for h2 in range(2):
                st_v.issue(scalar, 0,
                           agc_in.ap()[VOFF + h * T * 128: VOFF + (h + 1) * T * 128]
                           .rearrange("(tc p dc) -> p tc dc", p=128, dc=128)
                           [:, h2 * hh:(h2 + 1) * hh],
                           vtm_sb.ap().rearrange("p (tc dc) -> p tc dc", dc=128)
                           [:, h2 * hh:(h2 + 1) * hh])

    @blk.vector
    def _(vec):
        if STOP < 2:
            return
        vec.wait_ge(ld_aux.h, ld_aux.c)
        nmx_v = 0
        w2fold = {"n": 0}
        for h in range(NH):
            for kind, j in per_head[h]:
                if kind == "w1":
                    ld_w1.wait(vec, j % 2, 64 * (j // 2 + 1))
                    src = w1p.rearrange("p (b f w) -> p b f w", b=2, f=FC)[:, j % 2]
                    vec.tensor_tensor(mxw[:, 0:128], src[:, 0], src[:, 1], ALU.max)
                    vec.tensor_tensor(mnw[:, 0:128], src[:, 0], src[:, 1], ALU.min)
                    for fc in range(2, FC):
                        vec.drain()
                        vec.tensor_tensor(mxw[:, 0:128], mxw[:, 0:128], src[:, fc],
                                          ALU.max)
                        vec.tensor_tensor(mnw[:, 0:128], mnw[:, 0:128], src[:, fc],
                                          ALU.min)
                    vec.drain()
                    vec.tensor_scalar(mnw[:, 0:128], mnw[:, 0:128], -1.0, None, ALU.mult)
                    vec.drain()
                    vec.tensor_tensor(mxw[:, 0:128], mxw[:, 0:128], mnw[:, 0:128],
                                      ALU.max)
                    v_mx1.inc(vec.drain())
                    nmx_v += 1
                    vec.wait_ge(t_mx.h, nmx_v)
                    vec.tensor_reduce(csc[:, 1:2], pP1[:, :], AX.X, ALU.max)
                    v_pp1.inc(vec.drain())
                    if j >= 2:
                        vec.wait_ge(g_csc.h, 32 * (j - 1))
                    vec.tensor_scalar(csc[:, 4 + (j % 2):5 + (j % 2)], csc[:, 1:2],
                                      1.0 / 127.0, 1e-8, ALU.mult, ALU.max)
                    vec.drain()
                    vec.reciprocal(csc[:, 2 + (j % 2):3 + (j % 2)],
                                   csc[:, 4 + (j % 2):5 + (j % 2)])
                    v_mxr.inc(vec.drain())
                    vec.wait_ge(t_pb.h, j + 1)
                    vec.tensor_copy(rep1[:, :], pP1[:, :])
                    vec.drain()
                    v_pp1.inc(vec.drain())
                    if j >= 2:
                        g_w1.wait(vec, j % 2, 64 * (j // 2))
                    dstv = w1q_bf.rearrange("p (b f w) -> p b f w", b=2, f=FC)[:, j % 2]
                    for fc in range(FC):
                        vec.tensor_tensor(mxq[:, 0:128], src[:, fc], rep1[:, :], ALU.mult)
                        vec.drain()
                        vec.tensor_scalar(mxq[:, 0:128], mxq[:, 0:128],
                                          RND_C, RND_C, ALU.add, ALU.subtract)
                        vec.drain()
                        vec.tensor_scalar(dstv[:, fc], mxq[:, 0:128],
                                          127.0, -127.0, ALU.min, ALU.max)
                        vec.drain()
                    v_wq1.inc(vec.drain())
                elif kind == "w2m":
                    ld_w2.wait(vec, j % 2, 16 * (j // 2 + 1))
                    if j == 0 and NW1 > 0:
                        vec.wait_ge(t_mx.h, NW1)
                    src = w2p.rearrange("p (b f w) -> p b f w", b=2, f=W2C)[:, j % 2]
                    for f in range(W2C):
                        if j == 0 and f == 0:
                            vec.tensor_tensor(mxw[:, 0:OSL], src[:, 0],
                                              src[:, min(1, W2C - 1)], ALU.max)
                            vec.tensor_tensor(mnw[:, 0:OSL], src[:, 0],
                                              src[:, min(1, W2C - 1)], ALU.min)
                            vec.drain()
                            continue
                        if j == 0 and f == 1 and W2C > 1:
                            continue
                        vec.tensor_tensor(mxw[:, 0:OSL], mxw[:, 0:OSL], src[:, f],
                                          ALU.max)
                        vec.tensor_tensor(mnw[:, 0:OSL], mnw[:, 0:OSL], src[:, f],
                                          ALU.min)
                        vec.drain()
                    v_w2m.inc(vec.drain())
                    w2fold["n"] += 1
                    if w2fold["n"] == N2C:
                        vec.tensor_scalar(mnw[:, 0:OSL], mnw[:, 0:OSL], -1.0,
                                          None, ALU.mult)
                        vec.drain()
                        vec.tensor_tensor(mxw[:, 0:OSL], mxw[:, 0:OSL],
                                          mnw[:, 0:OSL], ALU.max)
                        v_mx1.inc(vec.drain())
                elif kind == "w2s":
                    if j > 0:
                        v_mx1.inc(vec.drain())
                    nmx_v += 1
                    vec.wait_ge(t_mx.h, nmx_v)
                    vec.tensor_reduce(csc[:, 1:2], pP1[:, :], AX.X, ALU.max)
                    v_pp1.inc(vec.drain())
                    vec.wait_ge(g_csc.h, 32 * NW1)
                    vec.tensor_scalar(csc[:, 4 + (j % 2):5 + (j % 2)], csc[:, 1:2],
                                      1.0 / 127.0, 1e-8, ALU.mult, ALU.max)
                    vec.drain()
                    vec.reciprocal(csc[:, 2 + (j % 2):3 + (j % 2)],
                                   csc[:, 4 + (j % 2):5 + (j % 2)])
                    v_mxr.inc(vec.drain())
                elif kind == "pb2":
                    vec.wait_ge(t_pb.h, NW1 + j + 1)
                    vec.tensor_copy(rep2[:, j * 128:(j + 1) * 128], pP1[:, :])
                    vec.drain()
                    v_pp1.inc(vec.drain())
                else:
                    ld_w2.wait(vec, j % 2, 16 * (N2C // 2 + j // 2 + 1))
                    if j >= 2:
                        g_w2.wait(vec, j % 2, 16 * (j // 2))
                    src = w2p.rearrange("p (b f w) -> p b f w", b=2, f=W2C)[:, j % 2]
                    dstv = w2q_bf.rearrange("p (b f w) -> p b f w", b=2, f=W2C)[:, j % 2]
                    for f in range(W2C):
                        vec.tensor_tensor(mxq[:, 0:OSL], src[:, f], rep2[:, :], ALU.mult)
                        vec.drain()
                        vec.tensor_scalar(mxq[:, 0:OSL], mxq[:, 0:OSL],
                                          RND_C, RND_C, ALU.add, ALU.subtract)
                        vec.drain()
                        vec.tensor_scalar(dstv[:, f], mxq[:, 0:OSL],
                                          127.0, -127.0, ALU.min, ALU.max)
                        vec.drain()
                    v_wq2.inc(vec.drain())
            vec.wait_ge(s_qb.h, (3 * h + 1) * NP)
            vec.wait_ge(t_r.h, 2 * h + 1)
            vec.tensor_tensor(m1r_sb[:, :], qb_sb[:, :], cos_sb[:, :], ALU.mult)
            vec.tensor_tensor(m2r_sb[:, :], pY[:, :], sin_sb[:, :], ALU.mult)
            vec.drain()
            vec.tensor_tensor(q_rope[:, h * T:(h + 1) * T], m1r_sb[:, :],
                              m2r_sb[:, :], ALU.add)
            v_rope.inc(vec.drain())
            vec.wait_ge(s_qb.h, (3 * h + 2) * NP)
            vec.wait_ge(t_r.h, 2 * h + 2)
            if h >= 1:
                st_k.wait(vec, 0, 16 * h)
            vec.tensor_tensor(m1r_sb[:, :], qb_sb[:, :], cos_sb[:, :], ALU.mult)
            vec.tensor_tensor(m2r_sb[:, :], pY[:, :], sin_sb[:, :], ALU.mult)
            vec.drain()
            vec.tensor_tensor(kr_sb[:, :], m1r_sb[:, :], m2r_sb[:, :], ALU.add)
            v_rope.inc(vec.drain())
            vec.wait_ge(t_vt.h, h + 1)
            if h >= 1:
                st_v.wait(vec, 0, 32 * h)
            vec.tensor_copy(vtm_sb[:, :], pT[:, 0:T])
            v_vte.inc(vec.drain())

    esA1b.close()
    esP2.close()
    esP3 = contextlib.ExitStack()
    q1slab = esP3.enter_context(nc.sbuf_tensor("q1slab", [128, 4 * HSL], BF16))
    h_sb = esP3.enter_context(nc.sbuf_tensor("h_sb", [128, 2 * T], F32))

    K1 = ISL // 128
    K2 = OSL // 128

    @blk.sync
    def _(sync):
        if STOP < 3:
            return
        sync.wait_ge(cc_s.h, 1)
        for r in range(2 * N_CORES):
            if r >= 1:
                sync.wait_ge(t_sw.h, r)
            if r < N_CORES:
                src = ags_out.ap()[r * ssz: r * ssz + ISL].rearrange("(k p) -> k p", p=128)
                ld_sw.inc(sync.dma_start(out=swsl[0:K1, :], in_=src), 16)
            else:
                rr = r - N_CORES
                src = ags_out.ap()[rr * ssz + ISL: (rr + 1) * ssz].rearrange("(k p) -> k p", p=128)
                ld_sw.inc(sync.dma_start(out=swsl[0:K2, :], in_=src), 16)

    @blk.tensor
    def _(tensor):
        if STOP < 3:
            return
        for r in range(2 * N_CORES):
            k = K1 if r < N_CORES else K2
            tensor.wait_ge(ld_sw.h, 16 * (r + 1))
            tensor.wait_ge(v_pp1.h, ppi["w"])
            t_sw.inc(tensor.transpose(pP1[:, 0:k], swsl[0:k, :], idf_sb[0:k, 0:k]))
            ppi["w"] += 1

    @blk.vector
    def _(vec):
        if STOP < 3:
            return
        for r in range(2 * N_CORES):
            vec.wait_ge(t_sw.h, r + 1)
            if r < N_CORES:
                vec.tensor_copy(sw1_col[:, r * K1:(r + 1) * K1], pP1[:, 0:K1])
            else:
                rr = r - N_CORES
                vec.tensor_copy(sw2_col[:, rr * K2:(rr + 1) * K2], pP1[:, 0:K2])
            vec.drain()
            v_swc.inc(vec.drain())
            v_pp1.inc(vec.drain())
        vec.wait_ge(g_gl1.h, 16)
        vec.tensor_scalar(csc[:, 6:7], sx1_col[:, :], 1.0 / 127.0, 1e-8,
                          ALU.mult, ALU.max)
        vec.drain()
        vec.reciprocal(r1_col[:, :], csc[:, 6:7])
        vec.drain()
        vec.tensor_scalar(s1_col[:, :], sw1_col[:, :], csc[:, 6:7], None, ALU.mult)
        v_sx.inc(vec.drain())

    @blk.sync
    def _(sync):
        if STOP < 3:
            return
        st_xn.wait_all(sync)
        sync.wait_ge(t_xtr.h, 2 * TC)
        for tc in range(TC):
            ld_xq.gate(sync, tc % 3)
            if tc >= 3:
                sync.wait_ge(t_xtr.h, 2 * TC + 2 * (tc - 2))
            ld_xq.issue(sync, tc % 3,
                        x_tiles[:, (tc % 3) * d:(tc % 3) * d + d],
                        xn_sp.ap()[tc])

    @blk.vector
    def _(vec):
        if STOP < 3:
            return
        def qx_evac(jt):
            for half in range(2):
                vec.wait_ge(t_xtr.h, 2 * TC + 2 * jt + half + 1)
                for jj in range(FH):
                    fc = half * FH + jj
                    vec.tensor_copy(xn_fm[:, fc * T + jt * 128: fc * T + (jt + 1) * 128],
                                    pT[:, jj * 128:(jj + 1) * 128])
                v_fme.inc(vec.drain())
        vec.wait_ge(t_p.h, 3 * NH * NP)
        for tc in range(TC):
            ld_xq.wait(vec, tc % 3, 16 * (tc // 3 + 1))
            xo = x_tiles[:, (tc % 3) * d:(tc % 3) * d + d]
            vec.tensor_scalar(xo[:, :], xo[:, :], r1_col[:, :], RND_C,
                              ALU.mult, ALU.add)
            vec.drain()
            vec.tensor_scalar(xo[:, :], xo[:, :], RND_C, 127.0,
                              ALU.subtract, ALU.min)
            vec.drain()
            vec.tensor_scalar(xo[:, :], xo[:, :], -127.0, None, ALU.max)
            v_qx.inc(vec.drain())
            if tc >= 1:
                qx_evac(tc - 1)
        qx_evac(TC - 1)

    @blk.tensor
    def _(tensor):
        if STOP < 3:
            return
        tensor.wait_ge(v_vte.h, NH)
        for tc in range(TC):
            tensor.wait_ge(v_qx.h, tc + 1)
            for half in range(2):
                tensor.wait_ge(v_fme.h, 2 * TC + 2 * tc + half)
                for j in range(FH):
                    fc = half * FH + j
                    mm = tensor.transpose(
                        pT[:, j * 128:(j + 1) * 128],
                        x_tiles[:, (tc % 3) * d + fc * 128:(tc % 3) * d + fc * 128 + 128],
                        idf_sb[:, :])
                t_xtr.inc(mm)

    @blk.sync
    def _(sync):
        if STOP < 3:
            return
        sync.wait_ge(cc_b1.h, 1)
        sync.wait_ge(v_wq2.h, N2C)
        sync.wait_ge(v_rope.h, 2 * NH)
        sync.wait_ge(v_vte.h, NH)
        st_k.wait(sync, 0, 16 * NH)
        st_v.wait(sync, 0, 32 * NH)
        for hf in range(2 * N_CORES):
            r, hh = hf // 2, hf % 2
            ld_q1.gate(sync, hf % 4)
            if hf >= 4:
                sync.wait_ge(t_f1.h, min(((hf - 4) // 2 + 1) * NW1, IC))
            ld_q1.issue(sync, hf % 4,
                        q1slab[:, (hf % 4) * HSL:(hf % 4 + 1) * HSL]
                        .rearrange("p (f i) -> p f i", f=FC // 2),
                        agb1r[r, :, hh * (FC // 2):(hh + 1) * (FC // 2), :])

    @blk.tensor
    def _(tensor):
        if STOP < 3:
            return
        tensor.wait_ge(v_fme.h, 4 * TC)
        tensor.wait_ge(s_qb.h, 3 * NH * NP)
        tensor.wait_ge(v_rope.h, 2 * NH)
        for ic in range(IC):
            r = ic // NW1
            for hf in (2 * r, 2 * r + 1):
                ld_q1.wait(tensor, hf % 4, 16 * (hf // 4 + 1))
            if ic >= 2:
                tensor.wait_ge(s_g.h, ic - 1)
            pp = pX if ic % 2 == 0 else pY
            for fc in range(FC):
                slot = (2 * r + fc // (FC // 2)) % 4
                off = slot * HSL + (fc % (FC // 2)) * ISL + (ic % NW1) * 128
                wt = q1slab[:, off:off + 128]
                for (o, w) in PC:
                    mm = tensor.matmul(pp[:, o:o + w], wt,
                                       xn_fm[:, fc * T + o: fc * T + o + w],
                                       start=(fc == 0), stop=(fc == FC - 1))
            t_f1.inc(mm)

    @blk.scalar
    def _(scalar):
        if STOP < 3:
            return
        for ic in range(IC):
            scalar.wait_ge(t_f1.h, ic + 1)
            if ic == 0:
                scalar.wait_ge(v_sx.h, 1)
            if ic >= 2:
                scalar.wait_ge(v_hx.h, ic - 1)
            st_h.gate(scalar, ic % 2)
            pp = pX if ic % 2 == 0 else pY
            scalar.activation(h_sb[:, (ic % 2) * T:(ic % 2) * T + T], pp[:, :],
                              AF.Gelu, bias=b1_sb[:, ic:ic + 1],
                              scale=s1_col[:, ic:ic + 1])
            s_g.inc(scalar.drain())
            st_h.issue(scalar, ic % 2, h_sp.ap()[ic],
                       h_sb[:, (ic % 2) * T:(ic % 2) * T + T])

    @blk.vector
    def _(vec):
        if STOP < 3:
            return
        for ic in range(IC):
            vec.wait_ge(s_g.h, ic + 1)
            vec.tensor_reduce(hmax_sb[:, ic:ic + 1],
                              h_sb[:, (ic % 2) * T:(ic % 2) * T + T], AX.X,
                              ALU.max, apply_absolute_value=True)
            v_hx.inc(vec.drain())
        vec.wait_ge(t_lm.h, 1)
        vec.tensor_reduce(csc[:, 0:1], hmax_sb[:, :], AX.X, ALU.max)
        v_lm.inc(vec.drain())

    @blk.tensor
    def _(tensor):
        if STOP < 3:
            return
        tensor.wait_ge(v_lm.h, 3)
        tensor.wait_ge(v_pp1.h, ppi["w"])
        t_lm.inc(tensor.matmul(pP1[0:1, :], csc[:, 0:1], idf_sb[:, :],
                               start=True, stop=True))
        ppi["w"] += 1

    @blk.vector
    def _(vec):
        if STOP < 3:
            return
        vec.wait_ge(t_lm.h, 2)
        vec.tensor_reduce(gm2_sb[:, :], pP1[0:1, :], AX.X, ALU.max)
        vec.drain()
        vec.tensor_scalar(g2row_sb[:, :], pP1[0:1, :], gm2_sb[0:1, 0:1],
                          None, ALU.max)
        vec.drain()
        v_pp1.inc(vec.drain())
        v_lm.inc(vec.drain())

    esT.close()
    esP3.close()
    esA1a.close()
    esP4 = contextlib.ExitStack()
    P4 = lambda n, s, dt: esP4.enter_context(nc.sbuf_tensor(n, s, dt))
    k_sb = P4("k_sb", [128, 2 * S], BF16)
    v_sb = P4("v_sb", [128, 2 * S], BF16)
    exp_sb = P4("exp_sb", [128, 4 * T], BF16)
    recip_sb = P4("recip_sb", [128, T], F32)
    acc2 = P4("acc2", [128, 2 * T], F32)
    hrd_sb = P4("hrd_sb", [128, 2 * T], F32) if QA else None
    es_qha = contextlib.ExitStack()
    qh_a = es_qha.enter_context(
        nc.sbuf_tensor("qh_a", [128, QA * T], BF16, side="right")) if QA else None
    es_ao = contextlib.ExitStack()
    attn_out = es_ao.enter_context(
        nc.sbuf_tensor("attn_out", [128, NH * T], BF16, side="right"))

    _qh_b_ref = []

    def qh_sl(ic, o, w):
        if ic < QA:
            return qh_a[:, ic * T + o: ic * T + o + w]
        qb = _qh_b_ref[0]
        return qb[:, (ic - QA) * T + o: (ic - QA) * T + o + w]
    esPO = contextlib.ExitStack()
    pO = esPO.enter_context(nc.psum_tensor("pO", [128, T], F32))
    pU = [esPO.enter_context(nc.psum_tensor(f"pU{i}", [128, PC[0][1]], F32))
          for i in range(NP)]

    agcr = agc_out.ap().rearrange("(r e) -> r e", r=GROUP)
    accb = v_acc.c
    p1b = s_p1.c

    @blk.sync
    def _(sync):
        if STOP < 4:
            return
        sync.wait_ge(cc_kv.h, 1)
        sync.wait_ge(t_f1.h, IC)
        st_h.wait_all(sync)
        sync.wait_ge(v_hx.h, IC)
        for h in range(NH):
            slot = h % 2
            ld_kv.gate(sync, slot)
            if h >= 2:
                sync.wait_ge(t_O.h, h - 1)
            for r in range(GROUP):
                src = agcr[r, KOFF + h * 128 * T: KOFF + (h + 1) * 128 * T] \
                    .rearrange("(p t) -> p t", p=128)
                ld_kv.issue(sync, slot,
                            k_sb[:, slot * S + r * T: slot * S + (r + 1) * T], src)
            for r in range(GROUP):
                src = agcr[r, VOFF + h * T * 128: VOFF + (h + 1) * T * 128] \
                    .rearrange("(tc p dc) -> p tc dc", p=128, dc=128)
                dstv = v_sb[:, slot * S + r * T: slot * S + (r + 1) * T] \
                    .rearrange("p (tc dc) -> p tc dc", dc=128)
                hh = TC // 2
                for h2 in range(2):
                    ld_kv.issue(sync, slot,
                                dstv[:, h2 * hh:(h2 + 1) * hh],
                                src[:, h2 * hh:(h2 + 1) * hh])

    @blk.tensor
    def _(tensor):
        if STOP < 4:
            return
        pS = [pX, pY]
        flat = [(h, kc) for h in range(NH) for kc in range(KC)]

        def emit_scores(h, kc):
            idx = h * KC + kc
            if kc == 0:
                ld_kv.wait(tensor, h % 2, 16 * 3 * GROUP * (h // 2 + 1))
            if idx == 0:
                tensor.wait_ge(s_g.h, IC)
                tensor.wait_ge(v_pp1.h, ppi["w"])
                tensor.wait_ge(v_fme.h, 4 * TC)
            if idx >= 2:
                tensor.wait_ge(s_exp.h, idx - 1)
            kt = k_sb[:, (h % 2) * S + kc * 128: (h % 2) * S + (kc + 1) * 128]
            for (o, w) in PC:
                mm = tensor.matmul(pS[idx % 2][:, o:o + w], kt,
                                   q_rope[:, h * T + o: h * T + o + w],
                                   start=True, stop=True)
            t_S.inc(mm)

        def emit_av(h, kc):
            idx = h * KC + kc
            tensor.wait_ge(s_exp.h, idx + 1)
            if kc == 0:
                tensor.wait_ge(v_att.h, h)
                tensor.wait_ge(s_rec.h, h)
            vt = v_sb[:, (h % 2) * S + kc * 128: (h % 2) * S + (kc + 1) * 128]
            ex = exp_sb[:, (idx % 4) * T:(idx % 4) * T + T]
            for (o, w) in PC:
                mm = tensor.matmul(pO[:, o:o + w], vt, ex[:, o:o + w],
                                   start=(kc == 0), stop=(kc == KC - 1))
            t_A.inc(mm)
            if kc == KC - 1:
                tensor.wait_ge(v_acc.h, accb + (h + 1) * KC)
                for pi, (o, w) in enumerate(PC):
                    tensor.matmul(pU[pi][:, :], onesf32_sb[:, :],
                                  acc2[:, (h % 2) * T + o:(h % 2) * T + o + w],
                                  start=True, stop=True)
                t_O.inc(tensor.drain())

        emit_scores(*flat[0])
        emit_scores(*flat[1])
        for i, (h, kc) in enumerate(flat):
            if i + 2 < len(flat):
                emit_scores(*flat[i + 2])
            emit_av(h, kc)

    @blk.scalar
    def _(scalar):
        if STOP < 4:
            return
        st_h.wait_all(scalar)
        for h in range(NH):
            for kc in range(KC):
                scalar.wait_ge(t_S.h, h * KC + kc + 1)
                if h * KC + kc >= 4:
                    scalar.wait_ge(t_A.h, h * KC + kc - 3)
                    scalar.wait_ge(v_acc.h, accb + h * KC + kc - 3)
                idx = h * KC + kc
                s_exp.inc(scalar.activation(
                    exp_sb[:, (idx % 4) * T:(idx % 4) * T + T],
                    (pX if idx % 2 == 0 else pY)[:, :],
                    AF.Exp, scale=EXPS, bias=EXPB))
            scalar.wait_ge(t_O.h, h + 1)
            if h > 0:
                scalar.wait_ge(v_att.h, h)
            for pi, (o, w) in enumerate(PC):
                scalar.activation(recip_sb[:, o:o + w], pU[pi][:, :], AF.Ln)
            scalar.drain()
            s_rec.inc(scalar.activation(recip_sb[:, :], recip_sb[:, :],
                                        AF.Exp, scale=np.float32(-1.0)))

    @blk.vector
    def _(vec):
        if STOP < 4:
            return
        for h in range(NH):
            acc = acc2[:, (h % 2) * T:(h % 2) * T + T]
            for kc in range(KC):
                idx = h * KC + kc
                vec.wait_ge(s_exp.h, idx + 1)
                ex = exp_sb[:, (idx % 4) * T:(idx % 4) * T + T]
                if kc == 0:
                    vec.tensor_scalar(acc[:, :], ex[:, :], 0.0,
                                      None, ALU.add)
                else:
                    vec.tensor_tensor(acc[:, :], acc[:, :], ex[:, :], ALU.add)
                v_acc.inc(vec.drain())
            vec.wait_ge(s_rec.h, h + 1)
            vec.tensor_tensor(attn_out[:, h * T:(h + 1) * T], pO[:, :],
                              recip_sb[:, :], ALU.mult)
            v_att.inc(vec.drain())

    esP4.close()
    esA2.close()
    esQR.close()
    esPO.close()
    esXY.close()
    esQH = contextlib.ExitStack()
    qh_b = esQH.enter_context(nc.sbuf_tensor("qh_b", [128, max(QB, 1) * T], BF16))
    _qh_b_ref.append(qh_b)
    hrd2_sb = esQH.enter_context(nc.sbuf_tensor("hrd2_sb", [128, 2 * T], F32))
    h2q_sb = esQH.enter_context(nc.sbuf_tensor("h2q_sb", [128, 2 * T], F32))

    esP5 = contextlib.ExitStack()
    P5 = lambda n, s, dt: esP5.enter_context(nc.sbuf_tensor(n, s, dt))
    wo_sb = P5("wo_sb", [128, 2 * FC * 128], BF16)
    asp_sb = P5("asp_sb", [128, 2 * T], F32)
    esW = contextlib.ExitStack()
    pW0 = esW.enter_context(nc.psum_tensor("pW0", [128, T], F32))
    pW1 = esW.enter_context(nc.psum_tensor("pW1", [128, T], F32))
    pF = esW.enter_context(nc.psum_tensor("pF", [128, 1024], F32))

    def pF_sl(s):
        c = (s % 2) * 512 + (s // 2) * 128
        return pF[:, c:c + 128]

    @blk.sync
    def _(sync):
        if STOP < 5:
            return
        sync.wait_ge(t_O.h, NH)
        st_h.wait_all(sync)
        for oc in range(FC):
            ld_wo.gate(sync, oc % 2)
            if oc >= 2:
                sync.wait_ge(t_wo.h, oc - 1)
            r = (oc * 128) // OSL
            co = (oc * 128) % OSL
            dst = wo_sb[:, (oc % 2) * FC * 128:(oc % 2 + 1) * FC * 128]
            fs = FC // 4
            for q4 in range(4):
                ld_wo.issue(sync, oc % 2,
                            dst.rearrange("p (f w) -> p f w", f=FC)
                            [:, q4 * fs:(q4 + 1) * fs],
                            agar[r, 3, :, q4 * fs:(q4 + 1) * fs, co:co + 128])


    @blk.gpsimd
    def _(g):
        if STOP < 5:
            return
        g.wait_ge(t_O.h, NH)
        g.wait_ge(s_rec.h, NH)
        g.wait_ge(v_att.h, NH)
        st_h.wait_all(g)
        for ic in range(IC):
            ld_h.gate(g, ic % 2)
            if ic >= 2:
                g.wait_ge(s_p1.h, p1b + ic - 1)
            ld_h.issue(g, ic % 2,
                       hrd2_sb[:, (ic % 2) * T:(ic % 2) * T + T],
                       h_sp.ap()[ic])

    @blk.tensor
    def _(tensor):
        if STOP < 5:
            return
        for oc in range(FC):
            ld_wo.wait(tensor, oc % 2, 64 * (oc // 2 + 1))
            if oc == 0:
                tensor.wait_ge(v_att.h, NH)
                tensor.wait_ge(s_rec.h, NH)
                tensor.wait_ge(s_exp.h, NH * KC)
            if oc >= 2:
                tensor.wait_ge(s_wo.h, oc - 1)
            pp = pW0 if oc % 2 == 0 else pW1
            for fc in range(FC):
                wt = wo_sb[:, (oc % 2) * FC * 128 + fc * 128:(oc % 2) * FC * 128 + fc * 128 + 128]
                for (o, w) in PC:
                    mm = tensor.matmul(pp[:, o:o + w], wt,
                                       attn_out[:, fc * T + o: fc * T + o + w],
                                       start=(fc == 0), stop=(fc == FC - 1))
            t_wo.inc(mm)

    @blk.scalar
    def _(scalar):
        if STOP < 5:
            return
        npo = IC // FC
        for oc in range(FC):
            scalar.wait_ge(t_wo.h, oc + 1)
            st_w.gate(scalar, oc % 2)
            scalar.activation(asp_sb[:, (oc % 2) * T:(oc % 2) * T + T],
                              (pW0 if oc % 2 == 0 else pW1)[:, :], AF.Copy)
            s_wo.inc(scalar.drain())
            st_w.issue(scalar, oc % 2, attn_sp.ap()[oc],
                       asp_sb[:, (oc % 2) * T:(oc % 2) * T + T])
            for ic in range(oc * npo, (oc + 1) * npo):
                if ic == 0:
                    scalar.wait_ge(v_sx.h, 2)
                if ic >= 2:
                    scalar.wait_ge(v_qh.h, ic - 1)
                ld_h.wait(scalar, ic % 2, 16 * (ic // 2 + 1))
                s_p1.inc(scalar.activation(
                    h2q_sb[:, (ic % 2) * T:(ic % 2) * T + T],
                    hrd2_sb[:, (ic % 2) * T:(ic % 2) * T + T],
                    AF.Identity, bias=np.float32(RND_C), scale=r2_col[:, 0:1]))

    @blk.vector
    def _(vec):
        if STOP < 5:
            return
        vec.wait_ge(g_gl2.h, 16)
        vec.tensor_scalar(csc[:, 7:8], sx2_col[:, :], 1.0 / 127.0, 1e-8,
                          ALU.mult, ALU.max)
        vec.drain()
        vec.reciprocal(r2_col[:, :], csc[:, 7:8])
        vec.drain()
        vec.tensor_scalar(s2_col[:, :], sw2_col[:, :], csc[:, 7:8], None, ALU.mult)
        v_sx.inc(vec.drain())
        for ic in range(IC):
            vec.wait_ge(s_p1.h, p1b + ic + 1)
            hs = h2q_sb[:, (ic % 2) * T:(ic % 2) * T + T]
            vec.tensor_scalar(hs[:, :], hs[:, :], RND_C, 127.0,
                              ALU.subtract, ALU.min)
            vec.drain()
            vec.tensor_scalar(qh_sl(ic, 0, T), hs[:, :], -127.0, None, ALU.max)
            v_qh.inc(vec.drain())

    es_ao.close()
    esP5.close()
    esP6 = contextlib.ExitStack()
    P6 = lambda n, s, dt: esP6.enter_context(nc.sbuf_tensor(n, s, dt))
    q2_sb = P6("q2_sb", [128, 2 * IC * 128], BF16)
    f2_sb = P6("f2_sb", [128, 2 * T], F32)
    asp2_sb = P6("asp2_sb", [128, T], F32)
    fin_sb = P6("fin_sb", [128, T], F32)
    xcol_sb = P6("xcol_sb", [128, 2 * T], F32)
    outt_sb = P6("outt_sb", [128, 4 * 128], F32)

    @blk.sync
    def _(sync):
        if STOP < 6:
            return
        sync.wait_ge(s_wo.h, FC)
        st_w.wait_all(sync)
        for oc in range(FC):
            ld_q2.gate(sync, oc % 2)
            if oc >= 2:
                sync.wait_ge(t_f2.h, oc - 1)
            r = (oc * 128) // OSL
            co = (oc * 128) % OSL
            i16 = max(1, IC // 16)
            for q16 in range(IC // i16):
                ld_q2.issue(sync, oc % 2,
                            q2_sb.rearrange("p (b f w) -> p b f w", b=2, f=IC)
                            [:, oc % 2, q16 * i16:(q16 + 1) * i16],
                            agb2r[r, :, q16 * i16:(q16 + 1) * i16, co:co + 128])
            if oc >= 1:
                sync.wait_ge(v_fin.h, oc)
            ld_fa.inc(sync.dma_start(
                out=asp2_sb[:, :], in_=attn_sp.ap()[oc]), 16)
            ld_fx.gate(sync, oc % 2)
            if oc >= 2:
                sync.wait_ge(v_ot.h, (oc - 1) * TC)
            hh = TC // 2
            for h2 in range(2):
                ld_fx.issue(sync, oc % 2,
                            xcol_sb.rearrange("p (b tc w) -> p b tc w", b=2, tc=TC)
                            [:, oc % 2, h2 * hh:(h2 + 1) * hh],
                            xcolr[:, h2 * hh:(h2 + 1) * hh, oc * 128:(oc + 1) * 128])
        st_out.wait_all(sync)

    def _btr(tensor, oc):
        tensor.wait_ge(v_fin.h, oc + 1)
        for tc in range(TC):
            ti = oc * TC + tc
            if ti >= 2:
                tensor.wait_ge(v_ot.h, ti - 1)
            mm = tensor.transpose(pF_sl(ti % 4),
                                  fin_sb[:, tc * 128:(tc + 1) * 128],
                                  idf_sb[:, :])
            t_F.inc(mm)

    @blk.tensor
    def _(tensor):
        if STOP < 6:
            return
        tensor.wait_ge(cc_b2.h, 1)
        boff = min(IC // 4, max(0, IC - TC))
        bstep = max(1, (IC - boff) // TC)
        if boff + bstep * (TC - 1) >= IC:
            boff, bstep = IC - TC, 1
        bt = {boff + bstep * tc: tc for tc in range(TC)}
        for oc in range(FC):
            ld_q2.wait(tensor, oc % 2, 16 * (IC // max(1, IC // 16)) * (oc // 2 + 1))
            if oc == 0:
                tensor.wait_ge(s_wo.h, FC)
            if oc >= 2:
                tensor.wait_ge(s_f2.h, oc - 1)
            pp = pW0 if oc % 2 == 0 else pW1
            for ic in range(IC):
                if oc == 0:
                    tensor.wait_ge(v_qh.h, ic + 1)
                if oc >= 1 and ic in bt:
                    tc = bt[ic]
                    ti = (oc - 1) * TC + tc
                    if tc == 0:
                        tensor.wait_ge(v_fin.h, oc)
                    if ti >= 2:
                        tensor.wait_ge(v_ot.h, ti - 1)
                    t_F.inc(tensor.transpose(pF_sl(ti % 4),
                                             fin_sb[:, tc * 128:(tc + 1) * 128],
                                             idf_sb[:, :]))
                wt = q2_sb[:, (oc % 2) * IC * 128 + ic * 128:(oc % 2) * IC * 128 + ic * 128 + 128]
                for (o, w) in PC:
                    mm = tensor.matmul(pp[:, o:o + w], wt, qh_sl(ic, o, w),
                                       start=(ic == 0), stop=(ic == IC - 1))
            t_f2.inc(mm)
        _btr(tensor, FC - 1)

    @blk.scalar
    def _(scalar):
        if STOP < 6:
            return
        for oc in range(FC):
            scalar.wait_ge(t_f2.h, oc + 1)
            if oc == 0:
                scalar.wait_ge(v_sx.h, 2)
            if oc >= 2:
                scalar.wait_ge(v_fin.h, oc - 1)
            s_f2.inc(scalar.activation(
                f2_sb[:, (oc % 2) * T:(oc % 2) * T + T],
                (pW0 if oc % 2 == 0 else pW1)[:, :], AF.Identity,
                bias=b2_sb[:, oc:oc + 1], scale=s2_col[:, oc:oc + 1]))

    @blk.vector
    def _(vec):
        if STOP < 6:
            return
        for oc in range(FC):
            vec.wait_ge(s_f2.h, oc + 1)
            vec.wait_ge(ld_fa.h, 16 * (oc + 1))
            if oc >= 1:
                vec.wait_ge(t_F.h, oc * TC)
            vec.tensor_tensor(fin_sb[:, :],
                              f2_sb[:, (oc % 2) * T:(oc % 2) * T + T],
                              asp2_sb[:, :], ALU.add)
            v_fin.inc(vec.drain())
            ld_fx.wait(vec, oc % 2, 32 * (oc // 2 + 1))
            for tc in range(TC):
                ti = oc * TC + tc
                vec.wait_ge(t_F.h, ti + 1)
                if ti >= 4:
                    st_out.wait(vec, ti % 4, 16 * ((ti - 4) // 4 + 1))
                vec.tensor_tensor(
                    outt_sb[:, (ti % 4) * 128:(ti % 4) * 128 + 128],
                    pF_sl(ti % 4),
                    xcol_sb[:, (oc % 2) * T + tc * 128:(oc % 2) * T + (tc + 1) * 128],
                    ALU.add)
                v_ot.inc(vec.drain())

    @blk.gpsimd
    def _(g):
        if STOP < 6:
            return
        for oc in range(FC):
            for tc in range(TC):
                ti = oc * TC + tc
                g.wait_ge(v_ot.h, ti + 1)
                st_out.gate(g, ti % 4)
                st_out.issue(g, ti % 4,
                             out_t[tc * 128:(tc + 1) * 128, oc * 128:(oc + 1) * 128],
                             outt_sb[:, (ti % 4) * 128:(ti % 4) * 128 + 128])

    if STOP < 6:
        @blk.sync
        def _(sync):
            if STOP >= 1:
                st_xn.wait_all(sync)
                sync.wait_ge(v_lm.h, 2)
                sync.wait_ge(v_fme.h, 2 * TC)
            if STOP >= 2:
                st_k.wait(sync, 0, 16 * NH)
                st_v.wait(sync, 0, 32 * NH)
                sync.wait_ge(v_wq2.h, N2C)
                sync.wait_ge(v_vte.h, NH)
            if STOP >= 3:
                st_h.wait_all(sync)
                sync.wait_ge(v_lm.h, 4)
                sync.wait_ge(v_fme.h, 4 * TC)
            if STOP >= 4:
                sync.wait_ge(v_att.h, NH)
            if STOP >= 5:
                st_w.wait_all(sync)
                sync.wait_ge(v_qh.h, IC)

    esP6.close()
    esQH.close()
    es_qha.close()
    esW.close()
    es.close()
    return nc



_CFG_FULL = {"d": 2048, "I": 8192, "T": 1024, "qa": 0}
_NC_CACHE = {}


def _get_nc(key, cfg):
    if key not in _NC_CACHE:
        _NC_CACHE[key] = build_nc(dict(cfg))
    return _NC_CACHE[key]


def host_inputs(core, x_full, Wq, bq, Wk, bk, Wv, bv, Wo, w1, b1, w2, b2, cfg):
    d, I, T = cfg["d"], cfg["I"], cfg["T"]
    NH, FC, IC = d // 128, d // 128, I // 128
    OSL, ISL = d // N_CORES, I // N_CORES
    b = core // GROUP
    q0 = (core % GROUP) * T
    f32 = np.float32
    bf16 = ml_dtypes.bfloat16

    m = {}
    m["x_in"] = np.ascontiguousarray(x_full[b, q0:q0 + T, :], dtype=f32)
    m["wqT_sl"] = np.ascontiguousarray(Wq.T[:, core * OSL:(core + 1) * OSL], dtype=f32)
    m["wkT_sl"] = np.ascontiguousarray(Wk.T[:, core * OSL:(core + 1) * OSL], dtype=f32)
    m["wvT_sl"] = np.ascontiguousarray(Wv.T[:, core * OSL:(core + 1) * OSL], dtype=f32)
    m["woT_sl"] = np.ascontiguousarray(Wo.T[:, core * OSL:(core + 1) * OSL], dtype=f32)
    m["w1T_sl"] = np.ascontiguousarray(w1.T[:, core * ISL:(core + 1) * ISL], dtype=f32)
    m["w2T_sl"] = np.ascontiguousarray(w2.T[:, core * OSL:(core + 1) * OSL], dtype=f32)
    m["bq_t"] = np.ascontiguousarray(bq.reshape(NH, 128).T, dtype=f32)
    m["bk_t"] = np.ascontiguousarray(bk.reshape(NH, 128).T, dtype=f32)
    m["bv_t"] = np.ascontiguousarray(bv.reshape(NH, 128).T, dtype=f32)
    m["b1_t"] = np.ascontiguousarray(b1.reshape(IC, 128).T, dtype=f32)
    m["b2_t"] = np.ascontiguousarray(b2.reshape(FC, 128).T, dtype=f32)
    hd = 128
    inv = (1.0 / (np.float32(1000000.0) **
                  (np.arange(0, hd, 2, dtype=f32) / f32(hd)))).astype(f32)
    pos = np.arange(q0, q0 + T, dtype=f32)
    fr = np.outer(pos, inv).astype(f32)
    emb = np.concatenate([fr, fr], axis=1)
    m["cos_t"] = np.ascontiguousarray(np.cos(emb).T, dtype=f32)
    m["sin_t"] = np.ascontiguousarray(np.sin(emb).T, dtype=f32)
    m["id_bf"] = np.eye(128, dtype=bf16)
    m["id_f32"] = np.eye(128, dtype=f32)
    R = np.zeros((128, 128), dtype=f32)
    for i in range(64):
        R[i, i + 64] = -1.0
        R[i + 64, i] = 1.0
    m["rT_bf"] = np.ascontiguousarray(R.T).astype(bf16)
    m["ones_bf"] = np.ones((128, 128), dtype=bf16)
    m["ones_f32"] = np.ones((128, 128), dtype=f32)
    m["onesf"] = np.ones((1, 128), dtype=f32)
    return m


def kernel(L0_attn_hidden_states, L0_attn_Wq, L0_attn_bq, L0_attn_Wk, L0_attn_bk,
           L0_attn_Wv, L0_attn_bv, L0_attn_Wo, L0_ffn_w1, L0_ffn_b1,
           L0_ffn_w2, L0_ffn_b2):
    cfg = _CFG_FULL
    x = np.asarray(L0_attn_hidden_states, dtype=np.float32)
    args = [np.asarray(a, dtype=np.float32) for a in
            (L0_attn_Wq, L0_attn_bq, L0_attn_Wk, L0_attn_bk, L0_attn_Wv,
             L0_attn_bv, L0_attn_Wo, L0_ffn_w1, L0_ffn_b1, L0_ffn_w2, L0_ffn_b2)]
    nc = _get_nc("full", cfg)
    in_maps = [host_inputs(c, x, *args, cfg) for c in range(N_CORES)]
    res = run_bass_kernel_spmd(nc, in_maps, core_ids=list(range(N_CORES)))
    T, d = cfg["T"], cfg["d"]
    B, Sq = x.shape[0], x.shape[1]
    out = np.empty((B, Sq, d), dtype=np.float32)
    for c in range(N_CORES):
        b = c // GROUP
        q0 = (c % GROUP) * T
        out[b, q0:q0 + T, :] = np.asarray(res.results[c]["out"])
    return out


# version 17
# speedup vs baseline: 1.1103x; 1.0302x over previous
import contextlib

import numpy as np
import ml_dtypes

import concourse.bass as bass
import concourse.mybir as mybir
from concourse.bass_utils import run_bass_kernel_spmd

F32 = mybir.dt.float32
BF16 = mybir.dt.bfloat16
AF = mybir.ActivationFunctionType
ALU = mybir.AluOpType
AX = mybir.AxisListType

RND_C = 3.0 * 2.0**22
N_CORES = 8
GROUP = 4


class Sem:
    def __init__(self, nc, es, name):
        self.h = es.enter_context(nc.semaphore(name))
        self.c = 0

    def inc(self, instr, n=1):
        instr.then_inc(self.h, n)
        self.c += n
        return self.c


class DSem:

    def __init__(self, nc, es, name, nslots):
        self.s = [Sem(nc, es, f"{name}{i}") for i in range(nslots)]

    def gate(self, eng, slot):
        sm = self.s[slot]
        if sm.c > 0:
            eng.wait_ge(sm.h, sm.c)

    def issue(self, eng, slot, out, in_):
        sm = self.s[slot]
        sm.inc(eng.dma_start(out=out, in_=in_), 16)
        return sm.c

    def cnt(self, slot):
        return self.s[slot].c

    def wait(self, eng, slot, cnt):
        eng.wait_ge(self.s[slot].h, cnt)

    def wait_all(self, eng):
        for sm in self.s:
            if sm.c:
                eng.wait_ge(sm.h, sm.c)


def _pieces(T):
    n = (T + 511) // 512
    w = T // n
    assert w * n == T
    return [(i * w, w) for i in range(n)]


def build_nc(cfg):
    d = cfg["d"]
    I = cfg["I"]
    T = cfg["T"]
    NH = d // 128
    FC = d // 128
    IC = I // 128
    TC = T // 128
    S = GROUP * T
    KC = S // 128
    OSL = d // N_CORES
    ISL = I // N_CORES
    PC = _pieces(T)
    NP = len(PC)
    assert OSL % 128 == 0 and ISL % 128 == 0
    NW1 = ISL // 128
    NW2 = OSL // 128
    W2C = max(1, IC // 16)
    N2C = IC // W2C
    n_bn = d // 512
    assert n_bn * 512 == d and T % 128 == 0
    QA = min(IC, cfg.get("qa", 20))
    QB = IC - QA
    EXPS = np.float32(np.float32(1.0) / np.sqrt(np.float32(128.0)))
    EXPB = np.float32(-20.0)
    HSL = (FC // 2) * ISL
    FH = FC // 2

    STOP = cfg.get("stop", 99)
    nc = bass.Bass("TRN2", target_bir_lowering=False)

    x_in = nc.dram_tensor("x_in", [T, d], F32, kind="ExternalInput").ap()
    wqT = nc.dram_tensor("wqT_sl", [d, OSL], F32, kind="ExternalInput").ap()
    wkT = nc.dram_tensor("wkT_sl", [d, OSL], F32, kind="ExternalInput").ap()
    wvT = nc.dram_tensor("wvT_sl", [d, OSL], F32, kind="ExternalInput").ap()
    woT = nc.dram_tensor("woT_sl", [d, OSL], F32, kind="ExternalInput").ap()
    w1T = nc.dram_tensor("w1T_sl", [d, ISL], F32, kind="ExternalInput").ap()
    w2T = nc.dram_tensor("w2T_sl", [I, OSL], F32, kind="ExternalInput").ap()
    bq_t = nc.dram_tensor("bq_t", [128, NH], F32, kind="ExternalInput").ap()
    bk_t = nc.dram_tensor("bk_t", [128, NH], F32, kind="ExternalInput").ap()
    bv_t = nc.dram_tensor("bv_t", [128, NH], F32, kind="ExternalInput").ap()
    b1_t = nc.dram_tensor("b1_t", [128, IC], F32, kind="ExternalInput").ap()
    b2_t = nc.dram_tensor("b2_t", [128, FC], F32, kind="ExternalInput").ap()
    cos_t = nc.dram_tensor("cos_t", [128, T], F32, kind="ExternalInput").ap()
    sin_t = nc.dram_tensor("sin_t", [128, T], F32, kind="ExternalInput").ap()
    id_bf = nc.dram_tensor("id_bf", [128, 128], BF16, kind="ExternalInput").ap()
    id_f32 = nc.dram_tensor("id_f32", [128, 128], F32, kind="ExternalInput").ap()
    rT_bf = nc.dram_tensor("rT_bf", [128, 128], BF16, kind="ExternalInput").ap()
    ones_bf = nc.dram_tensor("ones_bf", [128, 128], BF16, kind="ExternalInput").ap()
    ones_f32 = nc.dram_tensor("ones_f32", [128, 128], F32, kind="ExternalInput").ap()
    onesf_t = nc.dram_tensor("onesf", [1, 128], F32, kind="ExternalInput").ap()
    out_t = nc.dram_tensor("out", [T, d], F32, kind="ExternalOutput").ap()

    aga_in = nc.dram_tensor("aga_in", [4 * d, OSL], BF16)
    aga_out = nc.dram_tensor("aga_out", [N_CORES * 4 * d, OSL], BF16, addr_space="Shared")
    agb1_in = nc.dram_tensor("agb1_in", [d, ISL], BF16)
    agb1_out = nc.dram_tensor("agb1_out", [N_CORES * d, ISL], BF16, addr_space="Shared")
    agb2_in = nc.dram_tensor("agb2_in", [I, OSL], BF16)
    agb2_out = nc.dram_tensor("agb2_out", [N_CORES * I, OSL], BF16, addr_space="Shared")
    ssz = ISL + OSL
    ags_in = nc.dram_tensor("ags_in", [ssz], F32)
    ags_out = nc.dram_tensor("ags_out", [N_CORES * ssz], F32, addr_space="Shared")
    m1_in = nc.dram_tensor("m1_in", [128], F32)
    m1_out = nc.dram_tensor("m1_out", [128], F32, addr_space="Shared")
    m2_in = nc.dram_tensor("m2_in", [128], F32)
    m2_out = nc.dram_tensor("m2_out", [128], F32, addr_space="Shared")
    agc_sz = NH * 128 * T * 2
    agc_in = nc.dram_tensor("agc_in", [agc_sz], BF16)
    agc_out = nc.dram_tensor("agc_out", [GROUP * agc_sz], BF16)
    xn_sp = nc.dram_tensor("xn_sp", [TC, 128, d], F32)
    h_sp = nc.dram_tensor("h_sp", [IC, 128, T], F32)
    attn_sp = nc.dram_tensor("attn_sp", [FC, 128, T], F32)
    rs_scr = nc.dram_tensor("rs_scr", [max(ISL, OSL)], F32)
    KOFF = 0
    VOFF = NH * 128 * T

    es = contextlib.ExitStack()
    SM = lambda name: Sem(nc, es, name)
    DS = lambda name, n: DSem(nc, es, name, n)
    ld_aux = SM("ld_aux")
    ld_x = DS("ld_x", 3); ld_w1 = DS("ld_w1", 2); ld_w2 = DS("ld_w2", 2)
    ld_wq = DS("ld_wq", 2); ld_wo = DS("ld_wo", 2); ld_q1 = DS("ld_q1", 4)
    ld_kv = DS("ld_kv", 2); ld_h = DS("ld_h", 2); ld_q2 = DS("ld_q2", 2)
    ld_fa = SM("ld_fa"); ld_fx = DS("ld_fx", 2); ld_sw = SM("ld_sw")
    ld_xq = DS("ld_xq", 3)
    st_out = DS("st_out", 4)
    st_xn = DS("st_xn", 3); st_k = DS("st_k", 1); st_v = DS("st_v", 1)
    st_h = DS("st_h", 2); st_w = DS("st_w", 2)
    g_dma = SM("g_dma"); g_csc = SM("g_csc")
    g_w1 = DS("g_w1", 2); g_w2 = DS("g_w2", 2)
    cc_a = SM("cc_a"); cc_b1 = SM("cc_b1"); cc_b2 = SM("cc_b2")
    cc_s = SM("cc_s"); cc_m1 = SM("cc_m1"); cc_m2 = SM("cc_m2"); cc_kv = SM("cc_kv")
    t_xtr = SM("t_xtr"); t_mx = SM("t_mx"); t_sw = SM("t_sw")
    t_p = SM("t_p"); t_r = SM("t_r"); t_vt = SM("t_vt"); t_f1 = SM("t_f1")
    t_S = SM("t_S"); t_A = SM("t_A"); t_O = SM("t_O"); t_wo = SM("t_wo")
    t_f2 = SM("t_f2"); t_F = SM("t_F")
    v_ve = SM("v_ve"); v_xnb = SM("v_xnb"); v_fme = SM("v_fme")
    v_lm = SM("v_lm"); v_mx1 = SM("v_mx1"); v_w2m = SM("v_w2m")
    v_mxr = SM("v_mxr"); v_wq1 = SM("v_wq1"); v_wq2 = SM("v_wq2")
    v_swc = SM("v_swc"); v_sx = SM("v_sx"); v_rope = SM("v_rope")
    v_vte = SM("v_vte"); v_qx = SM("v_qx"); v_hx = SM("v_hx"); v_att = SM("v_att")
    v_qh = SM("v_qh"); v_fin = SM("v_fin"); v_ot = SM("v_ot"); v_pp1 = SM("v_pp1")
    s_ln = SM("s_ln"); s_qb = SM("s_qb"); s_g = SM("s_g"); s_exp = SM("s_exp")
    s_rec = SM("s_rec"); s_wo = SM("s_wo"); s_f2 = SM("s_f2")
    v_acc = v_vte; s_p1 = s_qb
    t_pb = SM("t_pb"); t_lm = SM("t_lm")
    g_rr1 = SM("g_rr1"); g_rr2 = SM("g_rr2"); g_gl1 = SM("g_gl1"); g_gl2 = SM("g_gl2")


    sb = lambda name, shape, dt: es.enter_context(nc.sbuf_tensor(name, shape, dt))
    id_sb = sb("id_sb", [128, 128], BF16)
    idf_sb = sb("idf_sb", [128, 128], F32)
    rT_sb = sb("rT_sb", [128, 128], BF16)
    ones_sb = sb("ones_sb", [128, 128], BF16)
    onesf32_sb = sb("onesf32_sb", [128, 128], F32)
    bq_sb = sb("bq_sb", [128, NH], F32)
    bk_sb = sb("bk_sb", [128, NH], F32)
    bv_sb = sb("bv_sb", [128, NH], F32)
    b1_sb = sb("b1_sb", [128, IC], F32)
    b2_sb = sb("b2_sb", [128, FC], F32)
    sw1_col = sb("sw1_col", [128, IC], F32)
    sw2_col = sb("sw2_col", [128, FC], F32)
    s1_col = sb("s1_col", [128, IC], F32)
    s2_col = sb("s2_col", [128, FC], F32)
    onesf_sb = sb("onesf_sb", [1, 128], F32)
    g1row_sb = sb("g1row_sb", [1, 128], F32)
    g2row_sb = sb("g2row_sb", [1, 128], F32)
    gm1_sb = sb("gm1_sb", [1, 1], F32)
    gm2_sb = sb("gm2_sb", [1, 1], F32)
    sx1_col = sb("sx1_col", [128, 1], F32)
    sx2_col = sb("sx2_col", [128, 1], F32)
    r1_col = sb("r1_col", [128, 1], F32)
    r2_col = sb("r2_col", [128, 1], F32)
    csc = sb("csc", [128, 8], F32)
    mv_all = sb("mv_all", [128, 2 * TC], F32)
    ve_all = sb("ve_all", [128, TC], F32)
    rstd_all = sb("rstd_all", [128, TC], F32)
    ln_tmp = sb("ln_tmp", [128, TC], F32)
    lmax_sb = sb("lmax_sb", [128, TC], F32)
    hmax_sb = sb("hmax_sb", [128, IC], F32)

    blk = es.enter_context(nc.Block())
    ppi = {"w": 0}

    esQR = contextlib.ExitStack()
    q_rope = esQR.enter_context(nc.sbuf_tensor("q_rope", [128, NH * T], BF16))
    esA2 = contextlib.ExitStack()
    A2 = lambda n, s, dt: esA2.enter_context(nc.sbuf_tensor(n, s, dt))
    xn_fm = A2("xn_fm", [128, FC * T], BF16)
    cos_sb = A2("cos_sb", [128, T], F32)
    sin_sb = A2("sin_sb", [128, T], F32)
    esA1a = contextlib.ExitStack()
    A1a = lambda n, s, dt: esA1a.enter_context(nc.sbuf_tensor(n, s, dt))
    x_tiles = A1a("x_tiles", [128, 3 * d], F32)
    bn_scr = A1a("bn_scr", [128, 6 * n_bn], F32)
    swsl = A1a("swsl", [max(ISL, OSL) // 128, 128], F32)
    esP2 = contextlib.ExitStack()
    P2 = lambda n, s, dt: esP2.enter_context(nc.sbuf_tensor(n, s, dt))
    wsl3 = P2("wsl3", [128, 2 * 3 * FC * 128], BF16)
    qb_sb = P2("qb_sb", [128, T], BF16)
    vb_sb = P2("vb_sb", [128, T], F32)
    vtm_sb = P2("vtm_sb", [128, T], BF16)
    kr_sb = P2("kr_sb", [128, T], BF16)
    m1r_sb = P2("m1r_sb", [128, T], F32)
    m2r_sb = P2("m2r_sb", [128, T], F32)
    esA1b = contextlib.ExitStack()
    A1b = lambda n, s, dt: esA1b.enter_context(nc.sbuf_tensor(n, s, dt))
    w1p = A1b("w1p", [128, 2 * FC * 128], F32)
    w1q_bf = A1b("w1q_bf", [128, 2 * FC * 128], BF16)
    w2p = A1b("w2p", [128, 2 * W2C * OSL], F32)
    w2q_bf = A1b("w2q_bf", [128, 2 * W2C * OSL], BF16)
    mxw = A1b("mxw", [128, max(128, OSL)], F32)
    mnw = A1b("mnw", [128, max(128, OSL)], F32)
    mxq = A1b("mxq", [128, max(128, OSL)], F32)
    rep1 = A1b("rep1", [128, 128], F32)
    rep2 = A1b("rep2", [128, OSL], F32)
    rr1 = A1b("rr1", [1, 128], F32)
    rr2 = A1b("rr2", [1, OSL], F32)
    esXY = contextlib.ExitStack()
    pX = esXY.enter_context(nc.psum_tensor("pX", [128, T], F32))
    pY = esXY.enter_context(nc.psum_tensor("pY", [128, T], F32))
    esT = contextlib.ExitStack()
    pT = esT.enter_context(nc.psum_tensor("pT", [128, 1024], F32))
    pP1 = esT.enter_context(nc.psum_tensor("pP1", [128, 128], F32))

    w1r = w1T.rearrange("(f p) i -> p f i", p=128)
    w2r = w2T.rearrange("(f p) o -> p f o", p=128)
    agar = aga_out.ap().rearrange("(r j f p) o -> r j p f o", j=4, f=FC, p=128)
    agb1r = agb1_out.ap().rearrange("(r f p) i -> r p f i", f=FC, p=128)
    agb2r = agb2_out.ap().rearrange("(r f p) o -> r p f o", f=IC, p=128)
    agb1w = agb1_in.ap().rearrange("(f p) i -> p f i", p=128)
    agb2w = agb2_in.ap().rearrange("(c f p) o -> c p f o", f=W2C, p=128)
    xcolr = x_in.rearrange("(tc p) o -> p tc o", p=128)

    units = ([("w1", j) for j in range(NW1)]
             + [("w2m", j) for j in range(N2C)]
             + [("w2s", j) for j in range(NW2)]
             + [("pb2", j) for j in range(NW2)]
             + [("w2q", j) for j in range(N2C)])
    NHP = max(1, NH // 2)
    per_head = [units[(len(units) * h) // NHP:(len(units) * (h + 1)) // NHP]
                if h < NHP else [] for h in range(NH)]


    @blk.sync
    def _(sync):
        if STOP < 1:
            return
        for src, dst in [(cos_t, cos_sb), (sin_t, sin_sb), (id_bf, id_sb),
                         (id_f32, idf_sb), (rT_bf, rT_sb), (ones_bf, ones_sb),
                         (ones_f32, onesf32_sb),
                         (bq_t, bq_sb), (bk_t, bk_sb), (bv_t, bv_sb),
                         (b1_t, b1_sb), (b2_t, b2_sb), (onesf_t, onesf_sb)]:
            ld_aux.inc(sync.dma_start(out=dst[:, :], in_=src), 16)
        for tc in range(TC):
            sl = tc % 3
            ld_x.gate(sync, sl)
            if tc >= 3:
                st_xn.wait(sync, (tc - 3) % 3, 16 * ((tc - 3) // 3 + 1))
                sync.wait_ge(t_xtr.h, 2 * (tc - 2))
            ld_x.issue(sync, sl,
                       x_tiles[:, sl * d:sl * d + d],
                       x_in[tc * 128:(tc + 1) * 128, :])

    @blk.gpsimd
    def _(g):
        if STOP < 2:
            return
        for j, w in enumerate([wqT, wkT, wvT, woT]):
            g_dma.inc(g.dma_start(out=aga_in[j * d:j * d + d // 2, :],
                                  in_=w[0:d // 2, :]), 16)
            g_dma.inc(g.dma_start(out=aga_in[j * d + d // 2:(j + 1) * d, :],
                                  in_=w[d // 2:d, :]), 16)
        g.wait_ge(g_dma.h, g_dma.c)
        cc_a.inc(g.collective_compute(
            "AllGather", ALU.bypass, replica_groups=[list(range(N_CORES))],
            ins=[aga_in.ap().opt()], outs=[aga_out.ap().opt()]))
        for j in range(NW1):
            g.wait_ge(v_mxr.h, j + 1)
            g_csc.inc(g.dma_start(out=rs_scr[j * 128:(j + 1) * 128],
                                  in_=csc[:, 2 + (j % 2):3 + (j % 2)]), 16)
            g_csc.inc(g.dma_start(out=ags_in[j * 128:(j + 1) * 128],
                                  in_=csc[:, 4 + (j % 2):5 + (j % 2)]), 16)
            g.wait_ge(g_csc.h, g_csc.c)
            if j >= 1:
                g.wait_ge(t_pb.h, j)
            g_rr1.inc(g.dma_start(out=rr1[:, :],
                                  in_=rs_scr[j * 128:(j + 1) * 128]), 16)
            g.wait_ge(v_wq1.h, j + 1)
            for q4 in range(4):
                fs = FC // 4
                g_w1.issue(g, j % 2,
                           agb1w[:, q4 * fs:(q4 + 1) * fs, j * 128:(j + 1) * 128],
                           w1q_bf.rearrange("p (b f w) -> p b f w", b=2, f=FC)
                           [:, j % 2, q4 * fs:(q4 + 1) * fs])
        g.wait_ge(g_dma.h, g_dma.c)
        g_w1.wait_all(g)
        cc_b1.inc(g.collective_compute(
            "AllGather", ALU.bypass, replica_groups=[list(range(N_CORES))],
            ins=[agb1_in.ap().opt()], outs=[agb1_out.ap().opt()]))
        for j in range(NW2):
            g.wait_ge(v_mxr.h, NW1 + j + 1)
            g_csc.inc(g.dma_start(out=rs_scr[j * 128:(j + 1) * 128],
                                  in_=csc[:, 2 + (j % 2):3 + (j % 2)]), 16)
            g_csc.inc(g.dma_start(out=ags_in[ISL + j * 128:ISL + (j + 1) * 128],
                                  in_=csc[:, 4 + (j % 2):5 + (j % 2)]), 16)
        g.wait_ge(g_csc.h, g_csc.c)
        g_rr2.inc(g.dma_start(out=rr2[:, :], in_=rs_scr[0:OSL]), 16)
        for j in range(N2C):
            g.wait_ge(v_wq2.h, j + 1)
            g_w2.issue(g, j % 2,
                       agb2w[j],
                       w2q_bf.rearrange("p (b f w) -> p b f w", b=2, f=W2C)[:, j % 2])
        g_w2.wait_all(g)
        cc_b2.inc(g.collective_compute(
            "AllGather", ALU.bypass, replica_groups=[list(range(N_CORES))],
            ins=[agb2_in.ap().opt()], outs=[agb2_out.ap().opt()]))
        cc_s.inc(g.collective_compute(
            "AllGather", ALU.bypass, replica_groups=[list(range(N_CORES))],
            ins=[ags_in.ap().opt()], outs=[ags_out.ap().opt()]))
        if STOP < 3:
            return
        g.wait_ge(v_lm.h, 2)
        g_dma.inc(g.dma_start(out=m1_in[0:128], in_=g1row_sb[:, :]), 16)
        g.wait_ge(g_dma.h, g_dma.c)
        cc_m1.inc(g.collective_compute(
            "AllReduce", ALU.max, replica_groups=[list(range(N_CORES))],
            ins=[m1_in.ap().opt()], outs=[m1_out.ap().opt()]))
        g.wait_ge(cc_m1.h, 1)
        g_gl1.inc(g.dma_start(out=sx1_col[:, :],
                              in_=m1_out.ap().rearrange("(p o) -> p o", o=1)), 16)
        if STOP < 4:
            pass
        st_k.wait(g, 0, 16 * NH)
        st_v.wait(g, 0, 32 * NH)
        cc_kv.inc(g.collective_compute(
            "AllGather", ALU.bypass,
            replica_groups=[list(range(G * GROUP, (G + 1) * GROUP))
                            for G in range(N_CORES // GROUP)],
            ins=[agc_in.ap().opt()], outs=[agc_out.ap().opt()]))
        g.wait_ge(v_lm.h, 4)
        g_dma.inc(g.dma_start(out=m2_in[0:128], in_=g2row_sb[:, :]), 16)
        g.wait_ge(g_dma.h, g_dma.c)
        cc_m2.inc(g.collective_compute(
            "AllReduce", ALU.max, replica_groups=[list(range(N_CORES))],
            ins=[m2_in.ap().opt()], outs=[m2_out.ap().opt()]))
        g.wait_ge(cc_m2.h, 1)
        g_gl2.inc(g.dma_start(out=sx2_col[:, :],
                              in_=m2_out.ap().rearrange("(p o) -> p o", o=1)), 16)

    @blk.scalar
    def _(scalar):
        if STOP < 1:
            return
        for tc in range(TC):
            scalar.wait_ge(v_ve.h, tc + 1)
            scalar.activation(ln_tmp[:, tc:tc + 1], ve_all[:, tc:tc + 1], AF.Ln)
            scalar.drain()
            s_ln.inc(scalar.activation(rstd_all[:, tc:tc + 1],
                                       ln_tmp[:, tc:tc + 1], AF.Exp, scale=np.float32(-0.5)))
            if tc >= 1:
                scalar.wait_ge(v_xnb.h, tc)
                st_xn.gate(scalar, (tc - 1) % 3)
                st_xn.issue(scalar, (tc - 1) % 3, xn_sp.ap()[tc - 1],
                            x_tiles[:, ((tc - 1) % 3) * d:((tc - 1) % 3) * d + d])
        scalar.wait_ge(v_xnb.h, TC)
        st_xn.gate(scalar, (TC - 1) % 3)
        st_xn.issue(scalar, (TC - 1) % 3, xn_sp.ap()[TC - 1],
                    x_tiles[:, ((TC - 1) % 3) * d:((TC - 1) % 3) * d + d])

    @blk.vector
    def _(vec):
        if STOP < 1:
            return
        def norm(tc):
            vec.wait_ge(s_ln.h, tc + 1)
            xt = x_tiles[:, (tc % 3) * d:(tc % 3) * d + d]
            vec.tensor_scalar(xt[:, :], xt[:, :], mv_all[:, 2 * tc:2 * tc + 1],
                              rstd_all[:, tc:tc + 1], ALU.subtract, ALU.mult)
            vec.drain()
            vec.tensor_reduce(lmax_sb[:, tc:tc + 1], xt[:, :], AX.X, ALU.max,
                              apply_absolute_value=True)
            v_xnb.inc(vec.drain())
        ALT = T >= FH * 128

        def xn_evac(jt):
            for half in range(2):
                u = 2 * jt + half
                vec.wait_ge(t_xtr.h, u + 1)
                buf = pX if (ALT and u % 2 == 1) else pT
                for jj in range(FH):
                    fc = half * FH + jj
                    vec.tensor_copy(xn_fm[:, fc * T + jt * 128: fc * T + (jt + 1) * 128],
                                    buf[:, jj * 128:(jj + 1) * 128])
                v_fme.inc(vec.drain())
        for tc in range(TC):
            ld_x.wait(vec, tc % 3, 16 * (tc // 3 + 1))
            xt = x_tiles[:, (tc % 3) * d:(tc % 3) * d + d]
            for j in range(n_bn):
                vec.bn_stats(bn_scr[:, 6 * j:6 * j + 6], xt[:, 512 * j:512 * (j + 1)])
            vec.drain()
            vec.bn_aggr(mv_all[:, 2 * tc:2 * tc + 2], bn_scr[:, :])
            vec.drain()
            vec.tensor_scalar(ve_all[:, tc:tc + 1], mv_all[:, 2 * tc + 1:2 * tc + 2],
                              1e-5, None, ALU.add)
            v_ve.inc(vec.drain())
            if tc >= 1:
                norm(tc - 1)
            if tc >= 2:
                xn_evac(tc - 2)
        norm(TC - 1)
        for jt in range(max(0, TC - 2), TC):
            xn_evac(jt)
        vec.tensor_reduce(csc[:, 0:1], lmax_sb[:, :], AX.X, ALU.max)
        v_lm.inc(vec.drain())

    @blk.tensor
    def _(tensor):
        if STOP < 1:
            return
        tensor.wait_ge(ld_aux.h, ld_aux.c)
        ALT = T >= FH * 128
        for tc in range(TC):
            tensor.wait_ge(v_xnb.h, tc + 1)
            for half in range(2):
                u = 2 * tc + half
                buf = pX if (ALT and u % 2 == 1) else pT
                lag = 1 if ALT else 0
                if u >= 1 + lag:
                    tensor.wait_ge(v_fme.h, u - lag)
                for j in range(FH):
                    fc = half * FH + j
                    mm = tensor.transpose(
                        buf[:, j * 128:(j + 1) * 128],
                        x_tiles[:, (tc % 3) * d + fc * 128:(tc % 3) * d + fc * 128 + 128],
                        idf_sb[:, :])
                t_xtr.inc(mm)
        tensor.wait_ge(v_lm.h, 1)
        tensor.wait_ge(v_pp1.h, ppi["w"])
        t_lm.inc(tensor.matmul(pP1[0:1, :], csc[:, 0:1], idf_sb[:, :],
                               start=True, stop=True))
        ppi["w"] += 1

    @blk.vector
    def _(vec):
        if STOP < 1:
            return
        vec.wait_ge(t_lm.h, 1)
        vec.tensor_reduce(gm1_sb[:, :], pP1[0:1, :], AX.X, ALU.max)
        vec.drain()
        vec.tensor_scalar(g1row_sb[:, :], pP1[0:1, :], gm1_sb[0:1, 0:1],
                          None, ALU.max)
        vec.drain()
        v_pp1.inc(vec.drain())
        v_lm.inc(vec.drain())

    def _unit_load(sync, kind, j):
        if kind == "w1":
            ld_w1.gate(sync, j % 2)
            if j >= 2:
                sync.wait_ge(v_wq1.h, j - 1)
            for q4 in range(4):
                fs = FC // 4
                ld_w1.issue(sync, j % 2,
                            w1p.rearrange("p (b f w) -> p b f w", b=2, f=FC)
                            [:, j % 2, q4 * fs:(q4 + 1) * fs],
                            w1r[:, q4 * fs:(q4 + 1) * fs, j * 128:(j + 1) * 128])
        elif kind == "w2m":
            ld_w2.gate(sync, j % 2)
            if j >= 2:
                sync.wait_ge(v_w2m.h, j - 1)
            ld_w2.issue(sync, j % 2,
                        w2p.rearrange("p (b f w) -> p b f w", b=2, f=W2C)[:, j % 2],
                        w2r[:, j * W2C:(j + 1) * W2C, :])
        elif kind == "w2q":
            ld_w2.gate(sync, j % 2)
            if j == 0:
                sync.wait_ge(v_mx1.h, NW1 + 1)
            if j >= 2:
                sync.wait_ge(v_wq2.h, j - 1)
            ld_w2.issue(sync, j % 2,
                        w2p.rearrange("p (b f w) -> p b f w", b=2, f=W2C)[:, j % 2],
                        w2r[:, j * W2C:(j + 1) * W2C, :])

    @blk.sync
    def _(sync):
        if STOP < 2:
            return
        sync.wait_ge(cc_a.h, 1)
        for kind, j in per_head[0]:
            _unit_load(sync, kind, j)
        for h in range(NH):
            if h >= 2:
                sync.wait_ge(t_p.h, 3 * NP * (h - 1))
            base = (h % 2) * 3 * FC * 128
            r = (h * 128) // OSL
            co = (h * 128) % OSL
            ld_wq.gate(sync, h % 2)
            fs = FC // 4
            for pj in range(3):
                dst = wsl3[:, base + pj * FC * 128: base + (pj + 1) * FC * 128]
                for q4 in range(4):
                    ld_wq.issue(sync, h % 2,
                                dst.rearrange("p (f w) -> p f w", f=FC)
                                [:, q4 * fs:(q4 + 1) * fs],
                                agar[r, pj, :, q4 * fs:(q4 + 1) * fs, co:co + 128])
            if h + 1 < NH:
                for kind, j in per_head[h + 1]:
                    _unit_load(sync, kind, j)

    @blk.tensor
    def _(tensor):
        if STOP < 2:
            return
        nmx = 0
        for h in range(NH):
            for kind, j in per_head[h]:
                if kind == "w1":
                    tensor.wait_ge(v_mx1.h, nmx + 1)
                    tensor.wait_ge(v_pp1.h, ppi["w"])
                    t_mx.inc(tensor.transpose(pP1[:, :], mxw[:, 0:128], idf_sb[:, :]))
                    ppi["w"] += 1
                    nmx += 1
                    tensor.wait_ge(g_rr1.h, 16 * (j + 1))
                    tensor.wait_ge(v_pp1.h, ppi["w"])
                    t_pb.inc(tensor.matmul(pP1[:, :], onesf_sb[:, :], rr1[:, :],
                                            start=True, stop=True))
                    ppi["w"] += 1
                elif kind == "w2s":
                    tensor.wait_ge(v_mx1.h, nmx + 1)
                    tensor.wait_ge(v_pp1.h, ppi["w"])
                    t_mx.inc(tensor.transpose(pP1[:, :], mxw[:, j * 128:(j + 1) * 128],
                                               idf_sb[:, :]))
                    ppi["w"] += 1
                    nmx += 1
                elif kind == "pb2":
                    tensor.wait_ge(g_rr2.h, 16)
                    tensor.wait_ge(v_pp1.h, ppi["w"])
                    t_pb.inc(tensor.matmul(pP1[:, :], onesf_sb[:, :],
                                            rr2[:, j * 128:(j + 1) * 128],
                                            start=True, stop=True))
                    ppi["w"] += 1
            base = (h % 2) * 3 * FC * 128
            ld_wq.wait(tensor, h % 2, 192 * (h // 2 + 1))
            if h == 0:
                tensor.wait_ge(v_fme.h, 2 * TC)
            for m in range(3):
                woff = base + m * FC * 128
                for pi, (o, w) in enumerate(PC):
                    k = (3 * h + m) * NP + pi
                    if k >= 1:
                        tensor.wait_ge(s_qb.h, k - 1)
                    for fc in range(FC):
                        wt = wsl3[:, woff + fc * 128: woff + fc * 128 + 128]
                        mm = tensor.matmul(pX[:, o:o + w], wt,
                                           xn_fm[:, fc * T + o: fc * T + o + w],
                                           start=(fc == 0), stop=(fc == FC - 1))
                    t_p.inc(mm)
                if m < 2:
                    tensor.wait_ge(s_qb.h, (3 * h + m + 1) * NP)
                    if m == 0 and h > 0:
                        tensor.wait_ge(v_rope.h, 2 * h)
                    if m == 1:
                        tensor.wait_ge(v_rope.h, 2 * h + 1)
                    for (o, w) in PC:
                        mm = tensor.matmul(pY[:, o:o + w], rT_sb[:, :],
                                           qb_sb[:, o:o + w],
                                           start=True, stop=True)
                    t_r.inc(mm)
            tensor.wait_ge(s_qb.h, (3 * h + 3) * NP)
            tensor.wait_ge(v_vte.h, h)
            if h == 0:
                tensor.wait_ge(v_fme.h, 2 * TC)
            for tc in range(TC):
                mm = tensor.transpose(pT[:, tc * 128:(tc + 1) * 128],
                                      vb_sb[:, tc * 128:(tc + 1) * 128], idf_sb[:, :])
            t_vt.inc(mm)

    @blk.scalar
    def _(scalar):
        if STOP < 2:
            return
        scalar.wait_ge(ld_aux.h, ld_aux.c)
        for h in range(NH):
            for m in range(3):
                bias = (bq_sb, bk_sb, bv_sb)[m]
                dst = vb_sb if m == 2 else qb_sb
                for pi, (o, w) in enumerate(PC):
                    scalar.wait_ge(t_p.h, (3 * h + m) * NP + pi + 1)
                    if pi == 0:
                        if m == 0 and h > 0:
                            scalar.wait_ge(v_rope.h, 2 * h)
                        if m == 1:
                            scalar.wait_ge(v_rope.h, 2 * h + 1)
                        if m == 2 and h > 0:
                            scalar.wait_ge(t_vt.h, h)
                    s_qb.inc(scalar.activation(dst[:, o:o + w], pX[:, o:o + w],
                                               AF.Identity,
                                               bias=bias[:, h:h + 1]))
            scalar.wait_ge(v_rope.h, 2 * h + 2)
            st_k.gate(scalar, 0)
            st_k.issue(scalar, 0,
                       agc_in.ap()[KOFF + h * 128 * T: KOFF + (h + 1) * 128 * T]
                       .rearrange("(p t) -> p t", p=128),
                       kr_sb[:, :])
            scalar.wait_ge(v_vte.h, h + 1)
            st_v.gate(scalar, 0)
            hh = TC // 2
            for h2 in range(2):
                st_v.issue(scalar, 0,
                           agc_in.ap()[VOFF + h * T * 128: VOFF + (h + 1) * T * 128]
                           .rearrange("(tc p dc) -> p tc dc", p=128, dc=128)
                           [:, h2 * hh:(h2 + 1) * hh],
                           vtm_sb.ap().rearrange("p (tc dc) -> p tc dc", dc=128)
                           [:, h2 * hh:(h2 + 1) * hh])

    @blk.vector
    def _(vec):
        if STOP < 2:
            return
        vec.wait_ge(ld_aux.h, ld_aux.c)
        nmx_v = 0
        w2fold = {"n": 0}
        for h in range(NH):
            for kind, j in per_head[h]:
                if kind == "w1":
                    ld_w1.wait(vec, j % 2, 64 * (j // 2 + 1))
                    src = w1p.rearrange("p (b f w) -> p b f w", b=2, f=FC)[:, j % 2]
                    vec.tensor_tensor(mxw[:, 0:128], src[:, 0], src[:, 1], ALU.max)
                    vec.tensor_tensor(mnw[:, 0:128], src[:, 0], src[:, 1], ALU.min)
                    for fc in range(2, FC):
                        vec.drain()
                        vec.tensor_tensor(mxw[:, 0:128], mxw[:, 0:128], src[:, fc],
                                          ALU.max)
                        vec.tensor_tensor(mnw[:, 0:128], mnw[:, 0:128], src[:, fc],
                                          ALU.min)
                    vec.drain()
                    vec.tensor_scalar(mnw[:, 0:128], mnw[:, 0:128], -1.0, None, ALU.mult)
                    vec.drain()
                    vec.tensor_tensor(mxw[:, 0:128], mxw[:, 0:128], mnw[:, 0:128],
                                      ALU.max)
                    v_mx1.inc(vec.drain())
                    nmx_v += 1
                    vec.wait_ge(t_mx.h, nmx_v)
                    vec.tensor_reduce(csc[:, 1:2], pP1[:, :], AX.X, ALU.max)
                    v_pp1.inc(vec.drain())
                    if j >= 2:
                        vec.wait_ge(g_csc.h, 32 * (j - 1))
                    vec.tensor_scalar(csc[:, 4 + (j % 2):5 + (j % 2)], csc[:, 1:2],
                                      1.0 / 127.0, 1e-8, ALU.mult, ALU.max)
                    vec.drain()
                    vec.reciprocal(csc[:, 2 + (j % 2):3 + (j % 2)],
                                   csc[:, 4 + (j % 2):5 + (j % 2)])
                    v_mxr.inc(vec.drain())
                    vec.wait_ge(t_pb.h, j + 1)
                    vec.tensor_copy(rep1[:, :], pP1[:, :])
                    vec.drain()
                    v_pp1.inc(vec.drain())
                    if j >= 2:
                        g_w1.wait(vec, j % 2, 64 * (j // 2))
                    dstv = w1q_bf.rearrange("p (b f w) -> p b f w", b=2, f=FC)[:, j % 2]
                    for fc in range(FC):
                        vec.tensor_tensor(mxq[:, 0:128], src[:, fc], rep1[:, :], ALU.mult)
                        vec.drain()
                        vec.tensor_scalar(mxq[:, 0:128], mxq[:, 0:128],
                                          RND_C, RND_C, ALU.add, ALU.subtract)
                        vec.drain()
                        vec.tensor_scalar(dstv[:, fc], mxq[:, 0:128],
                                          127.0, -127.0, ALU.min, ALU.max)
                        vec.drain()
                    v_wq1.inc(vec.drain())
                elif kind == "w2m":
                    ld_w2.wait(vec, j % 2, 16 * (j // 2 + 1))
                    if j == 0 and NW1 > 0:
                        vec.wait_ge(t_mx.h, NW1)
                    src = w2p.rearrange("p (b f w) -> p b f w", b=2, f=W2C)[:, j % 2]
                    for f in range(W2C):
                        if j == 0 and f == 0:
                            vec.tensor_tensor(mxw[:, 0:OSL], src[:, 0],
                                              src[:, min(1, W2C - 1)], ALU.max)
                            vec.tensor_tensor(mnw[:, 0:OSL], src[:, 0],
                                              src[:, min(1, W2C - 1)], ALU.min)
                            vec.drain()
                            continue
                        if j == 0 and f == 1 and W2C > 1:
                            continue
                        vec.tensor_tensor(mxw[:, 0:OSL], mxw[:, 0:OSL], src[:, f],
                                          ALU.max)
                        vec.tensor_tensor(mnw[:, 0:OSL], mnw[:, 0:OSL], src[:, f],
                                          ALU.min)
                        vec.drain()
                    v_w2m.inc(vec.drain())
                    w2fold["n"] += 1
                    if w2fold["n"] == N2C:
                        vec.tensor_scalar(mnw[:, 0:OSL], mnw[:, 0:OSL], -1.0,
                                          None, ALU.mult)
                        vec.drain()
                        vec.tensor_tensor(mxw[:, 0:OSL], mxw[:, 0:OSL],
                                          mnw[:, 0:OSL], ALU.max)
                        v_mx1.inc(vec.drain())
                elif kind == "w2s":
                    if j > 0:
                        v_mx1.inc(vec.drain())
                    nmx_v += 1
                    vec.wait_ge(t_mx.h, nmx_v)
                    vec.tensor_reduce(csc[:, 1:2], pP1[:, :], AX.X, ALU.max)
                    v_pp1.inc(vec.drain())
                    vec.wait_ge(g_csc.h, 32 * NW1)
                    vec.tensor_scalar(csc[:, 4 + (j % 2):5 + (j % 2)], csc[:, 1:2],
                                      1.0 / 127.0, 1e-8, ALU.mult, ALU.max)
                    vec.drain()
                    vec.reciprocal(csc[:, 2 + (j % 2):3 + (j % 2)],
                                   csc[:, 4 + (j % 2):5 + (j % 2)])
                    v_mxr.inc(vec.drain())
                elif kind == "pb2":
                    vec.wait_ge(t_pb.h, NW1 + j + 1)
                    vec.tensor_copy(rep2[:, j * 128:(j + 1) * 128], pP1[:, :])
                    vec.drain()
                    v_pp1.inc(vec.drain())
                else:
                    ld_w2.wait(vec, j % 2, 16 * (N2C // 2 + j // 2 + 1))
                    if j >= 2:
                        g_w2.wait(vec, j % 2, 16 * (j // 2))
                    src = w2p.rearrange("p (b f w) -> p b f w", b=2, f=W2C)[:, j % 2]
                    dstv = w2q_bf.rearrange("p (b f w) -> p b f w", b=2, f=W2C)[:, j % 2]
                    for f in range(W2C):
                        vec.tensor_tensor(mxq[:, 0:OSL], src[:, f], rep2[:, :], ALU.mult)
                        vec.drain()
                        vec.tensor_scalar(mxq[:, 0:OSL], mxq[:, 0:OSL],
                                          RND_C, RND_C, ALU.add, ALU.subtract)
                        vec.drain()
                        vec.tensor_scalar(dstv[:, f], mxq[:, 0:OSL],
                                          127.0, -127.0, ALU.min, ALU.max)
                        vec.drain()
                    v_wq2.inc(vec.drain())
            vec.wait_ge(s_qb.h, (3 * h + 1) * NP)
            vec.wait_ge(t_r.h, 2 * h + 1)
            vec.tensor_tensor(m1r_sb[:, :], qb_sb[:, :], cos_sb[:, :], ALU.mult)
            vec.tensor_tensor(m2r_sb[:, :], pY[:, :], sin_sb[:, :], ALU.mult)
            vec.drain()
            vec.tensor_tensor(q_rope[:, h * T:(h + 1) * T], m1r_sb[:, :],
                              m2r_sb[:, :], ALU.add)
            v_rope.inc(vec.drain())
            vec.wait_ge(s_qb.h, (3 * h + 2) * NP)
            vec.wait_ge(t_r.h, 2 * h + 2)
            if h >= 1:
                st_k.wait(vec, 0, 16 * h)
            vec.tensor_tensor(m1r_sb[:, :], qb_sb[:, :], cos_sb[:, :], ALU.mult)
            vec.tensor_tensor(m2r_sb[:, :], pY[:, :], sin_sb[:, :], ALU.mult)
            vec.drain()
            vec.tensor_tensor(kr_sb[:, :], m1r_sb[:, :], m2r_sb[:, :], ALU.add)
            v_rope.inc(vec.drain())
            vec.wait_ge(t_vt.h, h + 1)
            if h >= 1:
                st_v.wait(vec, 0, 32 * h)
            vec.tensor_copy(vtm_sb[:, :], pT[:, 0:T])
            v_vte.inc(vec.drain())

    esA1b.close()
    esP2.close()
    esP3 = contextlib.ExitStack()
    q1slab = esP3.enter_context(nc.sbuf_tensor("q1slab", [128, 4 * HSL], BF16))
    h_sb = esP3.enter_context(nc.sbuf_tensor("h_sb", [128, 2 * T], F32))

    K1 = ISL // 128
    K2 = OSL // 128

    @blk.sync
    def _(sync):
        if STOP < 3:
            return
        sync.wait_ge(cc_s.h, 1)
        for r in range(2 * N_CORES):
            if r >= 1:
                sync.wait_ge(t_sw.h, r)
            if r < N_CORES:
                src = ags_out.ap()[r * ssz: r * ssz + ISL].rearrange("(k p) -> k p", p=128)
                ld_sw.inc(sync.dma_start(out=swsl[0:K1, :], in_=src), 16)
            else:
                rr = r - N_CORES
                src = ags_out.ap()[rr * ssz + ISL: (rr + 1) * ssz].rearrange("(k p) -> k p", p=128)
                ld_sw.inc(sync.dma_start(out=swsl[0:K2, :], in_=src), 16)

    @blk.tensor
    def _(tensor):
        if STOP < 3:
            return
        for r in range(2 * N_CORES):
            k = K1 if r < N_CORES else K2
            tensor.wait_ge(ld_sw.h, 16 * (r + 1))
            tensor.wait_ge(v_pp1.h, ppi["w"])
            t_sw.inc(tensor.transpose(pP1[:, 0:k], swsl[0:k, :], idf_sb[0:k, 0:k]))
            ppi["w"] += 1

    @blk.vector
    def _(vec):
        if STOP < 3:
            return
        for r in range(2 * N_CORES):
            vec.wait_ge(t_sw.h, r + 1)
            if r < N_CORES:
                vec.tensor_copy(sw1_col[:, r * K1:(r + 1) * K1], pP1[:, 0:K1])
            else:
                rr = r - N_CORES
                vec.tensor_copy(sw2_col[:, rr * K2:(rr + 1) * K2], pP1[:, 0:K2])
            vec.drain()
            v_swc.inc(vec.drain())
            v_pp1.inc(vec.drain())
        vec.wait_ge(g_gl1.h, 16)
        vec.tensor_scalar(csc[:, 6:7], sx1_col[:, :], 1.0 / 127.0, 1e-8,
                          ALU.mult, ALU.max)
        vec.drain()
        vec.reciprocal(r1_col[:, :], csc[:, 6:7])
        vec.drain()
        vec.tensor_scalar(s1_col[:, :], sw1_col[:, :], csc[:, 6:7], None, ALU.mult)
        v_sx.inc(vec.drain())

    @blk.sync
    def _(sync):
        if STOP < 3:
            return
        st_xn.wait_all(sync)
        sync.wait_ge(t_xtr.h, 2 * TC)
        for tc in range(TC):
            ld_xq.gate(sync, tc % 3)
            if tc >= 3:
                sync.wait_ge(t_xtr.h, 2 * TC + 2 * (tc - 2))
            ld_xq.issue(sync, tc % 3,
                        x_tiles[:, (tc % 3) * d:(tc % 3) * d + d],
                        xn_sp.ap()[tc])

    @blk.vector
    def _(vec):
        if STOP < 3:
            return
        ALT = T >= FH * 128

        def qx_evac(jt):
            for half in range(2):
                u = 2 * TC + 2 * jt + half
                vec.wait_ge(t_xtr.h, u + 1)
                buf = pX if (ALT and u % 2 == 1) else pT
                for jj in range(FH):
                    fc = half * FH + jj
                    vec.tensor_copy(xn_fm[:, fc * T + jt * 128: fc * T + (jt + 1) * 128],
                                    buf[:, jj * 128:(jj + 1) * 128])
                v_fme.inc(vec.drain())
        vec.wait_ge(t_p.h, 3 * NH * NP)
        for tc in range(TC):
            ld_xq.wait(vec, tc % 3, 16 * (tc // 3 + 1))
            xo = x_tiles[:, (tc % 3) * d:(tc % 3) * d + d]
            vec.tensor_scalar(xo[:, :], xo[:, :], r1_col[:, :], RND_C,
                              ALU.mult, ALU.add)
            vec.drain()
            vec.tensor_scalar(xo[:, :], xo[:, :], RND_C, 127.0,
                              ALU.subtract, ALU.min)
            vec.drain()
            vec.tensor_scalar(xo[:, :], xo[:, :], -127.0, None, ALU.max)
            v_qx.inc(vec.drain())
            if tc >= 1:
                qx_evac(tc - 1)
        qx_evac(TC - 1)

    @blk.tensor
    def _(tensor):
        if STOP < 3:
            return
        tensor.wait_ge(v_vte.h, NH)
        ALT = T >= FH * 128
        for tc in range(TC):
            tensor.wait_ge(v_qx.h, tc + 1)
            for half in range(2):
                u = 2 * TC + 2 * tc + half
                buf = pX if (ALT and u % 2 == 1) else pT
                lag = 1 if ALT else 0
                tensor.wait_ge(v_fme.h, u - lag)
                for j in range(FH):
                    fc = half * FH + j
                    mm = tensor.transpose(
                        buf[:, j * 128:(j + 1) * 128],
                        x_tiles[:, (tc % 3) * d + fc * 128:(tc % 3) * d + fc * 128 + 128],
                        idf_sb[:, :])
                t_xtr.inc(mm)

    @blk.sync
    def _(sync):
        if STOP < 3:
            return
        sync.wait_ge(cc_b1.h, 1)
        sync.wait_ge(v_wq2.h, N2C)
        sync.wait_ge(v_rope.h, 2 * NH)
        sync.wait_ge(v_vte.h, NH)
        st_k.wait(sync, 0, 16 * NH)
        st_v.wait(sync, 0, 32 * NH)
        for hf in range(2 * N_CORES):
            r, hh = hf // 2, hf % 2
            ld_q1.gate(sync, hf % 4)
            if hf >= 4:
                sync.wait_ge(t_f1.h, min(((hf - 4) // 2 + 1) * NW1, IC))
            ld_q1.issue(sync, hf % 4,
                        q1slab[:, (hf % 4) * HSL:(hf % 4 + 1) * HSL]
                        .rearrange("p (f i) -> p f i", f=FC // 2),
                        agb1r[r, :, hh * (FC // 2):(hh + 1) * (FC // 2), :])

    @blk.tensor
    def _(tensor):
        if STOP < 3:
            return
        tensor.wait_ge(v_fme.h, 4 * TC)
        tensor.wait_ge(s_qb.h, 3 * NH * NP)
        tensor.wait_ge(v_rope.h, 2 * NH)
        for ic in range(IC):
            r = ic // NW1
            for hf in (2 * r, 2 * r + 1):
                ld_q1.wait(tensor, hf % 4, 16 * (hf // 4 + 1))
            if ic >= 2:
                tensor.wait_ge(s_g.h, ic - 1)
            pp = pX if ic % 2 == 0 else pY
            for fc in range(FC):
                slot = (2 * r + fc // (FC // 2)) % 4
                off = slot * HSL + (fc % (FC // 2)) * ISL + (ic % NW1) * 128
                wt = q1slab[:, off:off + 128]
                for (o, w) in PC:
                    mm = tensor.matmul(pp[:, o:o + w], wt,
                                       xn_fm[:, fc * T + o: fc * T + o + w],
                                       start=(fc == 0), stop=(fc == FC - 1))
            t_f1.inc(mm)

    @blk.scalar
    def _(scalar):
        if STOP < 3:
            return
        for ic in range(IC):
            scalar.wait_ge(t_f1.h, ic + 1)
            if ic == 0:
                scalar.wait_ge(v_sx.h, 1)
            if ic >= 2:
                scalar.wait_ge(v_hx.h, ic - 1)
            st_h.gate(scalar, ic % 2)
            pp = pX if ic % 2 == 0 else pY
            scalar.activation(h_sb[:, (ic % 2) * T:(ic % 2) * T + T], pp[:, :],
                              AF.Gelu, bias=b1_sb[:, ic:ic + 1],
                              scale=s1_col[:, ic:ic + 1])
            s_g.inc(scalar.drain())
            st_h.issue(scalar, ic % 2, h_sp.ap()[ic],
                       h_sb[:, (ic % 2) * T:(ic % 2) * T + T])

    @blk.vector
    def _(vec):
        if STOP < 3:
            return
        for ic in range(IC):
            vec.wait_ge(s_g.h, ic + 1)
            vec.tensor_reduce(hmax_sb[:, ic:ic + 1],
                              h_sb[:, (ic % 2) * T:(ic % 2) * T + T], AX.X,
                              ALU.max, apply_absolute_value=True)
            v_hx.inc(vec.drain())
        vec.wait_ge(t_lm.h, 1)
        vec.tensor_reduce(csc[:, 0:1], hmax_sb[:, :], AX.X, ALU.max)
        v_lm.inc(vec.drain())

    @blk.tensor
    def _(tensor):
        if STOP < 3:
            return
        tensor.wait_ge(v_lm.h, 3)
        tensor.wait_ge(v_pp1.h, ppi["w"])
        t_lm.inc(tensor.matmul(pP1[0:1, :], csc[:, 0:1], idf_sb[:, :],
                               start=True, stop=True))
        ppi["w"] += 1

    @blk.vector
    def _(vec):
        if STOP < 3:
            return
        vec.wait_ge(t_lm.h, 2)
        vec.tensor_reduce(gm2_sb[:, :], pP1[0:1, :], AX.X, ALU.max)
        vec.drain()
        vec.tensor_scalar(g2row_sb[:, :], pP1[0:1, :], gm2_sb[0:1, 0:1],
                          None, ALU.max)
        vec.drain()
        v_pp1.inc(vec.drain())
        v_lm.inc(vec.drain())

    esT.close()
    esP3.close()
    esA1a.close()
    esP4 = contextlib.ExitStack()
    P4 = lambda n, s, dt: esP4.enter_context(nc.sbuf_tensor(n, s, dt))
    k_sb = P4("k_sb", [128, 2 * S], BF16)
    v_sb = P4("v_sb", [128, 2 * S], BF16)
    exp_sb = P4("exp_sb", [128, 4 * T], BF16)
    recip_sb = P4("recip_sb", [128, T], F32)
    acc2 = P4("acc2", [128, 2 * T], F32)
    hrd_sb = P4("hrd_sb", [128, 2 * T], F32) if QA else None
    es_qha = contextlib.ExitStack()
    qh_a = es_qha.enter_context(
        nc.sbuf_tensor("qh_a", [128, QA * T], BF16, side="right")) if QA else None
    es_ao = contextlib.ExitStack()
    attn_out = es_ao.enter_context(
        nc.sbuf_tensor("attn_out", [128, NH * T], BF16, side="right"))

    _qh_b_ref = []

    def qh_sl(ic, o, w):
        if ic < QA:
            return qh_a[:, ic * T + o: ic * T + o + w]
        qb = _qh_b_ref[0]
        return qb[:, (ic - QA) * T + o: (ic - QA) * T + o + w]
    esPO = contextlib.ExitStack()
    pO = esPO.enter_context(nc.psum_tensor("pO", [128, T], F32))
    pU = [esPO.enter_context(nc.psum_tensor(f"pU{i}", [128, PC[0][1]], F32))
          for i in range(NP)]

    agcr = agc_out.ap().rearrange("(r e) -> r e", r=GROUP)
    accb = v_acc.c
    p1b = s_p1.c

    @blk.sync
    def _(sync):
        if STOP < 4:
            return
        sync.wait_ge(cc_kv.h, 1)
        sync.wait_ge(t_f1.h, IC)
        st_h.wait_all(sync)
        sync.wait_ge(v_hx.h, IC)
        for h in range(NH):
            slot = h % 2
            ld_kv.gate(sync, slot)
            if h >= 2:
                sync.wait_ge(t_O.h, h - 1)
            for r in range(GROUP):
                src = agcr[r, KOFF + h * 128 * T: KOFF + (h + 1) * 128 * T] \
                    .rearrange("(p t) -> p t", p=128)
                ld_kv.issue(sync, slot,
                            k_sb[:, slot * S + r * T: slot * S + (r + 1) * T], src)
            for r in range(GROUP):
                src = agcr[r, VOFF + h * T * 128: VOFF + (h + 1) * T * 128] \
                    .rearrange("(tc p dc) -> p tc dc", p=128, dc=128)
                dstv = v_sb[:, slot * S + r * T: slot * S + (r + 1) * T] \
                    .rearrange("p (tc dc) -> p tc dc", dc=128)
                hh = TC // 2
                for h2 in range(2):
                    ld_kv.issue(sync, slot,
                                dstv[:, h2 * hh:(h2 + 1) * hh],
                                src[:, h2 * hh:(h2 + 1) * hh])

    @blk.tensor
    def _(tensor):
        if STOP < 4:
            return
        pS = [pX, pY]
        flat = [(h, kc) for h in range(NH) for kc in range(KC)]

        def emit_scores(h, kc):
            idx = h * KC + kc
            if kc == 0:
                ld_kv.wait(tensor, h % 2, 16 * 3 * GROUP * (h // 2 + 1))
            if idx == 0:
                tensor.wait_ge(s_g.h, IC)
                tensor.wait_ge(v_pp1.h, ppi["w"])
                tensor.wait_ge(v_fme.h, 4 * TC)
            if idx >= 2:
                tensor.wait_ge(s_exp.h, idx - 1)
            kt = k_sb[:, (h % 2) * S + kc * 128: (h % 2) * S + (kc + 1) * 128]
            for (o, w) in PC:
                mm = tensor.matmul(pS[idx % 2][:, o:o + w], kt,
                                   q_rope[:, h * T + o: h * T + o + w],
                                   start=True, stop=True)
            t_S.inc(mm)

        def emit_av(h, kc):
            idx = h * KC + kc
            tensor.wait_ge(s_exp.h, idx + 1)
            if kc == 0:
                tensor.wait_ge(v_att.h, h)
                tensor.wait_ge(s_rec.h, h)
            vt = v_sb[:, (h % 2) * S + kc * 128: (h % 2) * S + (kc + 1) * 128]
            ex = exp_sb[:, (idx % 4) * T:(idx % 4) * T + T]
            for (o, w) in PC:
                mm = tensor.matmul(pO[:, o:o + w], vt, ex[:, o:o + w],
                                   start=(kc == 0), stop=(kc == KC - 1))
            t_A.inc(mm)
            if kc == KC - 1:
                tensor.wait_ge(v_acc.h, accb + (h + 1) * KC)
                for pi, (o, w) in enumerate(PC):
                    tensor.matmul(pU[pi][:, :], onesf32_sb[:, :],
                                  acc2[:, (h % 2) * T + o:(h % 2) * T + o + w],
                                  start=True, stop=True)
                t_O.inc(tensor.drain())

        emit_scores(*flat[0])
        emit_scores(*flat[1])
        for i, (h, kc) in enumerate(flat):
            if i + 2 < len(flat):
                emit_scores(*flat[i + 2])
            emit_av(h, kc)

    @blk.scalar
    def _(scalar):
        if STOP < 4:
            return
        st_h.wait_all(scalar)
        for h in range(NH):
            for kc in range(KC):
                scalar.wait_ge(t_S.h, h * KC + kc + 1)
                if h * KC + kc >= 4:
                    scalar.wait_ge(t_A.h, h * KC + kc - 3)
                    scalar.wait_ge(v_acc.h, accb + h * KC + kc - 3)
                idx = h * KC + kc
                s_exp.inc(scalar.activation(
                    exp_sb[:, (idx % 4) * T:(idx % 4) * T + T],
                    (pX if idx % 2 == 0 else pY)[:, :],
                    AF.Exp, scale=EXPS, bias=EXPB))
            scalar.wait_ge(t_O.h, h + 1)
            if h > 0:
                scalar.wait_ge(v_att.h, h)
            for pi, (o, w) in enumerate(PC):
                scalar.activation(recip_sb[:, o:o + w], pU[pi][:, :], AF.Ln)
            scalar.drain()
            s_rec.inc(scalar.activation(recip_sb[:, :], recip_sb[:, :],
                                        AF.Exp, scale=np.float32(-1.0)))

    @blk.vector
    def _(vec):
        if STOP < 4:
            return
        for h in range(NH):
            acc = acc2[:, (h % 2) * T:(h % 2) * T + T]
            for kc in range(KC):
                idx = h * KC + kc
                vec.wait_ge(s_exp.h, idx + 1)
                ex = exp_sb[:, (idx % 4) * T:(idx % 4) * T + T]
                if kc == 0:
                    vec.tensor_scalar(acc[:, :], ex[:, :], 0.0,
                                      None, ALU.add)
                else:
                    vec.tensor_tensor(acc[:, :], acc[:, :], ex[:, :], ALU.add)
                v_acc.inc(vec.drain())
            vec.wait_ge(s_rec.h, h + 1)
            vec.tensor_tensor(attn_out[:, h * T:(h + 1) * T], pO[:, :],
                              recip_sb[:, :], ALU.mult)
            v_att.inc(vec.drain())

    esP4.close()
    esA2.close()
    esQR.close()
    esPO.close()
    esXY.close()
    esQH = contextlib.ExitStack()
    qh_b = esQH.enter_context(nc.sbuf_tensor("qh_b", [128, max(QB, 1) * T], BF16))
    _qh_b_ref.append(qh_b)
    hrd2_sb = esQH.enter_context(nc.sbuf_tensor("hrd2_sb", [128, 2 * T], F32))
    h2q_sb = esQH.enter_context(nc.sbuf_tensor("h2q_sb", [128, 2 * T], F32))

    esP5 = contextlib.ExitStack()
    P5 = lambda n, s, dt: esP5.enter_context(nc.sbuf_tensor(n, s, dt))
    wo_sb = P5("wo_sb", [128, 2 * FC * 128], BF16)
    asp_sb = P5("asp_sb", [128, 2 * T], F32)
    esW = contextlib.ExitStack()
    pW0 = esW.enter_context(nc.psum_tensor("pW0", [128, T], F32))
    pW1 = esW.enter_context(nc.psum_tensor("pW1", [128, T], F32))
    pF = esW.enter_context(nc.psum_tensor("pF", [128, 1024], F32))

    def pF_sl(s):
        c = (s % 2) * 512 + (s // 2) * 128
        return pF[:, c:c + 128]

    @blk.sync
    def _(sync):
        if STOP < 5:
            return
        sync.wait_ge(t_O.h, NH)
        st_h.wait_all(sync)
        for oc in range(FC):
            ld_wo.gate(sync, oc % 2)
            if oc >= 2:
                sync.wait_ge(t_wo.h, oc - 1)
            r = (oc * 128) // OSL
            co = (oc * 128) % OSL
            dst = wo_sb[:, (oc % 2) * FC * 128:(oc % 2 + 1) * FC * 128]
            fs = FC // 4
            for q4 in range(4):
                ld_wo.issue(sync, oc % 2,
                            dst.rearrange("p (f w) -> p f w", f=FC)
                            [:, q4 * fs:(q4 + 1) * fs],
                            agar[r, 3, :, q4 * fs:(q4 + 1) * fs, co:co + 128])


    @blk.gpsimd
    def _(g):
        if STOP < 5:
            return
        g.wait_ge(t_O.h, NH)
        g.wait_ge(s_rec.h, NH)
        g.wait_ge(v_att.h, NH)
        st_h.wait_all(g)
        for ic in range(IC):
            ld_h.gate(g, ic % 2)
            if ic >= 2:
                g.wait_ge(s_p1.h, p1b + ic - 1)
            ld_h.issue(g, ic % 2,
                       hrd2_sb[:, (ic % 2) * T:(ic % 2) * T + T],
                       h_sp.ap()[ic])

    @blk.tensor
    def _(tensor):
        if STOP < 5:
            return
        for oc in range(FC):
            ld_wo.wait(tensor, oc % 2, 64 * (oc // 2 + 1))
            if oc == 0:
                tensor.wait_ge(v_att.h, NH)
                tensor.wait_ge(s_rec.h, NH)
                tensor.wait_ge(s_exp.h, NH * KC)
            if oc >= 2:
                tensor.wait_ge(s_wo.h, oc - 1)
            pp = pW0 if oc % 2 == 0 else pW1
            for fc in range(FC):
                wt = wo_sb[:, (oc % 2) * FC * 128 + fc * 128:(oc % 2) * FC * 128 + fc * 128 + 128]
                for (o, w) in PC:
                    mm = tensor.matmul(pp[:, o:o + w], wt,
                                       attn_out[:, fc * T + o: fc * T + o + w],
                                       start=(fc == 0), stop=(fc == FC - 1))
            t_wo.inc(mm)

    @blk.scalar
    def _(scalar):
        if STOP < 5:
            return
        npo = IC // FC
        for oc in range(FC):
            scalar.wait_ge(t_wo.h, oc + 1)
            st_w.gate(scalar, oc % 2)
            scalar.activation(asp_sb[:, (oc % 2) * T:(oc % 2) * T + T],
                              (pW0 if oc % 2 == 0 else pW1)[:, :], AF.Copy)
            s_wo.inc(scalar.drain())
            st_w.issue(scalar, oc % 2, attn_sp.ap()[oc],
                       asp_sb[:, (oc % 2) * T:(oc % 2) * T + T])
            for ic in range(oc * npo, (oc + 1) * npo):
                if ic == 0:
                    scalar.wait_ge(v_sx.h, 2)
                if ic >= 2:
                    scalar.wait_ge(v_qh.h, ic - 1)
                ld_h.wait(scalar, ic % 2, 16 * (ic // 2 + 1))
                s_p1.inc(scalar.activation(
                    h2q_sb[:, (ic % 2) * T:(ic % 2) * T + T],
                    hrd2_sb[:, (ic % 2) * T:(ic % 2) * T + T],
                    AF.Identity, bias=np.float32(RND_C), scale=r2_col[:, 0:1]))

    @blk.vector
    def _(vec):
        if STOP < 5:
            return
        vec.wait_ge(g_gl2.h, 16)
        vec.tensor_scalar(csc[:, 7:8], sx2_col[:, :], 1.0 / 127.0, 1e-8,
                          ALU.mult, ALU.max)
        vec.drain()
        vec.reciprocal(r2_col[:, :], csc[:, 7:8])
        vec.drain()
        vec.tensor_scalar(s2_col[:, :], sw2_col[:, :], csc[:, 7:8], None, ALU.mult)
        v_sx.inc(vec.drain())
        for ic in range(IC):
            vec.wait_ge(s_p1.h, p1b + ic + 1)
            hs = h2q_sb[:, (ic % 2) * T:(ic % 2) * T + T]
            vec.tensor_scalar(hs[:, :], hs[:, :], RND_C, 127.0,
                              ALU.subtract, ALU.min)
            vec.drain()
            vec.tensor_scalar(qh_sl(ic, 0, T), hs[:, :], -127.0, None, ALU.max)
            v_qh.inc(vec.drain())

    es_ao.close()
    esP5.close()
    esP6 = contextlib.ExitStack()
    P6 = lambda n, s, dt: esP6.enter_context(nc.sbuf_tensor(n, s, dt))
    q2_sb = P6("q2_sb", [128, 2 * IC * 128], BF16)
    f2_sb = P6("f2_sb", [128, 2 * T], F32)
    asp2_sb = P6("asp2_sb", [128, T], F32)
    fin_sb = P6("fin_sb", [128, T], F32)
    xcol_sb = P6("xcol_sb", [128, 2 * T], F32)
    outt_sb = P6("outt_sb", [128, 4 * 128], F32)

    @blk.sync
    def _(sync):
        if STOP < 6:
            return
        sync.wait_ge(s_wo.h, FC)
        st_w.wait_all(sync)
        for oc in range(FC):
            ld_q2.gate(sync, oc % 2)
            if oc >= 2:
                sync.wait_ge(t_f2.h, oc - 1)
            r = (oc * 128) // OSL
            co = (oc * 128) % OSL
            i16 = max(1, IC // 16)
            for q16 in range(IC // i16):
                ld_q2.issue(sync, oc % 2,
                            q2_sb.rearrange("p (b f w) -> p b f w", b=2, f=IC)
                            [:, oc % 2, q16 * i16:(q16 + 1) * i16],
                            agb2r[r, :, q16 * i16:(q16 + 1) * i16, co:co + 128])
            if oc >= 1:
                sync.wait_ge(v_fin.h, oc)
            ld_fa.inc(sync.dma_start(
                out=asp2_sb[:, :], in_=attn_sp.ap()[oc]), 16)
            ld_fx.gate(sync, oc % 2)
            if oc >= 2:
                sync.wait_ge(v_ot.h, (oc - 1) * TC)
            hh = TC // 2
            for h2 in range(2):
                ld_fx.issue(sync, oc % 2,
                            xcol_sb.rearrange("p (b tc w) -> p b tc w", b=2, tc=TC)
                            [:, oc % 2, h2 * hh:(h2 + 1) * hh],
                            xcolr[:, h2 * hh:(h2 + 1) * hh, oc * 128:(oc + 1) * 128])
        st_out.wait_all(sync)

    def _btr(tensor, oc):
        tensor.wait_ge(v_fin.h, oc + 1)
        for tc in range(TC):
            ti = oc * TC + tc
            if ti >= 2:
                tensor.wait_ge(v_ot.h, ti - 1)
            mm = tensor.transpose(pF_sl(ti % 4),
                                  fin_sb[:, tc * 128:(tc + 1) * 128],
                                  idf_sb[:, :])
            t_F.inc(mm)

    @blk.tensor
    def _(tensor):
        if STOP < 6:
            return
        tensor.wait_ge(cc_b2.h, 1)
        boff = min(IC // 4, max(0, IC - TC))
        bstep = max(1, (IC - boff) // TC)
        if boff + bstep * (TC - 1) >= IC:
            boff, bstep = IC - TC, 1
        bt = {boff + bstep * tc: tc for tc in range(TC)}
        for oc in range(FC):
            ld_q2.wait(tensor, oc % 2, 16 * (IC // max(1, IC // 16)) * (oc // 2 + 1))
            if oc == 0:
                tensor.wait_ge(s_wo.h, FC)
            if oc >= 2:
                tensor.wait_ge(s_f2.h, oc - 1)
            pp = pW0 if oc % 2 == 0 else pW1
            for ic in range(IC):
                if oc == 0:
                    tensor.wait_ge(v_qh.h, ic + 1)
                if oc >= 1 and ic in bt:
                    tc = bt[ic]
                    ti = (oc - 1) * TC + tc
                    if tc == 0:
                        tensor.wait_ge(v_fin.h, oc)
                    if ti >= 2:
                        tensor.wait_ge(v_ot.h, ti - 1)
                    t_F.inc(tensor.transpose(pF_sl(ti % 4),
                                             fin_sb[:, tc * 128:(tc + 1) * 128],
                                             idf_sb[:, :]))
                wt = q2_sb[:, (oc % 2) * IC * 128 + ic * 128:(oc % 2) * IC * 128 + ic * 128 + 128]
                for (o, w) in PC:
                    mm = tensor.matmul(pp[:, o:o + w], wt, qh_sl(ic, o, w),
                                       start=(ic == 0), stop=(ic == IC - 1))
            t_f2.inc(mm)
        _btr(tensor, FC - 1)

    @blk.scalar
    def _(scalar):
        if STOP < 6:
            return
        for oc in range(FC):
            scalar.wait_ge(t_f2.h, oc + 1)
            if oc == 0:
                scalar.wait_ge(v_sx.h, 2)
            if oc >= 2:
                scalar.wait_ge(v_fin.h, oc - 1)
            s_f2.inc(scalar.activation(
                f2_sb[:, (oc % 2) * T:(oc % 2) * T + T],
                (pW0 if oc % 2 == 0 else pW1)[:, :], AF.Identity,
                bias=b2_sb[:, oc:oc + 1], scale=s2_col[:, oc:oc + 1]))

    @blk.vector
    def _(vec):
        if STOP < 6:
            return
        for oc in range(FC):
            vec.wait_ge(s_f2.h, oc + 1)
            vec.wait_ge(ld_fa.h, 16 * (oc + 1))
            if oc >= 1:
                vec.wait_ge(t_F.h, oc * TC)
            vec.tensor_tensor(fin_sb[:, :],
                              f2_sb[:, (oc % 2) * T:(oc % 2) * T + T],
                              asp2_sb[:, :], ALU.add)
            v_fin.inc(vec.drain())
            ld_fx.wait(vec, oc % 2, 32 * (oc // 2 + 1))
            for tc in range(TC):
                ti = oc * TC + tc
                vec.wait_ge(t_F.h, ti + 1)
                if ti >= 4:
                    st_out.wait(vec, ti % 4, 16 * ((ti - 4) // 4 + 1))
                vec.tensor_tensor(
                    outt_sb[:, (ti % 4) * 128:(ti % 4) * 128 + 128],
                    pF_sl(ti % 4),
                    xcol_sb[:, (oc % 2) * T + tc * 128:(oc % 2) * T + (tc + 1) * 128],
                    ALU.add)
                v_ot.inc(vec.drain())

    @blk.gpsimd
    def _(g):
        if STOP < 6:
            return
        for oc in range(FC):
            for tc in range(TC):
                ti = oc * TC + tc
                g.wait_ge(v_ot.h, ti + 1)
                st_out.gate(g, ti % 4)
                st_out.issue(g, ti % 4,
                             out_t[tc * 128:(tc + 1) * 128, oc * 128:(oc + 1) * 128],
                             outt_sb[:, (ti % 4) * 128:(ti % 4) * 128 + 128])

    if STOP < 6:
        @blk.sync
        def _(sync):
            if STOP >= 1:
                st_xn.wait_all(sync)
                sync.wait_ge(v_lm.h, 2)
                sync.wait_ge(v_fme.h, 2 * TC)
            if STOP >= 2:
                st_k.wait(sync, 0, 16 * NH)
                st_v.wait(sync, 0, 32 * NH)
                sync.wait_ge(v_wq2.h, N2C)
                sync.wait_ge(v_vte.h, NH)
            if STOP >= 3:
                st_h.wait_all(sync)
                sync.wait_ge(v_lm.h, 4)
                sync.wait_ge(v_fme.h, 4 * TC)
            if STOP >= 4:
                sync.wait_ge(v_att.h, NH)
            if STOP >= 5:
                st_w.wait_all(sync)
                sync.wait_ge(v_qh.h, IC)

    esP6.close()
    esQH.close()
    es_qha.close()
    esW.close()
    es.close()
    return nc



_CFG_FULL = {"d": 2048, "I": 8192, "T": 1024, "qa": 0}
_NC_CACHE = {}


def _get_nc(key, cfg):
    if key not in _NC_CACHE:
        _NC_CACHE[key] = build_nc(dict(cfg))
    return _NC_CACHE[key]


def host_inputs(core, x_full, Wq, bq, Wk, bk, Wv, bv, Wo, w1, b1, w2, b2, cfg):
    d, I, T = cfg["d"], cfg["I"], cfg["T"]
    NH, FC, IC = d // 128, d // 128, I // 128
    OSL, ISL = d // N_CORES, I // N_CORES
    b = core // GROUP
    q0 = (core % GROUP) * T
    f32 = np.float32
    bf16 = ml_dtypes.bfloat16

    m = {}
    m["x_in"] = np.ascontiguousarray(x_full[b, q0:q0 + T, :], dtype=f32)
    m["wqT_sl"] = np.ascontiguousarray(Wq.T[:, core * OSL:(core + 1) * OSL], dtype=f32)
    m["wkT_sl"] = np.ascontiguousarray(Wk.T[:, core * OSL:(core + 1) * OSL], dtype=f32)
    m["wvT_sl"] = np.ascontiguousarray(Wv.T[:, core * OSL:(core + 1) * OSL], dtype=f32)
    m["woT_sl"] = np.ascontiguousarray(Wo.T[:, core * OSL:(core + 1) * OSL], dtype=f32)
    m["w1T_sl"] = np.ascontiguousarray(w1.T[:, core * ISL:(core + 1) * ISL], dtype=f32)
    m["w2T_sl"] = np.ascontiguousarray(w2.T[:, core * OSL:(core + 1) * OSL], dtype=f32)
    m["bq_t"] = np.ascontiguousarray(bq.reshape(NH, 128).T, dtype=f32)
    m["bk_t"] = np.ascontiguousarray(bk.reshape(NH, 128).T, dtype=f32)
    m["bv_t"] = np.ascontiguousarray(bv.reshape(NH, 128).T, dtype=f32)
    m["b1_t"] = np.ascontiguousarray(b1.reshape(IC, 128).T, dtype=f32)
    m["b2_t"] = np.ascontiguousarray(b2.reshape(FC, 128).T, dtype=f32)
    hd = 128
    inv = (1.0 / (np.float32(1000000.0) **
                  (np.arange(0, hd, 2, dtype=f32) / f32(hd)))).astype(f32)
    pos = np.arange(q0, q0 + T, dtype=f32)
    fr = np.outer(pos, inv).astype(f32)
    emb = np.concatenate([fr, fr], axis=1)
    m["cos_t"] = np.ascontiguousarray(np.cos(emb).T, dtype=f32)
    m["sin_t"] = np.ascontiguousarray(np.sin(emb).T, dtype=f32)
    m["id_bf"] = np.eye(128, dtype=bf16)
    m["id_f32"] = np.eye(128, dtype=f32)
    R = np.zeros((128, 128), dtype=f32)
    for i in range(64):
        R[i, i + 64] = -1.0
        R[i + 64, i] = 1.0
    m["rT_bf"] = np.ascontiguousarray(R.T).astype(bf16)
    m["ones_bf"] = np.ones((128, 128), dtype=bf16)
    m["ones_f32"] = np.ones((128, 128), dtype=f32)
    m["onesf"] = np.ones((1, 128), dtype=f32)
    return m


def kernel(L0_attn_hidden_states, L0_attn_Wq, L0_attn_bq, L0_attn_Wk, L0_attn_bk,
           L0_attn_Wv, L0_attn_bv, L0_attn_Wo, L0_ffn_w1, L0_ffn_b1,
           L0_ffn_w2, L0_ffn_b2):
    cfg = _CFG_FULL
    x = np.asarray(L0_attn_hidden_states, dtype=np.float32)
    args = [np.asarray(a, dtype=np.float32) for a in
            (L0_attn_Wq, L0_attn_bq, L0_attn_Wk, L0_attn_bk, L0_attn_Wv,
             L0_attn_bv, L0_attn_Wo, L0_ffn_w1, L0_ffn_b1, L0_ffn_w2, L0_ffn_b2)]
    nc = _get_nc("full", cfg)
    in_maps = [host_inputs(c, x, *args, cfg) for c in range(N_CORES)]
    res = run_bass_kernel_spmd(nc, in_maps, core_ids=list(range(N_CORES)))
    T, d = cfg["T"], cfg["d"]
    B, Sq = x.shape[0], x.shape[1]
    out = np.empty((B, Sq, d), dtype=np.float32)
    for c in range(N_CORES):
        b = c // GROUP
        q0 = (c % GROUP) * T
        out[b, q0:q0 + T, :] = np.asarray(res.results[c]["out"])
    return out
